# version 02
# baseline (speedup 1.0000x reference)
import contextlib

import numpy as np
import ml_dtypes

import concourse.bass as bass
import concourse.mybir as mybir
from concourse.bass_utils import run_bass_kernel_spmd

F32 = mybir.dt.float32
BF16 = mybir.dt.bfloat16
I32 = mybir.dt.int32
AF = mybir.ActivationFunctionType
ALU = mybir.AluOpType
BF = ml_dtypes.bfloat16

NC = 8
T = 8192
D = 2048
NTOK = 1024
CH = 512
NW = NTOK // CH
DCN = D // 128
NCH = T // CH
HID0 = 5632
HC0 = HID0 // 128
HID1 = 8192
HC1 = HID1 // 128
E50P = float(np.exp(np.float64(50.0)))
E50N = float(np.exp(np.float64(-50.0)))
RT128 = float(np.sqrt(np.float64(128.0)))
FBIG = 3.0e38
NEG1E9 = -1.0e9
EPS = 1e-5


def col(dc, w):
    return slice(dc * NTOK + w * CH, dc * NTOK + w * CH + CH)


def wcol(dc):
    return slice(dc * CH, dc * CH + CH)


class Sem:
    def __init__(self, h):
        self.h = h
        self.n = 0

    def inc(self, amt=1):
        self.n += amt
        return self.n


class K:

    def __init__(self):
        self.nc = bass.Bass("TRN2", target_bir_lowering=False)
        self.es = contextlib.ExitStack()
        self._sems = {}
        self.block = None
        self.store_sems = []
        self._useq = 0

    def sem(self, name) -> Sem:
        if name not in self._sems:
            self._sems[name] = Sem(self.es.enter_context(
                self.nc.semaphore(name)))
        return self._sems[name]

    def usem(self, prefix) -> Sem:
        self._useq += 1
        return self.sem(f"{prefix}{self._useq}")

    def sb(self, name, shape, dtype):
        return self.es.enter_context(self.nc.sbuf_tensor(name, shape, dtype))

    def din(self, name, shape, dtype):
        return self.nc.dram_tensor(name, shape, dtype,
                                   kind="ExternalInput").ap()

    def dout(self, name, shape, dtype):
        return self.nc.dram_tensor(name, shape, dtype,
                                   kind="ExternalOutput").ap()

    def start(self, consts=()):
        self.block = self.es.enter_context(self.nc.Block())
        self.pe = self.sem("pe")
        self.act = self.sem("act")
        self.dve = self.sem("dve")
        self.onec_f = self.sb("onec_f", [128, 1], F32)
        self.onec_b = self.sb("onec_b", [128, 1], BF16)
        self.oner_f = self.sb("oner_f", [1, 128], F32)
        self.neg128_b = self.sb("neg128_b", [1, 128], BF16)
        consts = (EPS,) + tuple(consts)
        self._cct = self.sb("ccols", [128, max(1, len(consts))], F32)
        self._cc = {}
        ph = Ph(self)
        ph.v(lambda v: v.memset(self.onec_f[:], 1.0))
        ph.v(lambda v: v.memset(self.onec_b[:], 1.0))
        ph.v(lambda v: v.memset(self.oner_f[:], 1.0))
        ph.v(lambda v: v.memset(self.neg128_b[:], -128.0))
        for i, cv in enumerate(consts):
            ap = self._cct[:, i:i + 1]
            ph.v(lambda v, ap=ap, cv=cv: v.memset(ap, float(cv)))
            self._cc[float(cv)] = ap
        ph.flush()

    def constcol(self, v):
        return self._cc[float(v)]

    def gate(self):
        return [(s, s.n) for s in self._sems.values() if s.n > 0]

    def finish(self):
        blk = self.block
        stores = list(self.store_sems)

        def fin(sy):
            for s in stores:
                sy.wait_ge(s.h, s.n)
        blk.sync(fin)
        self.es.close()
        return self.nc


class Ph:

    ENGS = ("sync", "gpsimd", "scalar", "vector", "tensor")

    def __init__(self, k: K, gate=None):
        self.k = k
        self.q = {e: [] for e in Ph.ENGS}
        self._gated = set()
        self.gate = list(gate) if gate else []

    def _add(self, eng, fn, waits, incs):
        waits = list(waits)
        if eng not in self._gated:
            self._gated.add(eng)
            waits = self.gate + waits
        self.q[eng].append((waits, fn, incs))

    def t(self, fn, waits=(), inc=0):
        incs = []
        if inc:
            self.k.pe.inc(inc)
            incs = [(self.k.pe, inc)]
        self._add("tensor", fn, waits, incs)
        return self.k.pe.n

    def s(self, fn, waits=()):
        a = self.k.act
        w = list(waits) + [(a, a.n)]
        a.inc(1)
        self._add("scalar", fn, w, [(a, 1)])
        return a.n

    def v(self, fn, waits=()):
        d = self.k.dve
        w = list(waits) + [(d, d.n)]
        d.inc(1)
        self._add("vector", fn, w, [(d, 1)])
        return d.n

    def dma(self, queue, pairs, sem: Sem, waits=()):
        h = sem.h

        def fn(e, pairs=tuple(pairs)):
            last = None
            for o, i in pairs:
                last = e.dma_start(out=o, in_=i).then_inc(h, 16)
            return last
        sem.inc(16 * len(pairs))
        self._add(queue, fn, waits, [])
        return sem.n

    def flush(self):
        blk = self.k.block
        for eng in Ph.ENGS:
            entries = self.q[eng]
            if not entries:
                continue

            def section(e, entries=entries):
                for waits, fn, incs in entries:
                    for s, val in waits:
                        if val > 0:
                            e.wait_ge(s.h, val)
                    inst = fn(e)
                    for s, amt in incs:
                        inst.then_inc(s.h, amt)
            getattr(blk, eng)(section)
        self.q = {e: [] for e in Ph.ENGS}
        self._gated = set()



def emit_load_xT(k, x_in, xT):
    ph = Ph(k, gate=k.gate())
    sem = k.usem("lx")
    ph.dma("sync", [(xT[:, dc * NTOK:(dc + 1) * NTOK],
                     x_in[dc * 128:(dc + 1) * 128, :]) for dc in range(DCN)],
           sem)
    ph.flush()
    return sem


def emit_store_fm(k, src, out_hbm, nrows, waits):
    ph = Ph(k, gate=k.gate())
    sem = k.usem("st")
    k.store_sems.append(sem)
    ph.dma("sync", [(out_hbm[r * 128:(r + 1) * 128, :],
                     src[:, r * NTOK:(r + 1) * NTOK])
                    for r in range(nrows // 128)], sem, waits=waits)
    ph.flush()
    return sem


def emit_vec_cols(k, src, n, dst, scale=None):
    ph = Ph(k, gate=k.gate())
    sem = k.usem("vc")
    nch = n // 128
    ph.dma("sync", [(dst[:, c:c + 1],
                     src[c * 128:(c + 1) * 128].rearrange("(p o) -> p o", o=1))
                    for c in range(nch)], sem)
    if scale is not None:
        ph.v(lambda v: v.tensor_scalar(out=dst[:], in0=dst[:], scalar1=scale,
                                       scalar2=None, op0=ALU.mult),
             waits=[(sem, sem.n)])
    ph.flush()
    return sem


def emit_ln(k, xT, w, ytgt, ycol, want_csum, csum_bf, ld_sem=None):
    nc = k.nc
    ph = Ph(k, gate=k.gate())
    k._useq += 1
    u = str(k._useq)
    with (
        nc.psum_tensor("ln_pa" + u, [128, CH], F32) as pa,
        nc.psum_tensor("ln_pb" + u, [128, CH], F32) as pb,
        nc.psum_tensor("ln_pc1" + u, [128, CH], F32) as pc1,
        nc.psum_tensor("ln_pc2" + u, [128, CH], F32) as pc2,
        nc.sbuf_tensor("ln_sq" + u, [128, 2 * CH], F32) as sq,
        nc.sbuf_tensor("ln_rows" + u, [1, 4 * CH], F32) as rows,
        nc.sbuf_tensor("ln_bc" + u, [128, 2 * CH], F32) as bc,
        nc.sbuf_tensor("ln_scr" + u, [128, CH], F32) as scr,
    ):
        ldw = [(ld_sem, ld_sem.n)] if ld_sem is not None else []
        sqb_marks = {}
        for dc in range(DCN):
            sw = list(ldw)
            if dc >= 2:
                sw.append((k.pe, sqb_marks[dc - 2]))
            am = ph.s(lambda s, dc=dc: s.activation(
                out=sq[:, (dc % 2) * CH:(dc % 2 + 1) * CH],
                in_=xT[:, col(dc, w)], func=AF.Square), waits=sw)
            ph.t(lambda t, dc=dc: t.matmul(
                pa[0:1, :], k.onec_f[:], xT[:, col(dc, w)],
                start=(dc == 0), stop=(dc == DCN - 1)), waits=ldw)
            sqb_marks[dc] = ph.t(lambda t, dc=dc: t.matmul(
                pb[0:1, :], k.onec_f[:],
                sq[:, (dc % 2) * CH:(dc % 2 + 1) * CH],
                start=(dc == 0), stop=(dc == DCN - 1)),
                waits=[(k.act, am)], inc=1)
        sums_pe = k.pe.n
        rm = rows[0:1, 0:CH]
        rt = rows[0:1, CH:2 * CH]
        rv = rows[0:1, 2 * CH:3 * CH]
        rr = rows[0:1, 3 * CH:4 * CH]
        ph.v(lambda v: v.tensor_scalar(out=rm, in0=pa[0:1, :],
                                       scalar1=1.0 / D, scalar2=None,
                                       op0=ALU.mult),
             waits=[(k.pe, sums_pe)])
        ph.v(lambda v: v.tensor_scalar(out=rt, in0=pb[0:1, :],
                                       scalar1=1.0 / D, scalar2=None,
                                       op0=ALU.mult))
        ph.v(lambda v: v.tensor_tensor(out=rv, in0=rm, in1=rm, op=ALU.mult))
        ph.v(lambda v: v.tensor_tensor(out=rv, in0=rt, in1=rv,
                                       op=ALU.subtract))
        vd = k.dve.n
        am = ph.s(lambda s: s.activation(out=rr, in_=rv, func=AF.Sqrt,
                                         bias=k.constcol(EPS)[0:1, 0:1]),
                  waits=[(k.dve, vd)])
        ph.v(lambda v: v.reciprocal(out=rv, in_=rr), waits=[(k.act, am)])
        ph.v(lambda v: v.scalar_tensor_tensor(out=rr, in0=rm, scalar=-1.0,
                                              in1=rv, op0=ALU.mult,
                                              op1=ALU.mult))
        vd2 = k.dve.n
        ph.t(lambda t: t.matmul(pc1[:], k.oner_f[:], rr, start=True,
                                stop=True), waits=[(k.dve, vd2)])
        ph.t(lambda t: t.matmul(pc2[:], k.oner_f[:], rv, start=True,
                                stop=True), inc=1)
        bc_pe = k.pe.n
        a1 = ph.s(lambda s: s.activation(out=bc[:, 0:CH], in_=pc1[:],
                                         func=AF.Copy),
                  waits=[(k.pe, bc_pe)])
        a2 = ph.s(lambda s: s.activation(out=bc[:, CH:2 * CH], in_=pc2[:],
                                         func=AF.Copy))
        y_marks = []
        for dc in range(DCN):
            ph.v(lambda v, dc=dc: v.tensor_tensor(
                out=scr[:], in0=xT[:, col(dc, w)], in1=bc[:, 0:CH],
                op=ALU.add), waits=[(k.act, a2)])
            y_marks.append(ph.v(lambda v, dc=dc: v.tensor_tensor(
                out=ytgt[:, ycol(dc)], in0=scr[:], in1=bc[:, CH:2 * CH],
                op=ALU.mult)))
        if want_csum:
            for dc in range(DCN):
                ph.t(lambda t, dc=dc: t.matmul(
                    pa[0:1, :], k.onec_b[:], ytgt[:, ycol(dc)],
                    start=(dc == 0), stop=(dc == DCN - 1)),
                    waits=[(k.dve, y_marks[-1])],
                    inc=(1 if dc == DCN - 1 else 0))
            cs_pe = k.pe.n
            ph.v(lambda v: v.tensor_copy(out=csum_bf, in_=pa[0:1, :]),
                 waits=[(k.pe, cs_pe)])
        ph.flush()
    return k.dve.n


def emit_ffn0(k, xT, yT, csum_bf, wgp_in, wup_in, wdp_in, sg_cols, su_cols,
              halfsd_cols, w, tag):
    nc = k.nc
    ph = Ph(k, gate=k.gate())
    with (
        nc.psum_tensor(f"f0g{tag}", [128, CH], F32) as pg,
        nc.psum_tensor(f"f0u{tag}", [128, CH], F32) as pu,
        nc.psum_tensor(f"f0g2{tag}", [128, CH], F32) as pg2,
        nc.psum_tensor(f"f0u2{tag}", [128, CH], F32) as pu2,
        nc.sbuf_tensor(f"f0wg{tag}", [128, 2 * D], BF16) as wgs,
        nc.sbuf_tensor(f"f0wu{tag}", [128, 2 * D], BF16) as wus,
        nc.sbuf_tensor(f"f0h{tag}", [128, HC0 * CH], BF16) as h_sb,
        nc.sbuf_tensor(f"f0sg{tag}", [128, 2 * CH], F32) as gsc,
        nc.sbuf_tensor(f"f0su{tag}", [128, 2 * CH], F32) as usc,
    ):
        lsem = k.usem(f"f0l{tag}")
        psG = [pg, pg2]
        psU = [pu, pu2]
        mm_marks = {}
        h_marks = {}
        for hc in range(HC0):
            b = hc % 2
            wwaits = [] if hc < 2 else [(k.pe, mm_marks[hc - 2])]
            ph.dma("gpsimd",
                   [(wgs[:, b * D:(b + 1) * D],
                     wgp_in[hc, :, :]),
                    (wus[:, b * D:(b + 1) * D],
                     wup_in[hc, :, :])], lsem, waits=wwaits)
            ld_n = lsem.n
            evw = [] if hc < 2 else [(k.act, h_marks[hc - 2][1])]

            def mm(t, hc=hc, b=b):
                for dc in range(DCN):
                    t.matmul(psG[b][:], wgs[:, b * D + dc * 128:
                                            b * D + (dc + 1) * 128],
                             yT[:, wcol(dc)], start=(dc == 0), stop=False)
                    t.matmul(psU[b][:], wus[:, b * D + dc * 128:
                                            b * D + (dc + 1) * 128],
                             yT[:, wcol(dc)], start=(dc == 0), stop=False)
                t.matmul(psG[b][:], k.neg128_b[:], csum_bf,
                         start=False, stop=True)
                return t.matmul(psU[b][:], k.neg128_b[:], csum_bf,
                                start=False, stop=True)
            mm_marks[hc] = ph.t(mm, waits=[(lsem, ld_n)] + evw, inc=1)
            gw = [(k.pe, mm_marks[hc])]
            if hc >= 2:
                gw.append((k.dve, h_marks[hc - 2][0]))
            a1 = ph.s(lambda s, hc=hc, b=b: s.activation(
                out=gsc[:, b * CH:(b + 1) * CH], in_=psG[b][:], func=AF.Silu,
                scale=sg_cols[:, hc:hc + 1]), waits=gw)
            a2 = ph.s(lambda s, hc=hc, b=b: s.mul(
                out=usc[:, b * CH:(b + 1) * CH], in_=psU[b][:],
                mul=su_cols[:, hc:hc + 1]))
            hm = ph.v(lambda v, hc=hc, b=b: v.tensor_tensor(
                out=h_sb[:, hc * CH:(hc + 1) * CH],
                in0=gsc[:, b * CH:(b + 1) * CH],
                in1=usc[:, b * CH:(b + 1) * CH], op=ALU.mult),
                waits=[(k.act, a2)])
            h_marks[hc] = (hm, a2)
        h_done = k.dve.n

        for hc in range(HC0):
            ph.t(lambda t, hc=hc: t.matmul(
                pg[0:1, :], k.onec_b[:], h_sb[:, hc * CH:(hc + 1) * CH],
                start=(hc == 0), stop=(hc == HC0 - 1)),
                waits=[(k.dve, h_done)], inc=(1 if hc == HC0 - 1 else 0))
        hs_pe = k.pe.n
        hsr = gsc[0:1, 0:CH]
        ph.v(lambda v: v.tensor_scalar(out=hsr, in0=pg[0:1, :],
                                       scalar1=-128.0, scalar2=None,
                                       op0=ALU.mult), waits=[(k.pe, hs_pe)])
        hs_v = k.dve.n
        ph.t(lambda t: t.matmul(pg2[:], k.oner_f[:], hsr, start=True,
                                stop=True), waits=[(k.dve, hs_v)], inc=1)
        hb_pe = k.pe.n
        hbm = ph.s(lambda s: s.activation(out=usc[:, 0:CH], in_=pg2[:],
                                          func=AF.Copy),
                   waits=[(k.pe, hb_pe)])

        dsem = k.usem(f"f0d{tag}")
        with nc.sbuf_tensor(f"f0wd{tag}", [128, 2 * HID0], BF16) as wds:
            dn_marks = {}
            x_marks = {}
            for dc in range(DCN):
                b = dc % 2
                wwaits = [] if dc < 2 else [(k.pe, dn_marks[dc - 2])]
                ph.dma("gpsimd", [(wds[:, b * HID0:(b + 1) * HID0],
                                   wdp_in[dc, :, :])], dsem, waits=wwaits)
                ld_n = dsem.n
                if dc < 2:
                    evw = [(k.dve, hs_v), (k.act, hbm)]
                else:
                    evw = [(k.dve, x_marks[dc - 2])]

                def mmd(t, dc=dc, b=b):
                    last = None
                    for hc in range(HC0):
                        last = t.matmul(
                            psG[b][:], wds[:, b * HID0 + hc * 128:
                                           b * HID0 + (hc + 1) * 128],
                            h_sb[:, hc * CH:(hc + 1) * CH],
                            start=(hc == 0), stop=(hc == HC0 - 1))
                    return last
                dn_marks[dc] = ph.t(mmd, waits=[(dsem, ld_n)] + evw, inc=1)
                ph.v(lambda v, b=b: v.tensor_tensor(
                    out=gsc[:, CH:2 * CH], in0=psG[b][:], in1=usc[:, 0:CH],
                    op=ALU.add), waits=[(k.pe, dn_marks[dc]), (k.act, hbm)])
                x_marks[dc] = ph.v(lambda v, dc=dc: v.scalar_tensor_tensor(
                    out=xT[:, col(dc, w)], in0=gsc[:, CH:2 * CH],
                    scalar=halfsd_cols[:, dc:dc + 1], in1=xT[:, col(dc, w)],
                    op0=ALU.mult, op1=ALU.add))
            ph.flush()
    return k.dve.n


def emit_ffn1(k, xT, yT, wgp_in, wup_in, wdp_in, w, tag):
    nc = k.nc
    ph = Ph(k, gate=k.gate())
    gsc_es = contextlib.ExitStack()
    EG = gsc_es.enter_context
    with (
        nc.psum_tensor(f"f1a{tag}", [128, CH], F32) as p0,
        nc.psum_tensor(f"f1b{tag}", [128, CH], F32) as p1,
        nc.psum_tensor(f"f1c{tag}", [128, CH], F32) as p2,
        nc.psum_tensor(f"f1d{tag}", [128, CH], F32) as p3,
        nc.sbuf_tensor(f"f1h{tag}", [128, HC1 * CH], BF16) as h_sb,
    ):
        wgs = EG(nc.sbuf_tensor(f"f1wg{tag}", [128, 2 * D], BF16))
        wus = EG(nc.sbuf_tensor(f"f1wu{tag}", [128, 2 * D], BF16))
        h4 = EG(nc.sbuf_tensor(f"f1h4{tag}", [128, 4 * CH], F32))
        sq = EG(nc.sbuf_tensor(f"f1sq{tag}", [128, 4 * CH], F32))
        pr = EG(nc.sbuf_tensor(f"f1pr{tag}", [128, 3 * CH], F32))
        geb = EG(nc.sbuf_tensor(f"f1ge{tag}", [128, CH], F32))
        u1 = EG(nc.sbuf_tensor(f"f1u1{tag}", [128, CH], F32))
        lsem = k.usem(f"f1l{tag}")
        psG = [p0, p1]
        psU = [p2, p3]
        mm_marks = {}
        h4_marks = {}
        for c in range(DCN):
            for r in range(4):
                it = c * 4 + r
                b = it % 2
                wwaits = [] if it < 2 else [(k.pe, mm_marks[it - 2])]
                ph.dma("gpsimd",
                       [(wgs[:, b * D:(b + 1) * D], wgp_in[it, :, :]),
                        (wus[:, b * D:(b + 1) * D], wup_in[it, :, :])],
                       lsem, waits=wwaits)
                ld_n = lsem.n
                evw = [] if it < 2 else [(k.dve, h4_marks[it - 2])]

                def mm(t, b=b):
                    for dc in range(DCN):
                        t.matmul(psG[b][:], wgs[:, b * D + dc * 128:
                                                b * D + (dc + 1) * 128],
                                 yT[:, wcol(dc)], start=(dc == 0),
                                 stop=(dc == DCN - 1))
                        last = t.matmul(
                            psU[b][:], wus[:, b * D + dc * 128:
                                           b * D + (dc + 1) * 128],
                            yT[:, wcol(dc)], start=(dc == 0),
                            stop=(dc == DCN - 1))
                    return last
                mm_marks[it] = ph.t(mm, waits=[(lsem, ld_n)] + evw, inc=1)
                gw = [(k.pe, mm_marks[it])]
                if c >= 1:
                    gw.append((k.dve, h4_marks[(c - 1) * 4 + 3]))
                a1 = ph.s(lambda s, b=b, r=r: s.activation(
                    out=h4[:, r * CH:(r + 1) * CH], in_=psG[b][:],
                    func=AF.Silu), waits=gw)
                uw = [] if it == 0 else [(k.dve, h4_marks[it - 1])]
                a2 = ph.s(lambda s, b=b: s.activation(
                    out=u1[:], in_=psU[b][:], func=AF.Copy), waits=uw)
                h4_marks[it] = ph.v(lambda v, r=r: v.tensor_tensor(
                    out=h4[:, r * CH:(r + 1) * CH],
                    in0=h4[:, r * CH:(r + 1) * CH], in1=u1[:], op=ALU.mult),
                    waits=[(k.act, a2)])
            hs = [h4[:, r * CH:(r + 1) * CH] for r in range(4)]
            sqs = [sq[:, r * CH:(r + 1) * CH] for r in range(4)]
            for r in range(4):
                ph.v(lambda v, r=r: v.tensor_tensor(
                    out=sqs[r], in0=hs[r], in1=hs[r], op=ALU.mult))
            pr0 = pr[:, 0:CH]
            pr1 = pr[:, CH:2 * CH]
            pr2 = pr[:, 2 * CH:3 * CH]
            ph.v(lambda v: v.tensor_tensor(out=pr0, in0=sqs[0], in1=sqs[1],
                                           op=ALU.max))
            ph.v(lambda v: v.tensor_tensor(out=pr1, in0=sqs[2], in1=sqs[3],
                                           op=ALU.max))
            ph.v(lambda v: v.tensor_tensor(out=pr0, in0=pr0, in1=pr1,
                                           op=ALU.min))
            ph.v(lambda v: v.tensor_tensor(out=pr1, in0=sqs[0], in1=sqs[1],
                                           op=ALU.min))
            ph.v(lambda v: v.tensor_tensor(out=pr2, in0=sqs[2], in1=sqs[3],
                                           op=ALU.min))
            ph.v(lambda v: v.tensor_tensor(out=pr1, in0=pr1, in1=pr2,
                                           op=ALU.max))
            ph.v(lambda v: v.tensor_tensor(out=pr0, in0=pr0, in1=pr1,
                                           op=ALU.max))
            for r in range(4):
                ph.v(lambda v, r=r: v.tensor_tensor(
                    out=geb[:], in0=sqs[r], in1=pr0, op=ALU.is_ge))
                m = ph.v(lambda v, r=r, c=c: v.tensor_tensor(
                    out=h_sb[:, (c * 4 + r) * CH:(c * 4 + r + 1) * CH],
                    in0=hs[r], in1=geb[:], op=ALU.mult))
                h4_marks[c * 4 + r] = m
        h_done = k.dve.n
        last_mm = mm_marks[DCN * 4 - 1]
        gsc_es.close()

        dsem = k.usem(f"f1ds{tag}")
        with nc.sbuf_tensor(f"f1wd{tag}", [128, 2 * HID1], BF16) as wds:
            dn_marks = {}
            x_marks = {}
            for dc in range(DCN):
                b = dc % 2
                wwaits = [(k.pe, last_mm), (k.dve, h_done)] if dc < 2 \
                    else [(k.pe, dn_marks[dc - 2])]
                ph.dma("gpsimd", [(wds[:, b * HID1:(b + 1) * HID1],
                                   wdp_in[dc, :, :])], dsem, waits=wwaits)
                ld_n = dsem.n
                evw = [(k.dve, h_done)] if dc < 2 else \
                    [(k.dve, x_marks[dc - 2])]

                def mmd(t, b=b):
                    last = None
                    for hc in range(HC1):
                        last = t.matmul(
                            psG[b][:], wds[:, b * HID1 + hc * 128:
                                           b * HID1 + (hc + 1) * 128],
                            h_sb[:, hc * CH:(hc + 1) * CH],
                            start=(hc == 0), stop=(hc == HC1 - 1))
                    return last
                dn_marks[dc] = ph.t(mmd, waits=[(dsem, ld_n)] + evw, inc=1)
                x_marks[dc] = ph.v(lambda v, dc=dc, b=b: v.scalar_tensor_tensor(
                    out=xT[:, col(dc, w)], in0=psG[b][:], scalar=0.5,
                    in1=xT[:, col(dc, w)], op0=ALU.mult, op1=ALU.add),
                    waits=[(k.pe, dn_marks[dc])])
            ph.flush()
    return k.dve.n


def emit_qkproj(k, yT, wpack_in, mcn, out_hbm, scale_col, tag, wpack2_in=None,
                ctab=None, stab=None):
    skw = {} if scale_col is None else {"scale": scale_col}
    """out_hbm[m*128.., w*CH..] = bf16(scale * W.T @ y)  (feature-major).
    If wpack2_in given: RoPE combine  qa*C + qb*S  (both scaled)."""
    nc = k.nc
    ph = Ph(k, gate=k.gate())
    rope = wpack2_in is not None
    with (
        nc.psum_tensor(f"qp0{tag}", [128, CH], F32) as p0,
        nc.psum_tensor(f"qp1{tag}", [128, CH], F32) as p1,
        nc.psum_tensor(f"qp2{tag}", [128, CH], F32) as p2,
        nc.psum_tensor(f"qp3{tag}", [128, CH], F32) as p3,
        nc.sbuf_tensor(f"qw{tag}", [128, 2 * D], BF16) as ws,
        nc.sbuf_tensor(f"qw2{tag}", [128, 2 * D], BF16) as ws2,
        nc.sbuf_tensor(f"qstg{tag}", [128, 2 * CH], BF16) as qstg,
        nc.sbuf_tensor(f"qsc{tag}", [128, 3 * CH], F32) as qsc,
    ):
        lsem = k.usem(f"qpl{tag}")
        ssem = k.usem(f"qps{tag}")
        k.store_sems.append(ssem)
        psA = [p0, p1]
        psB = [p2, p3]
        mm_marks = {}
        prev_vm = 0
        for m in range(mcn):
            b = m % 2
            wwaits = [] if m < 2 else [(k.pe, mm_marks[(m - 2, NW - 1)])]
            pairs = [(ws[:, b * D:(b + 1) * D], wpack_in[m, :, :])]
            if rope:
                pairs.append((ws2[:, b * D:(b + 1) * D], wpack2_in[m, :, :]))
            ph.dma("gpsimd", pairs, lsem, waits=wwaits)
            ld_n = lsem.n
            for w in range(NW):
                i = m * NW + w
                bb = i % 2

                def mm(t, b=b, bb=bb, w=w):
                    last = None
                    for dc in range(DCN):
                        last = t.matmul(
                            psA[bb][:], ws[:, b * D + dc * 128:
                                           b * D + (dc + 1) * 128],
                            yT[:, col(dc, w)], start=(dc == 0),
                            stop=(dc == DCN - 1))
                        if rope:
                            last = t.matmul(
                                psB[bb][:], ws2[:, b * D + dc * 128:
                                                b * D + (dc + 1) * 128],
                                yT[:, col(dc, w)], start=(dc == 0),
                                stop=(dc == DCN - 1))
                    return last
                evw = [] if i < 2 else [(ssem, 16 * (i - 1))]
                mm_marks[(m, w)] = ph.t(mm, waits=[(lsem, ld_n)] + evw,
                                        inc=1)
                if not rope:
                    am = ph.s(lambda s, bb=bb, i=i: s.activation(
                        out=qstg[:, (i % 2) * CH:(i % 2 + 1) * CH],
                        in_=psA[bb][:], func=AF.Copy, **skw),
                        waits=[(k.pe, mm_marks[(m, w)])])
                    gate_done = (k.act, am)
                else:
                    ph.s(lambda s, bb=bb: s.activation(
                        out=qsc[:, 0:CH], in_=psA[bb][:], func=AF.Copy,
                        **skw),
                        waits=[(k.pe, mm_marks[(m, w)]), (k.dve, prev_vm)])
                    am = ph.s(lambda s, bb=bb: s.activation(
                        out=qsc[:, CH:2 * CH], in_=psB[bb][:], func=AF.Copy,
                        **skw))
                    ph.v(lambda v, w=w: v.tensor_tensor(
                        out=qsc[:, 0:CH], in0=qsc[:, 0:CH],
                        in1=ctab[:, w * CH:(w + 1) * CH], op=ALU.mult),
                        waits=[(k.act, am)])
                    ph.v(lambda v, w=w: v.tensor_tensor(
                        out=qsc[:, 2 * CH:3 * CH], in0=qsc[:, CH:2 * CH],
                        in1=stab[:, w * CH:(w + 1) * CH], op=ALU.mult))
                    vm = ph.v(lambda v, i=i: v.tensor_tensor(
                        out=qstg[:, (i % 2) * CH:(i % 2 + 1) * CH],
                        in0=qsc[:, 0:CH], in1=qsc[:, 2 * CH:3 * CH],
                        op=ALU.add))
                    prev_vm = vm
                    gate_done = (k.dve, vm)
                ph.dma("sync", [(out_hbm[m * 128:(m + 1) * 128,
                                         w * CH:(w + 1) * CH],
                                 qstg[:, (i % 2) * CH:(i % 2 + 1) * CH])],
                       ssem, waits=[gate_done])
        ph.flush()
    return ssem


def emit_vproj(k, yT, wv_in, out_hbm, tag):
    nc = k.nc
    ph = Ph(k, gate=k.gate())
    with (
        nc.psum_tensor(f"vp0{tag}", [128, CH], F32) as p0,
        nc.psum_tensor(f"vp1{tag}", [128, CH], F32) as p1,
        nc.sbuf_tensor(f"vw{tag}", [128, DCN * CH], BF16) as wv_sb,
        nc.sbuf_tensor(f"vstg{tag}", [128, 2 * CH], BF16) as vstg,
    ):
        lsem = k.usem(f"vpl{tag}")
        ssem = k.usem(f"vps{tag}")
        k.store_sems.append(ssem)
        ph.dma("gpsimd", [(wv_sb[:, dc * CH:(dc + 1) * CH],
                           wv_in[dc * 128:(dc + 1) * 128, :])
                          for dc in range(DCN)], lsem)
        ld_n = lsem.n
        ps = [p0, p1]
        mm_marks = {}
        for tt in range(NTOK // 128):
            b = tt % 2
            evw = [] if tt < 2 else [(k.act, mm_marks[tt - 2])]

            def mm(t, tt=tt, b=b):
                last = None
                for dc in range(DCN):
                    last = t.matmul(
                        ps[b][:], yT[:, dc * NTOK + tt * 128:
                                     dc * NTOK + (tt + 1) * 128],
                        wv_sb[:, dc * CH:(dc + 1) * CH],
                        start=(dc == 0), stop=(dc == DCN - 1))
                return last
            pm = ph.t(mm, waits=[(lsem, ld_n)] + evw, inc=1)
            stw = [] if tt < 2 else [(ssem, 16 * (tt - 1))]
            am = ph.s(lambda s, tt=tt, b=b: s.activation(
                out=vstg[:, b * CH:(b + 1) * CH], in_=ps[b][:],
                func=AF.Copy), waits=[(k.pe, pm)] + stw)
            mm_marks[tt] = am
            ph.dma("sync", [(out_hbm[tt * 128:(tt + 1) * 128, :],
                             vstg[:, b * CH:(b + 1) * CH])], ssem,
                   waits=[(k.act, am)])
        ph.flush()
    return ssem


def emit_attn(k, xT, q_in, k_ins, v_ins, flags_sb, md_sb, wo_pack_in,
              hd, nh, bout_cols=None, clamp=True, tag="a"):
    nc = k.nc
    kv_rep = 4
    with (
        nc.psum_tensor(f"as0{tag}", [128, CH], F32) as ps0,
        nc.psum_tensor(f"as1{tag}", [128, CH], F32) as ps1,
        nc.psum_tensor(f"ac0{tag}", [128, CH], F32) as pc0,
        nc.psum_tensor(f"ac1{tag}", [128, CH], F32) as pc1,
        nc.psum_tensor(f"ar0{tag}", [128, CH], F32) as pr0,
        nc.psum_tensor(f"ar1{tag}", [128, CH], F32) as pr1,
        nc.psum_tensor(f"ab{tag}", [128, CH], F32) as pb,
        nc.sbuf_tensor(f"aq{tag}", [128 if hd == 128 else 64,
                                     (DCN if hd == 128 else nh) * CH],
                       BF16) as qw_sb,
        nc.sbuf_tensor(f"akl{tag}", [128, 2 * CH], BF16) as ksl,
        nc.sbuf_tensor(f"avl{tag}", [128, 2 * 4 * hd], BF16) as vsl,
        nc.sbuf_tensor(f"aet{tag}", [128, 2 * CH], F32) as et,
        nc.sbuf_tensor(f"apt{tag}", [128, 2 * CH], BF16) as pt,
        nc.sbuf_tensor(f"actx{tag}", [128, DCN * CH], BF16) as ctx_sb,
        nc.sbuf_tensor(f"abc{tag}", [128, CH], F32) as bcr,
        nc.sbuf_tensor(f"ascr{tag}", [128, CH], F32) as scrD,
        nc.sbuf_tensor(f"arr{tag}", [1, CH], F32) as rrow,
        nc.sbuf_tensor(f"awo{tag}", [128, 2 * D], BF16) as wos,
    ):
        psS = [ps0, ps1]
        psC = [pc0, pc1]
        psR = [pr0, pr1]
        for w in range(NW):
            ph = Ph(k, gate=k.gate())
            qsem = k.usem(f"aq{tag}{w}")
            if hd == 128:
                qpairs = [(qw_sb[:, m * CH:(m + 1) * CH],
                           q_in[m * 128:(m + 1) * 128,
                                w * CH:(w + 1) * CH]) for m in range(DCN)]
            else:
                qpairs = [(qw_sb[0:64, h * CH:(h + 1) * CH],
                           q_in[h * 64:(h + 1) * 64,
                                w * CH:(w + 1) * CH]) for h in range(nh)]
            ph.dma("sync", qpairs, qsem)
            q_n = qsem.n
            kvsem = k.usem(f"akv{tag}{w}")
            exp_marks = {}
            pt_marks = {}
            den_marks = {}
            den_marks_fine = {}
            ctx_done = {}
            prev_bm = 0
            prev_am = 0
            for h in range(nh):
                kv = h // kv_rep
                if hd == 128:
                    qrhs = qw_sb[:, h * CH:(h + 1) * CH]
                else:
                    qrhs = qw_sb[0:64, h * CH:(h + 1) * CH]
                for s in range(16):
                    b = s % 2
                    it = h * 16 + s
                    wwaits = [] if it < 2 else \
                        [(k.pe, den_marks[it - 2])]
                    if hd == 128:
                        kpair = (ksl[:, b * CH:(b + 1) * CH],
                                 k_ins[w][kv * 128:(kv + 1) * 128,
                                          s * CH:(s + 1) * CH])
                    else:
                        kpair = (ksl[0:64, b * CH:(b + 1) * CH],
                                 k_ins[w][0:64, (kv * 16 + s) * CH:
                                          (kv * 16 + s + 1) * CH])
                    vpairs = [(vsl[:, (b * 4 + ks) * hd:
                                   (b * 4 + ks + 1) * hd],
                               v_ins[w][s * CH + ks * 128:
                                        s * CH + (ks + 1) * 128,
                                        kv * hd:(kv + 1) * hd])
                              for ks in range(4)]
                    ph.dma("sync", [kpair] + vpairs, kvsem, waits=wwaits)
                    kv_n = kvsem.n
                    for ks in range(4):
                        i4 = it * 4 + ks
                        sb = ks % 2
                        sw = [(kvsem, kv_n), (qsem, q_n)]
                        if i4 >= 2:
                            sw.append((k.act, exp_marks[i4 - 2]))

                        def mmS(t, b=b, sb=sb, ks=ks, qrhs=qrhs):
                            if hd == 128:
                                lhs = ksl[:, b * CH + ks * 128:
                                          b * CH + ks * 128 + 128]
                            else:
                                lhs = ksl[0:64, b * CH + ks * 128:
                                          b * CH + ks * 128 + 128]
                            return t.matmul(psS[sb][:], lhs, qrhs,
                                            start=True, stop=True)
                        sm = ph.t(mmS, waits=sw, inc=1)
                        eww = [] if i4 < 2 else [(k.dve, pt_marks[i4 - 2])]
                        if s == 15:
                            dv = ph.v(lambda v, sb=sb, ks=ks: v.tensor_tensor(
                                out=scrD[:], in0=psS[sb][:],
                                in1=md_sb[:, ks * CH:(ks + 1) * CH],
                                op=ALU.add),
                                waits=[(k.pe, sm), (k.act, k.act.n)])
                            em = ph.s(lambda s_, sb=sb: s_.activation(
                                out=et[:, sb * CH:(sb + 1) * CH],
                                in_=scrD[:], func=AF.Exp),
                                waits=[(k.dve, dv)] + eww)
                        else:
                            em = ph.s(lambda s_, sb=sb, fcol=w * 16 + s:
                                      s_.activation(
                                          out=et[:, sb * CH:(sb + 1) * CH],
                                          in_=psS[sb][:], func=AF.Exp,
                                          bias=flags_sb[:, fcol:fcol + 1]),
                                      waits=[(k.pe, sm)] + eww)
                        exp_marks[i4] = em
                        ptw = [(k.act, em)]
                        if i4 >= 2:
                            ptw.append((k.pe, den_marks_fine[i4 - 2]))
                        if clamp:
                            pm = ph.v(lambda v, sb=sb: v.tensor_scalar(
                                out=pt[:, sb * CH:(sb + 1) * CH],
                                in0=et[:, sb * CH:(sb + 1) * CH],
                                scalar1=E50P, scalar2=E50N, op0=ALU.min,
                                op1=ALU.max), waits=ptw)
                        else:
                            pm = ph.v(lambda v, sb=sb: v.tensor_scalar(
                                out=pt[:, sb * CH:(sb + 1) * CH],
                                in0=et[:, sb * CH:(sb + 1) * CH],
                                scalar1=FBIG, scalar2=None, op0=ALU.min),
                                waits=ptw)
                        pt_marks[i4] = pm
                        cw = [(k.dve, pm), (kvsem, kv_n)]
                        if s == 0 and ks == 0 and h >= 2:
                            cw.append((k.dve, ctx_done[h - 2]))
                        st_ = (s == 0 and ks == 0)
                        sp_ = (s == 15 and ks == 3)

                        def mmC(t, b=b, sb=sb, ks=ks, h=h, st_=st_, sp_=sp_):
                            t.matmul(psC[h % 2][0:hd, :],
                                     vsl[:, (b * 4 + ks) * hd:
                                         (b * 4 + ks + 1) * hd],
                                     pt[:, sb * CH:(sb + 1) * CH],
                                     start=st_, stop=sp_)
                            return t.matmul(
                                psR[h % 2][0:1, :], k.onec_b[:],
                                pt[:, sb * CH:(sb + 1) * CH],
                                start=st_, stop=sp_)
                        dm = ph.t(mmC, waits=cw, inc=1)
                        den_marks_fine[i4] = dm
                    den_marks[it] = den_marks_fine[it * 4 + 3]
                dpe = den_marks[h * 16 + 15]
                rw = [(k.pe, dpe)]
                if h >= 1:
                    rw.append((k.pe, prev_bm))
                ph.v(lambda v, h=h: v.reciprocal(
                    out=rrow[:], in_=psR[h % 2][0:1, :]), waits=rw)
                rv = k.dve.n
                bw = [(k.dve, rv)]
                if h >= 1:
                    bw.append((k.act, prev_am))
                bm = ph.t(lambda t: t.matmul(
                    pb[0:hd, :], k.oner_f[0:1, 0:hd], rrow[:],
                    start=True, stop=True), waits=bw, inc=1)
                prev_bm = bm
                aw = [(k.pe, bm)]
                if h >= 1:
                    aw.append((k.dve, ctx_done[h - 1]))
                am = ph.s(lambda s: s.activation(
                    out=bcr[0:hd, :], in_=pb[0:hd, :], func=AF.Copy),
                    waits=aw)
                prev_am = am
                if hd == 128:
                    cdst = ctx_sb[:, h * CH:(h + 1) * CH]
                    csrc = psC[h % 2][:]
                    bsrc = bcr[:]
                else:
                    cdst = ctx_sb[(h % 2) * 64:(h % 2) * 64 + 64,
                                  (h // 2) * CH:(h // 2 + 1) * CH]
                    csrc = psC[h % 2][0:64, :]
                    bsrc = bcr[0:64, :]
                ctx_done[h] = ph.v(lambda v, cdst=cdst, csrc=csrc,
                                   bsrc=bsrc: v.tensor_tensor(
                    out=cdst, in0=csrc, in1=bsrc, op=ALU.mult),
                    waits=[(k.act, am)])
            ctx_all = k.dve.n

            osem = k.usem(f"ao{tag}{w}")
            op_marks = {}
            x_marks = {}
            for mc in range(DCN):
                b = mc % 2
                wwaits = [] if mc < 2 else [(k.pe, op_marks[mc - 2])]
                ph.dma("gpsimd", [(wos[:, b * D:(b + 1) * D],
                                   wo_pack_in[mc, :, :])], osem,
                       waits=wwaits)
                ld_n = osem.n
                evw = [(k.dve, ctx_all)] if mc < 2 else \
                    [(k.dve, x_marks[mc - 2])]

                def mmO(t, b=b):
                    last = None
                    for kc in range(DCN):
                        last = t.matmul(
                            psS[b][:], wos[:, b * D + kc * 128:
                                           b * D + (kc + 1) * 128],
                            ctx_sb[:, kc * CH:(kc + 1) * CH],
                            start=(kc == 0), stop=(kc == DCN - 1))
                    return last
                op_marks[mc] = ph.t(mmO, waits=[(osem, ld_n)] + evw, inc=1)
                if bout_cols is None:
                    x_marks[mc] = ph.v(lambda v, mc=mc, b=b: v.tensor_tensor(
                        out=xT[:, col(mc, w)], in0=psS[b][:],
                        in1=xT[:, col(mc, w)], op=ALU.add),
                        waits=[(k.pe, op_marks[mc])])
                else:
                    x_marks[mc] = ph.v(
                        lambda v, mc=mc, b=b: v.scalar_tensor_tensor(
                            out=xT[:, col(mc, w)], in0=psS[b][:],
                            scalar=bout_cols[:, mc:mc + 1],
                            in1=xT[:, col(mc, w)], op0=ALU.add, op1=ALU.add),
                        waits=[(k.pe, op_marks[mc])])
            ph.flush()
    return k.dve.n


def emit_load_misc(k, pairs):
    ph = Ph(k, gate=k.gate())
    sem = k.usem("misc")
    ph.dma("sync", pairs, sem)
    ph.flush()
    return sem



def build_p1():
    k = K()
    xin = k.din("xT_in", [D, NTOK], F32)
    wgp = k.din("wg0p", [HC0, 128, D], I32)
    wup = k.din("wu0p", [HC0, 128, D], I32)
    wdp = k.din("wd0p", [DCN, 128, HID0], I32)
    sg = k.din("s_gate", [HID0], F32)
    su = k.din("s_up", [HID0], F32)
    sd = k.din("s_down", [D], F32)
    wqp = k.din("wq0p", [DCN, 128, D], F32)
    wkp = k.din("wk0p", [4, 128, D], F32)
    wv = k.din("wv0", [D, CH], F32)
    x1out = k.dout("x1T_out", [D, NTOK], F32)
    qout = k.dout("q0T_out", [D, NTOK], BF16)
    kout = k.dout("k0T_out", [CH, NTOK], BF16)
    vout = k.dout("v0_out", [NTOK, CH], BF16)

    k.start(consts=(RT128,))
    xT = k.sb("xT", [128, DCN * NTOK], F32)
    yT = k.sb("yT", [128, DCN * NTOK], BF16)
    sgc = k.sb("sgc", [128, HC0], F32)
    suc = k.sb("suc", [128, HC0], F32)
    sdc = k.sb("sdc", [128, DCN], F32)
    csum = k.sb("csum", [1, 2 * CH], BF16)

    lx = emit_load_xT(k, xin, xT)
    emit_vec_cols(k, sg, HID0, sgc)
    emit_vec_cols(k, su, HID0, suc)
    emit_vec_cols(k, sd, D, sdc, scale=0.5)
    for w in range(NW):
        yw = yT
        emit_ln(k, xT, w, yw, lambda dc: wcol(dc), True,
                csum[0:1, w * CH:(w + 1) * CH], ld_sem=lx)
        emit_ffn0(k, xT, yw, csum[0:1, w * CH:(w + 1) * CH], wgp, wup, wdp,
                  sgc, suc, sdc, w, f"w{w}")
    emit_store_fm(k, xT, x1out, D, waits=k.gate())
    for w in range(NW):
        emit_ln(k, xT, w, yT, lambda dc, w=w: col(dc, w), False, None)
    emit_qkproj(k, yT, wqp, DCN, qout, k.constcol(RT128), "q0")
    emit_qkproj(k, yT, wkp, 4, kout, None, "k0")
    emit_vproj(k, yT, wv, vout, "v0")
    return k.finish()


def build_p2():
    k = K()
    xin = k.din("x1T_in", [D, NTOK], F32)
    qin = k.din("q0T_in", [D, NTOK], BF16)
    kw = [k.din(f"K0w{w}", [CH, NCH * CH], BF16) for w in range(NW)]
    vw = [k.din(f"V0w{w}", [T, CH], BF16) for w in range(NW)]
    flags = k.din("flags0", [128, 32], F32)
    md = k.din("md0", [128, 4 * CH], F32)
    wop = k.din("wo0p", [DCN, 128, D], F32)
    wgp = k.din("wg0p", [HC0, 128, D], I32)
    wup = k.din("wu0p", [HC0, 128, D], I32)
    wdp = k.din("wd0p", [DCN, 128, HID0], I32)
    sg = k.din("s_gate", [HID0], F32)
    su = k.din("s_up", [HID0], F32)
    sd = k.din("s_down", [D], F32)
    x3out = k.dout("x3T_out", [D, NTOK], F32)

    k.start()
    xT = k.sb("xT", [128, DCN * NTOK], F32)
    flags_sb = k.sb("flags_sb", [128, 32], F32)
    md_sb = k.sb("md_sb", [128, 4 * CH], F32)
    sgc = k.sb("sgc", [128, HC0], F32)
    suc = k.sb("suc", [128, HC0], F32)
    sdc = k.sb("sdc", [128, DCN], F32)

    lx = emit_load_xT(k, xin, xT)
    lm = emit_load_misc(k, [(flags_sb[:], flags), (md_sb[:], md)])
    emit_vec_cols(k, sg, HID0, sgc)
    emit_vec_cols(k, su, HID0, suc)
    emit_vec_cols(k, sd, D, sdc, scale=0.5)
    emit_attn(k, xT, qin, kw, vw, flags_sb, md_sb, wop, 128, 16,
              bout_cols=None, clamp=True, tag="a0")
    with k.nc.sbuf_tensor("yTw", [128, DCN * CH], BF16) as yw, \
            k.nc.sbuf_tensor("csum2", [1, 2 * CH], BF16) as csum:
        for w in range(NW):
            emit_ln(k, xT, w, yw, lambda dc: wcol(dc), True,
                    csum[0:1, w * CH:(w + 1) * CH])
            emit_ffn0(k, xT, yw, csum[0:1, w * CH:(w + 1) * CH], wgp, wup,
                      wdp, sgc, suc, sdc, w, f"b{w}")
    emit_store_fm(k, xT, x3out, D, waits=k.gate())
    return k.finish()


def build_p3():
    k = K()
    xin = k.din("x3T_in", [D, NTOK], F32)
    wgp = k.din("wg1p", [HC1, 128, D], F32)
    wup = k.din("wu1p", [HC1, 128, D], F32)
    wdp = k.din("wd1p", [DCN, 128, HID1], F32)
    wqp = k.din("wq1p", [DCN, 128, D], F32)
    wqsp = k.din("wq1sp", [DCN, 128, D], F32)
    wkp = k.din("wk1p", [4, 128, D], F32)
    wksp = k.din("wk1sp", [4, 128, D], F32)
    wv = k.din("wv1", [D, CH], F32)
    ctab_in = k.din("ropeC", [128, NTOK], F32)
    stab_in = k.din("ropeS", [128, NTOK], F32)
    x4out = k.dout("x4T_out", [D, NTOK], F32)
    qout = k.dout("q1T_out", [D, NTOK], BF16)
    kout = k.dout("k1T_out", [CH, NTOK], BF16)
    vout = k.dout("v1_out", [NTOK, CH], BF16)

    k.start(consts=(0.125,))
    xT = k.sb("xT", [128, DCN * NTOK], F32)
    ctab = k.sb("ctab", [128, NTOK], F32)
    stab = k.sb("stab", [128, NTOK], F32)

    lx = emit_load_xT(k, xin, xT)
    emit_load_misc(k, [(ctab[:], ctab_in), (stab[:], stab_in)])
    with k.nc.sbuf_tensor("yTw", [128, DCN * CH], BF16) as yw:
        for w in range(NW):
            emit_ln(k, xT, w, yw, lambda dc: wcol(dc), False, None,
                    ld_sem=lx)
            emit_ffn1(k, xT, yw, wgp, wup, wdp, w, f"w{w}")
    emit_store_fm(k, xT, x4out, D, waits=k.gate())
    yT = k.sb("yTf", [128, DCN * NTOK], BF16)
    for w in range(NW):
        emit_ln(k, xT, w, yT, lambda dc, w=w: col(dc, w), False, None)
    emit_qkproj(k, yT, wqp, DCN, qout, k.constcol(0.125), "q1", wpack2_in=wqsp,
                ctab=ctab, stab=stab)
    emit_qkproj(k, yT, wkp, 4, kout, None, "k1", wpack2_in=wksp,
                ctab=ctab, stab=stab)
    emit_vproj(k, yT, wv, vout, "v1")
    return k.finish()


def build_p4():
    k = K()
    xin = k.din("x4T_in", [D, NTOK], F32)
    qin = k.din("q1T_in", [D, NTOK], BF16)
    kw = [k.din(f"K1w{w}", [64, 8 * NCH * CH], BF16) for w in range(NW)]
    vw = [k.din(f"V1w{w}", [T, CH], BF16) for w in range(NW)]
    flags = k.din("flags1", [128, 32], F32)
    md = k.din("md1", [128, 4 * CH], F32)
    wop = k.din("wo1p", [DCN, 128, D], F32)
    bout = k.din("b_out", [D], F32)
    wgp = k.din("wg1p", [HC1, 128, D], F32)
    wup = k.din("wu1p", [HC1, 128, D], F32)
    wdp = k.din("wd1p", [DCN, 128, HID1], F32)
    x6out = k.dout("x6T_out", [D, NTOK], F32)

    k.start()
    xT = k.sb("xT", [128, DCN * NTOK], F32)
    flags_sb = k.sb("flags_sb", [128, 32], F32)
    md_sb = k.sb("md_sb", [128, 4 * CH], F32)
    bout_cols = k.sb("bout_cols", [128, DCN], F32)

    lx = emit_load_xT(k, xin, xT)
    lm = emit_load_misc(k, [(flags_sb[:], flags), (md_sb[:], md)])
    emit_vec_cols(k, bout, D, bout_cols)
    emit_attn(k, xT, qin, kw, vw, flags_sb, md_sb, wop, 64, 32,
              bout_cols=bout_cols, clamp=False, tag="a1")
    with k.nc.sbuf_tensor("yTw", [128, DCN * CH], BF16) as yw:
        for w in range(NW):
            emit_ln(k, xT, w, yw, lambda dc: wcol(dc), False, None)
            emit_ffn1(k, xT, yw, wgp, wup, wdp, w, f"b{w}")
    emit_store_fm(k, xT, x6out, D, waits=k.gate())
    return k.finish()



def _pack_lhsT(W):
    Kd, N = W.shape
    a = W.reshape(Kd // 128, 128, N // 128, 128).transpose(2, 1, 0, 3)
    return np.ascontiguousarray(a.reshape(N // 128, 128, Kd))


def _md_mask():
    m = np.zeros((128, 4 * CH), np.float32)
    for ks in range(4):
        i = np.arange(128)[:, None] + 128 * ks
        j = np.arange(CH)[None, :]
        m[:, ks * CH:(ks + 1) * CH] = np.where(i > j, NEG1E9, 0.0)
    return m


def _flags(core):
    f = np.zeros((128, 32), np.float32)
    for w in range(NW):
        jd = 2 * core + w
        for s in range(16):
            if s < jd or s == 15:
                f[:, w * 16 + s] = 0.0
            else:
                f[:, w * 16 + s] = NEG1E9
    return f


def _slot_order(jd):
    return [c for c in range(NCH) if c != jd] + [jd]


def _rope_tabs(core):
    pos = np.arange(core * NTOK, (core + 1) * NTOK, dtype=np.float64)
    inv = 1.0 / (10000.0 ** (np.arange(32, dtype=np.float64) / 32.0))
    fr = np.outer(inv, pos)
    cos = np.cos(fr).astype(np.float32)
    sin = np.sin(fr).astype(np.float32)
    C = np.tile(cos, (4, 1))
    S = np.concatenate([-sin, sin, -sin, sin], axis=0)
    return np.ascontiguousarray(C), np.ascontiguousarray(S)


def _swap_heads(W, head_w):
    N = W.shape[1]
    idx = np.arange(N)
    h = idx // head_w
    r = idx % head_w
    half = head_w // 2
    src = h * head_w + np.where(r < half, r + half, r - half)
    return W[:, src]


def _ffn1_perm():
    n = np.arange(HID1)
    c = n // 512
    r = (n % 512) // 128
    p = n % 128
    return 512 * c + 4 * p + r


_CACHE = {}


def _progs():
    if "p" not in _CACHE:
        _CACHE["p"] = (build_p1(), build_p2(), build_p3(), build_p4())
    return _CACHE["p"]


def kernel(**inputs):
    inputs = {kk: np.asarray(vv) for kk, vv in inputs.items()}
    x = np.ascontiguousarray(inputs["L0_attn_hidden_states"][0])
    xT = np.ascontiguousarray(x.T)
    nc1, nc2, nc3, nc4 = _progs()

    wg0p = _pack_lhsT(np.ascontiguousarray(inputs["L0_ffn_wq_gate"].T))
    wu0p = _pack_lhsT(np.ascontiguousarray(inputs["L0_ffn_wq_up"].T))
    wd0p = _pack_lhsT(np.ascontiguousarray(inputs["L0_ffn_wq_down"].T))
    wq0p = _pack_lhsT(inputs["L0_attn_Wq"])
    wk0p = _pack_lhsT(inputs["L0_attn_Wk"])
    wo0p = _pack_lhsT(inputs["L0_attn_Wo"])
    perm = _ffn1_perm()
    wg1p = _pack_lhsT(np.ascontiguousarray(inputs["L1_ffn_w_gate"][:, perm]))
    wu1p = _pack_lhsT(np.ascontiguousarray(inputs["L1_ffn_w_up"][:, perm]))
    wd1p = _pack_lhsT(np.ascontiguousarray(inputs["L1_ffn_w_down"][perm, :]))
    wq1 = inputs["L1_attn_Wq"]
    wk1 = inputs["L1_attn_Wk"]
    wq1p = _pack_lhsT(wq1)
    wq1sp = _pack_lhsT(np.ascontiguousarray(_swap_heads(wq1, 64)))
    wk1p = _pack_lhsT(wk1)
    wk1sp = _pack_lhsT(np.ascontiguousarray(_swap_heads(wk1, 64)))
    wo1p = _pack_lhsT(inputs["L1_attn_Wo"])
    md = _md_mask()

    maps = []
    for c in range(NC):
        maps.append({
            "xT_in": np.ascontiguousarray(xT[:, c * NTOK:(c + 1) * NTOK]),
            "wg0p": wg0p, "wu0p": wu0p, "wd0p": wd0p,
            "s_gate": inputs["L0_ffn_s_gate"],
            "s_up": inputs["L0_ffn_s_up"],
            "s_down": inputs["L0_ffn_s_down"],
            "wq0p": wq0p, "wk0p": wk0p, "wv0": inputs["L0_attn_Wv"],
        })
    r1 = run_bass_kernel_spmd(nc1, maps, core_ids=list(range(NC))).results
    k0 = np.concatenate([r1[c]["k0T_out"] for c in range(NC)], axis=1)
    v0 = np.concatenate([r1[c]["v0_out"] for c in range(NC)], axis=0)

    maps = []
    for c in range(NC):
        m = {
            "x1T_in": r1[c]["x1T_out"],
            "q0T_in": r1[c]["q0T_out"],
            "flags0": _flags(c), "md0": md, "wo0p": wo0p,
            "wg0p": wg0p, "wu0p": wu0p, "wd0p": wd0p,
            "s_gate": inputs["L0_ffn_s_gate"],
            "s_up": inputs["L0_ffn_s_up"],
            "s_down": inputs["L0_ffn_s_down"],
        }
        for w in range(NW):
            order = _slot_order(2 * c + w)
            m[f"K0w{w}"] = np.ascontiguousarray(np.concatenate(
                [k0[:, ch * CH:(ch + 1) * CH] for ch in order], axis=1))
            m[f"V0w{w}"] = np.ascontiguousarray(np.concatenate(
                [v0[ch * CH:(ch + 1) * CH] for ch in order], axis=0))
        maps.append(m)
    r2 = run_bass_kernel_spmd(nc2, maps, core_ids=list(range(NC))).results

    maps = []
    for c in range(NC):
        Ct, St = _rope_tabs(c)
        maps.append({
            "x3T_in": r2[c]["x3T_out"],
            "wg1p": wg1p, "wu1p": wu1p, "wd1p": wd1p,
            "wq1p": wq1p, "wq1sp": wq1sp, "wk1p": wk1p, "wk1sp": wk1sp,
            "wv1": inputs["L1_attn_Wv"],
            "ropeC": Ct, "ropeS": St,
        })
    r3 = run_bass_kernel_spmd(nc3, maps, core_ids=list(range(NC))).results
    k1 = np.concatenate([r3[c]["k1T_out"] for c in range(NC)], axis=1)
    v1 = np.concatenate([r3[c]["v1_out"] for c in range(NC)], axis=0)
    k1r = k1.reshape(8, 64, NCH, CH)

    maps = []
    for c in range(NC):
        m = {
            "x4T_in": r3[c]["x4T_out"],
            "q1T_in": r3[c]["q1T_out"],
            "flags1": _flags(c), "md1": md, "wo1p": wo1p,
            "b_out": inputs["L1_attn_b_out"],
            "wg1p": wg1p, "wu1p": wu1p, "wd1p": wd1p,
        }
        for w in range(NW):
            order = _slot_order(2 * c + w)
            kk = k1r[:, :, order, :]
            kk = kk.transpose(1, 0, 2, 3).reshape(64, 8 * NCH * CH)
            m[f"K1w{w}"] = np.ascontiguousarray(kk)
            m[f"V1w{w}"] = np.ascontiguousarray(np.concatenate(
                [v1[ch * CH:(ch + 1) * CH] for ch in order], axis=0))
        maps.append(m)
    r4 = run_bass_kernel_spmd(nc4, maps, core_ids=list(range(NC))).results

    outT = np.concatenate([np.asarray(r4[c]["x6T_out"], np.float32)
                           for c in range(NC)], axis=1)
    return np.ascontiguousarray(outT.T)[None, :, :].astype(np.float32)


# version 04
# speedup vs baseline: 1.1055x; 1.1055x over previous
import contextlib

import numpy as np
import ml_dtypes

import concourse.bass as bass
import concourse.mybir as mybir
from concourse.bass_utils import run_bass_kernel_spmd

F32 = mybir.dt.float32
BF16 = mybir.dt.bfloat16
I32 = mybir.dt.int32
AF = mybir.ActivationFunctionType
ALU = mybir.AluOpType
BF = ml_dtypes.bfloat16

NC = 8
T = 8192
D = 2048
NTOK = 1024
CH = 512
NW = NTOK // CH
DCN = D // 128
NCH = T // CH
HID0 = 5632
HC0 = HID0 // 128
HID1 = 8192
HC1 = HID1 // 128
E50P = float(np.exp(np.float64(50.0)))
E50N = float(np.exp(np.float64(-50.0)))
RT128 = float(np.sqrt(np.float64(128.0)))
FBIG = 3.0e38
NEG1E9 = -1.0e9
EPS = 1e-5


def col(dc, w):
    return slice(dc * NTOK + w * CH, dc * NTOK + w * CH + CH)


def wcol(dc):
    return slice(dc * CH, dc * CH + CH)


class Sem:
    def __init__(self, h):
        self.h = h
        self.n = 0

    def inc(self, amt=1):
        self.n += amt
        return self.n


class K:

    def __init__(self):
        self.nc = bass.Bass("TRN2", target_bir_lowering=False)
        self.es = contextlib.ExitStack()
        self._sems = {}
        self.block = None
        self.store_sems = []
        self._useq = 0

    def sem(self, name) -> Sem:
        if name not in self._sems:
            self._sems[name] = Sem(self.es.enter_context(
                self.nc.semaphore(name)))
        return self._sems[name]

    def usem(self, prefix) -> Sem:
        self._useq += 1
        return self.sem(f"{prefix}{self._useq}")

    def sb(self, name, shape, dtype):
        return self.es.enter_context(self.nc.sbuf_tensor(name, shape, dtype))

    def din(self, name, shape, dtype):
        return self.nc.dram_tensor(name, shape, dtype,
                                   kind="ExternalInput").ap()

    def dout(self, name, shape, dtype):
        return self.nc.dram_tensor(name, shape, dtype,
                                   kind="ExternalOutput").ap()

    def start(self, consts=()):
        self.block = self.es.enter_context(self.nc.Block())
        self.pe = self.sem("pe")
        self.act = self.sem("act")
        self.dve = self.sem("dve")
        self.onec_f = self.sb("onec_f", [128, 1], F32)
        self.onec_b = self.sb("onec_b", [128, 1], BF16)
        self.oner_f = self.sb("oner_f", [1, 128], F32)
        self.neg128_b = self.sb("neg128_b", [1, 128], BF16)
        consts = (EPS,) + tuple(consts)
        self._cct = self.sb("ccols", [128, max(1, len(consts))], F32)
        self._cc = {}
        ph = Ph(self)
        ph.v(lambda v: v.memset(self.onec_f[:], 1.0))
        ph.v(lambda v: v.memset(self.onec_b[:], 1.0))
        ph.v(lambda v: v.memset(self.oner_f[:], 1.0))
        ph.v(lambda v: v.memset(self.neg128_b[:], -128.0))
        for i, cv in enumerate(consts):
            ap = self._cct[:, i:i + 1]
            ph.v(lambda v, ap=ap, cv=cv: v.memset(ap, float(cv)))
            self._cc[float(cv)] = ap
        ph.flush()

    def constcol(self, v):
        return self._cc[float(v)]

    def gate(self):
        return [(s, s.n) for s in self._sems.values() if s.n > 0]

    def finish(self):
        blk = self.block
        stores = list(self.store_sems)

        def fin(sy):
            for s in stores:
                sy.wait_ge(s.h, s.n)
        blk.sync(fin)
        self.es.close()
        return self.nc


class Ph:

    ENGS = ("sync", "gpsimd", "scalar", "vector", "tensor")

    def __init__(self, k: K, gate=None):
        self.k = k
        self.q = {e: [] for e in Ph.ENGS}
        self._gated = set()
        self.gate = list(gate) if gate else []

    def _add(self, eng, fn, waits, incs):
        waits = list(waits)
        if eng not in self._gated:
            self._gated.add(eng)
            waits = self.gate + waits
        self.q[eng].append((waits, fn, incs))

    def t(self, fn, waits=(), inc=0):
        incs = []
        if inc:
            self.k.pe.inc(inc)
            incs = [(self.k.pe, inc)]
        self._add("tensor", fn, waits, incs)
        return self.k.pe.n

    def s(self, fn, waits=()):
        a = self.k.act
        w = list(waits) + [(a, a.n)]
        a.inc(1)
        self._add("scalar", fn, w, [(a, 1)])
        return a.n

    def v(self, fn, waits=()):
        d = self.k.dve
        w = list(waits) + [(d, d.n)]
        d.inc(1)
        self._add("vector", fn, w, [(d, 1)])
        return d.n

    def dma(self, queue, pairs, sem: Sem, waits=()):
        h = sem.h

        def fn(e, pairs=tuple(pairs)):
            last = None
            for o, i in pairs:
                last = e.dma_start(out=o, in_=i).then_inc(h, 16)
            return last
        sem.inc(16 * len(pairs))
        self._add(queue, fn, waits, [])
        return sem.n

    def flush(self):
        blk = self.k.block
        for eng in Ph.ENGS:
            entries = self.q[eng]
            if not entries:
                continue

            def section(e, entries=entries):
                for waits, fn, incs in entries:
                    for s, val in waits:
                        if val > 0:
                            e.wait_ge(s.h, val)
                    inst = fn(e)
                    for s, amt in incs:
                        inst.then_inc(s.h, amt)
            getattr(blk, eng)(section)
        self.q = {e: [] for e in Ph.ENGS}
        self._gated = set()



def emit_load_xT(k, x_in, xT):
    ph = Ph(k, gate=k.gate())
    sem = k.usem("lx")
    ph.dma("sync", [(xT[:, dc * NTOK:(dc + 1) * NTOK],
                     x_in[dc * 128:(dc + 1) * 128, :]) for dc in range(DCN)],
           sem)
    ph.flush()
    return sem


def emit_store_fm(k, src, out_hbm, nrows, waits):
    ph = Ph(k, gate=k.gate())
    sem = k.usem("st")
    k.store_sems.append(sem)
    ph.dma("sync", [(out_hbm[r * 128:(r + 1) * 128, :],
                     src[:, r * NTOK:(r + 1) * NTOK])
                    for r in range(nrows // 128)], sem, waits=waits)
    ph.flush()
    return sem


def emit_vec_cols(k, src, n, dst, scale=None):
    ph = Ph(k, gate=k.gate())
    sem = k.usem("vc")
    nch = n // 128
    ph.dma("sync", [(dst[:, c:c + 1],
                     src[c * 128:(c + 1) * 128].rearrange("(p o) -> p o", o=1))
                    for c in range(nch)], sem)
    if scale is not None:
        ph.v(lambda v: v.tensor_scalar(out=dst[:], in0=dst[:], scalar1=scale,
                                       scalar2=None, op0=ALU.mult),
             waits=[(sem, sem.n)])
    ph.flush()
    return sem


def emit_ln(k, xT, w, ytgt, ycol, want_csum, csum_bf, ld_sem=None):
    nc = k.nc
    ph = Ph(k, gate=k.gate())
    k._useq += 1
    u = str(k._useq)
    with (
        nc.psum_tensor("ln_pa" + u, [128, CH], F32) as pa,
        nc.psum_tensor("ln_pb" + u, [128, CH], F32) as pb,
        nc.psum_tensor("ln_pc1" + u, [128, CH], F32) as pc1,
        nc.psum_tensor("ln_pc2" + u, [128, CH], F32) as pc2,
        nc.sbuf_tensor("ln_sq" + u, [128, 2 * CH], F32) as sq,
        nc.sbuf_tensor("ln_rows" + u, [1, 4 * CH], F32) as rows,
        nc.sbuf_tensor("ln_bc" + u, [128, 2 * CH], F32) as bc,
        nc.sbuf_tensor("ln_scr" + u, [128, CH], F32) as scr,
    ):
        ldw = [(ld_sem, ld_sem.n)] if ld_sem is not None else []
        sqb_marks = {}
        for dc in range(DCN):
            sw = list(ldw)
            if dc >= 2:
                sw.append((k.pe, sqb_marks[dc - 2]))
            am = ph.s(lambda s, dc=dc: s.activation(
                out=sq[:, (dc % 2) * CH:(dc % 2 + 1) * CH],
                in_=xT[:, col(dc, w)], func=AF.Square), waits=sw)
            ph.t(lambda t, dc=dc: t.matmul(
                pa[0:1, :], k.onec_f[:], xT[:, col(dc, w)],
                start=(dc == 0), stop=(dc == DCN - 1)), waits=ldw)
            sqb_marks[dc] = ph.t(lambda t, dc=dc: t.matmul(
                pb[0:1, :], k.onec_f[:],
                sq[:, (dc % 2) * CH:(dc % 2 + 1) * CH],
                start=(dc == 0), stop=(dc == DCN - 1)),
                waits=[(k.act, am)], inc=1)
        sums_pe = k.pe.n
        rm = rows[0:1, 0:CH]
        rt = rows[0:1, CH:2 * CH]
        rv = rows[0:1, 2 * CH:3 * CH]
        rr = rows[0:1, 3 * CH:4 * CH]
        ph.v(lambda v: v.tensor_scalar(out=rm, in0=pa[0:1, :],
                                       scalar1=1.0 / D, scalar2=None,
                                       op0=ALU.mult),
             waits=[(k.pe, sums_pe)])
        ph.v(lambda v: v.tensor_scalar(out=rt, in0=pb[0:1, :],
                                       scalar1=1.0 / D, scalar2=None,
                                       op0=ALU.mult))
        ph.v(lambda v: v.tensor_tensor(out=rv, in0=rm, in1=rm, op=ALU.mult))
        ph.v(lambda v: v.tensor_tensor(out=rv, in0=rt, in1=rv,
                                       op=ALU.subtract))
        vd = k.dve.n
        am = ph.s(lambda s: s.activation(out=rr, in_=rv, func=AF.Sqrt,
                                         bias=k.constcol(EPS)[0:1, 0:1]),
                  waits=[(k.dve, vd)])
        ph.v(lambda v: v.reciprocal(out=rv, in_=rr), waits=[(k.act, am)])
        ph.v(lambda v: v.scalar_tensor_tensor(out=rr, in0=rm, scalar=-1.0,
                                              in1=rv, op0=ALU.mult,
                                              op1=ALU.mult))
        vd2 = k.dve.n
        ph.t(lambda t: t.matmul(pc1[:], k.oner_f[:], rr, start=True,
                                stop=True), waits=[(k.dve, vd2)])
        ph.t(lambda t: t.matmul(pc2[:], k.oner_f[:], rv, start=True,
                                stop=True), inc=1)
        bc_pe = k.pe.n
        a1 = ph.s(lambda s: s.activation(out=bc[:, 0:CH], in_=pc1[:],
                                         func=AF.Copy),
                  waits=[(k.pe, bc_pe)])
        a2 = ph.s(lambda s: s.activation(out=bc[:, CH:2 * CH], in_=pc2[:],
                                         func=AF.Copy))
        y_marks = []
        for dc in range(DCN):
            ph.v(lambda v, dc=dc: v.tensor_tensor(
                out=scr[:], in0=xT[:, col(dc, w)], in1=bc[:, 0:CH],
                op=ALU.add), waits=[(k.act, a2)])
            y_marks.append(ph.v(lambda v, dc=dc: v.tensor_tensor(
                out=ytgt[:, ycol(dc)], in0=scr[:], in1=bc[:, CH:2 * CH],
                op=ALU.mult)))
        if want_csum:
            for dc in range(DCN):
                ph.t(lambda t, dc=dc: t.matmul(
                    pa[0:1, :], k.onec_b[:], ytgt[:, ycol(dc)],
                    start=(dc == 0), stop=(dc == DCN - 1)),
                    waits=[(k.dve, y_marks[-1])],
                    inc=(1 if dc == DCN - 1 else 0))
            cs_pe = k.pe.n
            ph.v(lambda v: v.tensor_copy(out=csum_bf, in_=pa[0:1, :]),
                 waits=[(k.pe, cs_pe)])
        ph.flush()
    return k.dve.n


def emit_ffn0(k, xT, yT, csum_bf, wgp_in, wup_in, wdp_in, sg_cols, su_cols,
              halfsd_cols, w, tag):
    nc = k.nc
    ph = Ph(k, gate=k.gate())
    with (
        nc.psum_tensor(f"f0g{tag}", [128, CH], F32) as pg,
        nc.psum_tensor(f"f0u{tag}", [128, CH], F32) as pu,
        nc.psum_tensor(f"f0g2{tag}", [128, CH], F32) as pg2,
        nc.psum_tensor(f"f0u2{tag}", [128, CH], F32) as pu2,
        nc.sbuf_tensor(f"f0wg{tag}", [128, 2 * D], BF16) as wgs,
        nc.sbuf_tensor(f"f0wu{tag}", [128, 2 * D], BF16) as wus,
        nc.sbuf_tensor(f"f0h{tag}", [128, HC0 * CH], BF16) as h_sb,
        nc.sbuf_tensor(f"f0sg{tag}", [128, 2 * CH], F32) as gsc,
        nc.sbuf_tensor(f"f0su{tag}", [128, 2 * CH], F32) as usc,
    ):
        lsem = k.usem(f"f0l{tag}")
        psG = [pg, pg2]
        psU = [pu, pu2]
        mm_marks = {}
        h_marks = {}
        for hc in range(HC0):
            b = hc % 2
            wwaits = [] if hc < 2 else [(k.pe, mm_marks[hc - 2])]
            ph.dma("gpsimd",
                   [(wgs[:, b * D:(b + 1) * D],
                     wgp_in[hc, :, :]),
                    (wus[:, b * D:(b + 1) * D],
                     wup_in[hc, :, :])], lsem, waits=wwaits)
            ld_n = lsem.n
            evw = [] if hc < 2 else [(k.act, h_marks[hc - 2][1])]

            def mm(t, hc=hc, b=b):
                for dc in range(DCN):
                    t.matmul(psG[b][:], wgs[:, b * D + dc * 128:
                                            b * D + (dc + 1) * 128],
                             yT[:, wcol(dc)], start=(dc == 0), stop=False)
                    t.matmul(psU[b][:], wus[:, b * D + dc * 128:
                                            b * D + (dc + 1) * 128],
                             yT[:, wcol(dc)], start=(dc == 0), stop=False)
                t.matmul(psG[b][:], k.neg128_b[:], csum_bf,
                         start=False, stop=True)
                return t.matmul(psU[b][:], k.neg128_b[:], csum_bf,
                                start=False, stop=True)
            mm_marks[hc] = ph.t(mm, waits=[(lsem, ld_n)] + evw, inc=1)
            gw = [(k.pe, mm_marks[hc])]
            if hc >= 2:
                gw.append((k.dve, h_marks[hc - 2][0]))
            a1 = ph.s(lambda s, hc=hc, b=b: s.activation(
                out=gsc[:, b * CH:(b + 1) * CH], in_=psG[b][:], func=AF.Silu,
                scale=sg_cols[:, hc:hc + 1]), waits=gw)
            a2 = ph.s(lambda s, hc=hc, b=b: s.mul(
                out=usc[:, b * CH:(b + 1) * CH], in_=psU[b][:],
                mul=su_cols[:, hc:hc + 1]))
            hm = ph.v(lambda v, hc=hc, b=b: v.tensor_tensor(
                out=h_sb[:, hc * CH:(hc + 1) * CH],
                in0=gsc[:, b * CH:(b + 1) * CH],
                in1=usc[:, b * CH:(b + 1) * CH], op=ALU.mult),
                waits=[(k.act, a2)])
            h_marks[hc] = (hm, a2)
        h_done = k.dve.n

        for hc in range(HC0):
            ph.t(lambda t, hc=hc: t.matmul(
                pg[0:1, :], k.onec_b[:], h_sb[:, hc * CH:(hc + 1) * CH],
                start=(hc == 0), stop=(hc == HC0 - 1)),
                waits=[(k.dve, h_done)], inc=(1 if hc == HC0 - 1 else 0))
        hs_pe = k.pe.n
        hsr = gsc[0:1, 0:CH]
        ph.v(lambda v: v.tensor_scalar(out=hsr, in0=pg[0:1, :],
                                       scalar1=-128.0, scalar2=None,
                                       op0=ALU.mult), waits=[(k.pe, hs_pe)])
        hs_v = k.dve.n
        ph.t(lambda t: t.matmul(pg2[:], k.oner_f[:], hsr, start=True,
                                stop=True), waits=[(k.dve, hs_v)], inc=1)
        hb_pe = k.pe.n
        hbm = ph.s(lambda s: s.activation(out=usc[:, 0:CH], in_=pg2[:],
                                          func=AF.Copy),
                   waits=[(k.pe, hb_pe)])

        dsem = k.usem(f"f0d{tag}")
        with nc.sbuf_tensor(f"f0wd{tag}", [128, 2 * HID0], BF16) as wds:
            dn_marks = {}
            x_marks = {}
            for dc in range(DCN):
                b = dc % 2
                wwaits = [] if dc < 2 else [(k.pe, dn_marks[dc - 2])]
                ph.dma("gpsimd", [(wds[:, b * HID0:(b + 1) * HID0],
                                   wdp_in[dc, :, :])], dsem, waits=wwaits)
                ld_n = dsem.n
                if dc < 2:
                    evw = [(k.dve, hs_v), (k.act, hbm)]
                else:
                    evw = [(k.dve, x_marks[dc - 2])]

                def mmd(t, dc=dc, b=b):
                    last = None
                    for hc in range(HC0):
                        last = t.matmul(
                            psG[b][:], wds[:, b * HID0 + hc * 128:
                                           b * HID0 + (hc + 1) * 128],
                            h_sb[:, hc * CH:(hc + 1) * CH],
                            start=(hc == 0), stop=(hc == HC0 - 1))
                    return last
                dn_marks[dc] = ph.t(mmd, waits=[(dsem, ld_n)] + evw, inc=1)
                ph.v(lambda v, b=b: v.tensor_tensor(
                    out=gsc[:, CH:2 * CH], in0=psG[b][:], in1=usc[:, 0:CH],
                    op=ALU.add), waits=[(k.pe, dn_marks[dc]), (k.act, hbm)])
                x_marks[dc] = ph.v(lambda v, dc=dc: v.scalar_tensor_tensor(
                    out=xT[:, col(dc, w)], in0=gsc[:, CH:2 * CH],
                    scalar=halfsd_cols[:, dc:dc + 1], in1=xT[:, col(dc, w)],
                    op0=ALU.mult, op1=ALU.add))
            ph.flush()
    return k.dve.n


def emit_ffn1(k, xT, yT, wgp_in, wup_in, wdp_in, w, tag):
    nc = k.nc
    ph = Ph(k, gate=k.gate())
    gsc_es = contextlib.ExitStack()
    EG = gsc_es.enter_context
    with (
        nc.psum_tensor(f"f1a{tag}", [128, CH], F32) as p0,
        nc.psum_tensor(f"f1b{tag}", [128, CH], F32) as p1,
        nc.psum_tensor(f"f1c{tag}", [128, CH], F32) as p2,
        nc.psum_tensor(f"f1d{tag}", [128, CH], F32) as p3,
        nc.sbuf_tensor(f"f1h{tag}", [128, HC1 * CH], BF16) as h_sb,
    ):
        wgs = EG(nc.sbuf_tensor(f"f1wg{tag}", [128, 2 * D], BF16))
        wus = EG(nc.sbuf_tensor(f"f1wu{tag}", [128, 2 * D], BF16))
        h4 = EG(nc.sbuf_tensor(f"f1h4{tag}", [128, 4 * CH], F32))
        sq = EG(nc.sbuf_tensor(f"f1sq{tag}", [128, 4 * CH], F32))
        pr = EG(nc.sbuf_tensor(f"f1pr{tag}", [128, 3 * CH], F32))
        geb = EG(nc.sbuf_tensor(f"f1ge{tag}", [128, CH], F32))
        u1 = EG(nc.sbuf_tensor(f"f1u1{tag}", [128, CH], F32))
        lsem = k.usem(f"f1l{tag}")
        psG = [p0, p1]
        psU = [p2, p3]
        mm_marks = {}
        h4_marks = {}
        for c in range(DCN):
            for r in range(4):
                it = c * 4 + r
                b = it % 2
                wwaits = [] if it < 2 else [(k.pe, mm_marks[it - 2])]
                ph.dma("gpsimd",
                       [(wgs[:, b * D:(b + 1) * D], wgp_in[it, :, :]),
                        (wus[:, b * D:(b + 1) * D], wup_in[it, :, :])],
                       lsem, waits=wwaits)
                ld_n = lsem.n
                evw = [] if it < 2 else [(k.dve, h4_marks[it - 2])]

                def mm(t, b=b):
                    for dc in range(DCN):
                        t.matmul(psG[b][:], wgs[:, b * D + dc * 128:
                                                b * D + (dc + 1) * 128],
                                 yT[:, wcol(dc)], start=(dc == 0),
                                 stop=(dc == DCN - 1))
                        last = t.matmul(
                            psU[b][:], wus[:, b * D + dc * 128:
                                           b * D + (dc + 1) * 128],
                            yT[:, wcol(dc)], start=(dc == 0),
                            stop=(dc == DCN - 1))
                    return last
                mm_marks[it] = ph.t(mm, waits=[(lsem, ld_n)] + evw, inc=1)
                gw = [(k.pe, mm_marks[it])]
                if c >= 1:
                    gw.append((k.dve, h4_marks[(c - 1) * 4 + 3]))
                a1 = ph.s(lambda s, b=b, r=r: s.activation(
                    out=h4[:, r * CH:(r + 1) * CH], in_=psG[b][:],
                    func=AF.Silu), waits=gw)
                uw = [] if it == 0 else [(k.dve, h4_marks[it - 1])]
                a2 = ph.s(lambda s, b=b: s.activation(
                    out=u1[:], in_=psU[b][:], func=AF.Copy), waits=uw)
                h4_marks[it] = ph.v(lambda v, r=r: v.tensor_tensor(
                    out=h4[:, r * CH:(r + 1) * CH],
                    in0=h4[:, r * CH:(r + 1) * CH], in1=u1[:], op=ALU.mult),
                    waits=[(k.act, a2)])
            hs = [h4[:, r * CH:(r + 1) * CH] for r in range(4)]
            sqs = [sq[:, r * CH:(r + 1) * CH] for r in range(4)]
            for r in range(4):
                ph.v(lambda v, r=r: v.tensor_tensor(
                    out=sqs[r], in0=hs[r], in1=hs[r], op=ALU.mult))
            pr0 = pr[:, 0:CH]
            pr1 = pr[:, CH:2 * CH]
            pr2 = pr[:, 2 * CH:3 * CH]
            ph.v(lambda v: v.tensor_tensor(out=pr0, in0=sqs[0], in1=sqs[1],
                                           op=ALU.max))
            ph.v(lambda v: v.tensor_tensor(out=pr1, in0=sqs[2], in1=sqs[3],
                                           op=ALU.max))
            ph.v(lambda v: v.tensor_tensor(out=pr0, in0=pr0, in1=pr1,
                                           op=ALU.min))
            ph.v(lambda v: v.tensor_tensor(out=pr1, in0=sqs[0], in1=sqs[1],
                                           op=ALU.min))
            ph.v(lambda v: v.tensor_tensor(out=pr2, in0=sqs[2], in1=sqs[3],
                                           op=ALU.min))
            ph.v(lambda v: v.tensor_tensor(out=pr1, in0=pr1, in1=pr2,
                                           op=ALU.max))
            ph.v(lambda v: v.tensor_tensor(out=pr0, in0=pr0, in1=pr1,
                                           op=ALU.max))
            for r in range(4):
                ph.v(lambda v, r=r: v.tensor_tensor(
                    out=geb[:], in0=sqs[r], in1=pr0, op=ALU.is_ge))
                m = ph.v(lambda v, r=r, c=c: v.tensor_tensor(
                    out=h_sb[:, (c * 4 + r) * CH:(c * 4 + r + 1) * CH],
                    in0=hs[r], in1=geb[:], op=ALU.mult))
                h4_marks[c * 4 + r] = m
        h_done = k.dve.n
        last_mm = mm_marks[DCN * 4 - 1]
        gsc_es.close()

        dsem = k.usem(f"f1ds{tag}")
        with nc.sbuf_tensor(f"f1wd{tag}", [128, 2 * HID1], BF16) as wds:
            dn_marks = {}
            x_marks = {}
            for dc in range(DCN):
                b = dc % 2
                wwaits = [(k.pe, last_mm), (k.dve, h_done)] if dc < 2 \
                    else [(k.pe, dn_marks[dc - 2])]
                ph.dma("gpsimd", [(wds[:, b * HID1:(b + 1) * HID1],
                                   wdp_in[dc, :, :])], dsem, waits=wwaits)
                ld_n = dsem.n
                evw = [(k.dve, h_done)] if dc < 2 else \
                    [(k.dve, x_marks[dc - 2])]

                def mmd(t, b=b):
                    last = None
                    for hc in range(HC1):
                        last = t.matmul(
                            psG[b][:], wds[:, b * HID1 + hc * 128:
                                           b * HID1 + (hc + 1) * 128],
                            h_sb[:, hc * CH:(hc + 1) * CH],
                            start=(hc == 0), stop=(hc == HC1 - 1))
                    return last
                dn_marks[dc] = ph.t(mmd, waits=[(dsem, ld_n)] + evw, inc=1)
                x_marks[dc] = ph.v(lambda v, dc=dc, b=b: v.scalar_tensor_tensor(
                    out=xT[:, col(dc, w)], in0=psG[b][:], scalar=0.5,
                    in1=xT[:, col(dc, w)], op0=ALU.mult, op1=ALU.add),
                    waits=[(k.pe, dn_marks[dc])])
            ph.flush()
    return k.dve.n


def emit_qkproj(k, yT, wpack_in, mcn, out_hbm, scale_col, tag, wpack2_in=None,
                ctab=None, stab=None):
    skw = {} if scale_col is None else {"scale": scale_col}
    """out_hbm[m*128.., w*CH..] = bf16(scale * W.T @ y)  (feature-major).
    If wpack2_in given: RoPE combine  qa*C + qb*S  (both scaled)."""
    nc = k.nc
    ph = Ph(k, gate=k.gate())
    rope = wpack2_in is not None
    with (
        nc.psum_tensor(f"qp0{tag}", [128, CH], F32) as p0,
        nc.psum_tensor(f"qp1{tag}", [128, CH], F32) as p1,
        nc.psum_tensor(f"qp2{tag}", [128, CH], F32) as p2,
        nc.psum_tensor(f"qp3{tag}", [128, CH], F32) as p3,
        nc.sbuf_tensor(f"qw{tag}", [128, 2 * D], BF16) as ws,
        nc.sbuf_tensor(f"qw2{tag}", [128, 2 * D], BF16) as ws2,
        nc.sbuf_tensor(f"qstg{tag}", [128, 2 * CH], BF16) as qstg,
        nc.sbuf_tensor(f"qsc{tag}", [128, 3 * CH], F32) as qsc,
    ):
        lsem = k.usem(f"qpl{tag}")
        ssem = k.usem(f"qps{tag}")
        k.store_sems.append(ssem)
        psA = [p0, p1]
        psB = [p2, p3]
        mm_marks = {}
        prev_vm = 0
        for m in range(mcn):
            b = m % 2
            wwaits = [] if m < 2 else [(k.pe, mm_marks[(m - 2, NW - 1)])]
            pairs = [(ws[:, b * D:(b + 1) * D], wpack_in[m, :, :])]
            if rope:
                pairs.append((ws2[:, b * D:(b + 1) * D], wpack2_in[m, :, :]))
            ph.dma("gpsimd", pairs, lsem, waits=wwaits)
            ld_n = lsem.n
            for w in range(NW):
                i = m * NW + w
                bb = i % 2

                def mm(t, b=b, bb=bb, w=w):
                    last = None
                    for dc in range(DCN):
                        last = t.matmul(
                            psA[bb][:], ws[:, b * D + dc * 128:
                                           b * D + (dc + 1) * 128],
                            yT[:, col(dc, w)], start=(dc == 0),
                            stop=(dc == DCN - 1))
                        if rope:
                            last = t.matmul(
                                psB[bb][:], ws2[:, b * D + dc * 128:
                                                b * D + (dc + 1) * 128],
                                yT[:, col(dc, w)], start=(dc == 0),
                                stop=(dc == DCN - 1))
                    return last
                evw = [] if i < 2 else [(ssem, 16 * (i - 1))]
                mm_marks[(m, w)] = ph.t(mm, waits=[(lsem, ld_n)] + evw,
                                        inc=1)
                if not rope:
                    am = ph.s(lambda s, bb=bb, i=i: s.activation(
                        out=qstg[:, (i % 2) * CH:(i % 2 + 1) * CH],
                        in_=psA[bb][:], func=AF.Copy, **skw),
                        waits=[(k.pe, mm_marks[(m, w)])])
                    gate_done = (k.act, am)
                else:
                    ph.s(lambda s, bb=bb: s.activation(
                        out=qsc[:, 0:CH], in_=psA[bb][:], func=AF.Copy,
                        **skw),
                        waits=[(k.pe, mm_marks[(m, w)]), (k.dve, prev_vm)])
                    am = ph.s(lambda s, bb=bb: s.activation(
                        out=qsc[:, CH:2 * CH], in_=psB[bb][:], func=AF.Copy,
                        **skw))
                    ph.v(lambda v, w=w: v.tensor_tensor(
                        out=qsc[:, 0:CH], in0=qsc[:, 0:CH],
                        in1=ctab[:, w * CH:(w + 1) * CH], op=ALU.mult),
                        waits=[(k.act, am)])
                    ph.v(lambda v, w=w: v.tensor_tensor(
                        out=qsc[:, 2 * CH:3 * CH], in0=qsc[:, CH:2 * CH],
                        in1=stab[:, w * CH:(w + 1) * CH], op=ALU.mult))
                    vm = ph.v(lambda v, i=i: v.tensor_tensor(
                        out=qstg[:, (i % 2) * CH:(i % 2 + 1) * CH],
                        in0=qsc[:, 0:CH], in1=qsc[:, 2 * CH:3 * CH],
                        op=ALU.add))
                    prev_vm = vm
                    gate_done = (k.dve, vm)
                ph.dma("sync", [(out_hbm[m * 128:(m + 1) * 128,
                                         w * CH:(w + 1) * CH],
                                 qstg[:, (i % 2) * CH:(i % 2 + 1) * CH])],
                       ssem, waits=[gate_done])
        ph.flush()
    return ssem


def emit_vproj(k, yT, wv_in, out_hbm, tag):
    nc = k.nc
    ph = Ph(k, gate=k.gate())
    with (
        nc.psum_tensor(f"vp0{tag}", [128, CH], F32) as p0,
        nc.psum_tensor(f"vp1{tag}", [128, CH], F32) as p1,
        nc.sbuf_tensor(f"vw{tag}", [128, DCN * CH], BF16) as wv_sb,
        nc.sbuf_tensor(f"vstg{tag}", [128, 2 * CH], BF16) as vstg,
    ):
        lsem = k.usem(f"vpl{tag}")
        ssem = k.usem(f"vps{tag}")
        k.store_sems.append(ssem)
        ph.dma("gpsimd", [(wv_sb[:, dc * CH:(dc + 1) * CH],
                           wv_in[dc * 128:(dc + 1) * 128, :])
                          for dc in range(DCN)], lsem)
        ld_n = lsem.n
        ps = [p0, p1]
        mm_marks = {}
        for tt in range(NTOK // 128):
            b = tt % 2
            evw = [] if tt < 2 else [(k.act, mm_marks[tt - 2])]

            def mm(t, tt=tt, b=b):
                last = None
                for dc in range(DCN):
                    last = t.matmul(
                        ps[b][:], yT[:, dc * NTOK + tt * 128:
                                     dc * NTOK + (tt + 1) * 128],
                        wv_sb[:, dc * CH:(dc + 1) * CH],
                        start=(dc == 0), stop=(dc == DCN - 1))
                return last
            pm = ph.t(mm, waits=[(lsem, ld_n)] + evw, inc=1)
            stw = [] if tt < 2 else [(ssem, 16 * (tt - 1))]
            am = ph.s(lambda s, tt=tt, b=b: s.activation(
                out=vstg[:, b * CH:(b + 1) * CH], in_=ps[b][:],
                func=AF.Copy), waits=[(k.pe, pm)] + stw)
            mm_marks[tt] = am
            ph.dma("sync", [(out_hbm[tt * 128:(tt + 1) * 128, :],
                             vstg[:, b * CH:(b + 1) * CH])], ssem,
                   waits=[(k.act, am)])
        ph.flush()
    return ssem


def emit_attn(k, xT, q_in, k_ins, v_ins, flags_sb, md_sb, wo_pack_in,
              hd, nh, bout_cols=None, clamp=True, tag="a"):
    nc = k.nc
    kv_rep = 4
    with (
        nc.psum_tensor(f"as0{tag}", [128, CH], F32) as ps0,
        nc.psum_tensor(f"as1{tag}", [128, CH], F32) as ps1,
        nc.psum_tensor(f"ac0{tag}", [128, CH], F32) as pc0,
        nc.psum_tensor(f"ac1{tag}", [128, CH], F32) as pc1,
        nc.psum_tensor(f"ar0{tag}", [128, CH], F32) as pr0,
        nc.psum_tensor(f"ar1{tag}", [128, CH], F32) as pr1,
        nc.psum_tensor(f"ab{tag}", [128, CH], F32) as pb,
        nc.sbuf_tensor(f"aq{tag}", [128 if hd == 128 else 64,
                                     (DCN if hd == 128 else nh) * CH],
                       BF16) as qw_sb,
        nc.sbuf_tensor(f"akl{tag}", [128, 2 * CH], BF16) as ksl,
        nc.sbuf_tensor(f"avl{tag}", [128, 2 * 4 * hd], BF16) as vsl,
        nc.sbuf_tensor(f"aet{tag}", [128, 2 * CH], F32) as et,
        nc.sbuf_tensor(f"apt{tag}", [128, 2 * CH], BF16) as pt,
        nc.sbuf_tensor(f"actx{tag}", [128, DCN * CH], BF16) as ctx_sb,
        nc.sbuf_tensor(f"abc{tag}", [128, CH], F32) as bcr,
        nc.sbuf_tensor(f"ascr{tag}", [128, CH], F32) as scrD,
        nc.sbuf_tensor(f"arr{tag}", [1, CH], F32) as rrow,
        nc.sbuf_tensor(f"awo{tag}", [128, 2 * D], BF16) as wos,
    ):
        psS = [ps0, ps1]
        psC = [pc0, pc1]
        psR = [pr0, pr1]
        for w in range(NW):
            ph = Ph(k, gate=k.gate())
            qsem = k.usem(f"aq{tag}{w}")
            if hd == 128:
                qpairs = [(qw_sb[:, m * CH:(m + 1) * CH],
                           q_in[m * 128:(m + 1) * 128,
                                w * CH:(w + 1) * CH]) for m in range(DCN)]
            else:
                qpairs = [(qw_sb[0:64, h * CH:(h + 1) * CH],
                           q_in[h * 64:(h + 1) * 64,
                                w * CH:(w + 1) * CH]) for h in range(nh)]
            ph.dma("sync", qpairs, qsem)
            q_n = qsem.n
            kvsem = k.usem(f"akv{tag}{w}")
            exp_marks = {}
            pt_marks = {}
            den_marks = {}
            den_marks_fine = {}
            ctx_done = {}
            prev_bm = 0
            prev_am = 0
            for h in range(nh):
                kv = h // kv_rep
                if hd == 128:
                    qrhs = qw_sb[:, h * CH:(h + 1) * CH]
                else:
                    qrhs = qw_sb[0:64, h * CH:(h + 1) * CH]
                for s in range(16):
                    b = s % 2
                    it = h * 16 + s
                    wwaits = [] if it < 2 else \
                        [(k.pe, den_marks[it - 2])]
                    if hd == 128:
                        kpair = (ksl[:, b * CH:(b + 1) * CH],
                                 k_ins[w][kv * 128:(kv + 1) * 128,
                                          s * CH:(s + 1) * CH])
                    else:
                        kpair = (ksl[0:64, b * CH:(b + 1) * CH],
                                 k_ins[w][0:64, (kv * 16 + s) * CH:
                                          (kv * 16 + s + 1) * CH])
                    vpairs = [(vsl[:, (b * 4 + ks) * hd:
                                   (b * 4 + ks + 1) * hd],
                               v_ins[w][s * CH + ks * 128:
                                        s * CH + (ks + 1) * 128,
                                        kv * hd:(kv + 1) * hd])
                              for ks in range(4)]
                    ph.dma("sync", [kpair] + vpairs, kvsem, waits=wwaits)
                    kv_n = kvsem.n
                    for ks in range(4):
                        i4 = it * 4 + ks
                        sb = ks % 2
                        sw = [(kvsem, kv_n), (qsem, q_n)]
                        if i4 >= 2:
                            sw.append((k.act, exp_marks[i4 - 2]))

                        def mmS(t, b=b, sb=sb, ks=ks, qrhs=qrhs):
                            if hd == 128:
                                lhs = ksl[:, b * CH + ks * 128:
                                          b * CH + ks * 128 + 128]
                            else:
                                lhs = ksl[0:64, b * CH + ks * 128:
                                          b * CH + ks * 128 + 128]
                            return t.matmul(psS[sb][:], lhs, qrhs,
                                            start=True, stop=True)
                        sm = ph.t(mmS, waits=sw, inc=1)
                        eww = [] if i4 < 2 else [(k.dve, pt_marks[i4 - 2])]
                        if s == 15:
                            dv = ph.v(lambda v, sb=sb, ks=ks: v.tensor_tensor(
                                out=scrD[:], in0=psS[sb][:],
                                in1=md_sb[:, ks * CH:(ks + 1) * CH],
                                op=ALU.add),
                                waits=[(k.pe, sm), (k.act, k.act.n)])
                            em = ph.s(lambda s_, sb=sb: s_.activation(
                                out=et[:, sb * CH:(sb + 1) * CH],
                                in_=scrD[:], func=AF.Exp),
                                waits=[(k.dve, dv)] + eww)
                        else:
                            em = ph.s(lambda s_, sb=sb, fcol=w * 16 + s:
                                      s_.activation(
                                          out=et[:, sb * CH:(sb + 1) * CH],
                                          in_=psS[sb][:], func=AF.Exp,
                                          bias=flags_sb[:, fcol:fcol + 1]),
                                      waits=[(k.pe, sm)] + eww)
                        exp_marks[i4] = em
                        ptw = [(k.act, em)]
                        if i4 >= 2:
                            ptw.append((k.pe, den_marks_fine[i4 - 2]))
                        if clamp:
                            pm = ph.v(lambda v, sb=sb: v.tensor_scalar(
                                out=pt[:, sb * CH:(sb + 1) * CH],
                                in0=et[:, sb * CH:(sb + 1) * CH],
                                scalar1=E50P, scalar2=E50N, op0=ALU.min,
                                op1=ALU.max), waits=ptw)
                        else:
                            pm = ph.v(lambda v, sb=sb: v.tensor_scalar(
                                out=pt[:, sb * CH:(sb + 1) * CH],
                                in0=et[:, sb * CH:(sb + 1) * CH],
                                scalar1=FBIG, scalar2=None, op0=ALU.min),
                                waits=ptw)
                        pt_marks[i4] = pm
                        cw = [(k.dve, pm), (kvsem, kv_n)]
                        if s == 0 and ks == 0 and h >= 2:
                            cw.append((k.dve, ctx_done[h - 2]))
                        st_ = (s == 0 and ks == 0)
                        sp_ = (s == 15 and ks == 3)

                        def mmC(t, b=b, sb=sb, ks=ks, h=h, st_=st_, sp_=sp_):
                            t.matmul(psC[h % 2][0:hd, :],
                                     vsl[:, (b * 4 + ks) * hd:
                                         (b * 4 + ks + 1) * hd],
                                     pt[:, sb * CH:(sb + 1) * CH],
                                     start=st_, stop=sp_)
                            return t.matmul(
                                psR[h % 2][0:1, :], k.onec_b[:],
                                pt[:, sb * CH:(sb + 1) * CH],
                                start=st_, stop=sp_)
                        dm = ph.t(mmC, waits=cw, inc=1)
                        den_marks_fine[i4] = dm
                    den_marks[it] = den_marks_fine[it * 4 + 3]
                dpe = den_marks[h * 16 + 15]
                rw = [(k.pe, dpe)]
                if h >= 1:
                    rw.append((k.pe, prev_bm))
                ph.v(lambda v, h=h: v.reciprocal(
                    out=rrow[:], in_=psR[h % 2][0:1, :]), waits=rw)
                rv = k.dve.n
                bw = [(k.dve, rv)]
                if h >= 1:
                    bw.append((k.act, prev_am))
                bm = ph.t(lambda t: t.matmul(
                    pb[0:hd, :], k.oner_f[0:1, 0:hd], rrow[:],
                    start=True, stop=True), waits=bw, inc=1)
                prev_bm = bm
                aw = [(k.pe, bm)]
                if h >= 1:
                    aw.append((k.dve, ctx_done[h - 1]))
                am = ph.s(lambda s: s.activation(
                    out=bcr[0:hd, :], in_=pb[0:hd, :], func=AF.Copy),
                    waits=aw)
                prev_am = am
                if hd == 128:
                    cdst = ctx_sb[:, h * CH:(h + 1) * CH]
                    csrc = psC[h % 2][:]
                    bsrc = bcr[:]
                else:
                    cdst = ctx_sb[(h % 2) * 64:(h % 2) * 64 + 64,
                                  (h // 2) * CH:(h // 2 + 1) * CH]
                    csrc = psC[h % 2][0:64, :]
                    bsrc = bcr[0:64, :]
                ctx_done[h] = ph.v(lambda v, cdst=cdst, csrc=csrc,
                                   bsrc=bsrc: v.tensor_tensor(
                    out=cdst, in0=csrc, in1=bsrc, op=ALU.mult),
                    waits=[(k.act, am)])
            ctx_all = k.dve.n

            osem = k.usem(f"ao{tag}{w}")
            op_marks = {}
            x_marks = {}
            for mc in range(DCN):
                b = mc % 2
                wwaits = [] if mc < 2 else [(k.pe, op_marks[mc - 2])]
                ph.dma("gpsimd", [(wos[:, b * D:(b + 1) * D],
                                   wo_pack_in[mc, :, :])], osem,
                       waits=wwaits)
                ld_n = osem.n
                evw = [(k.dve, ctx_all)] if mc < 2 else \
                    [(k.dve, x_marks[mc - 2])]

                def mmO(t, b=b):
                    last = None
                    for kc in range(DCN):
                        last = t.matmul(
                            psS[b][:], wos[:, b * D + kc * 128:
                                           b * D + (kc + 1) * 128],
                            ctx_sb[:, kc * CH:(kc + 1) * CH],
                            start=(kc == 0), stop=(kc == DCN - 1))
                    return last
                op_marks[mc] = ph.t(mmO, waits=[(osem, ld_n)] + evw, inc=1)
                if bout_cols is None:
                    x_marks[mc] = ph.v(lambda v, mc=mc, b=b: v.tensor_tensor(
                        out=xT[:, col(mc, w)], in0=psS[b][:],
                        in1=xT[:, col(mc, w)], op=ALU.add),
                        waits=[(k.pe, op_marks[mc])])
                else:
                    x_marks[mc] = ph.v(
                        lambda v, mc=mc, b=b: v.scalar_tensor_tensor(
                            out=xT[:, col(mc, w)], in0=psS[b][:],
                            scalar=bout_cols[:, mc:mc + 1],
                            in1=xT[:, col(mc, w)], op0=ALU.add, op1=ALU.add),
                        waits=[(k.pe, op_marks[mc])])
            ph.flush()
    return k.dve.n


def emit_load_misc(k, pairs):
    ph = Ph(k, gate=k.gate())
    sem = k.usem("misc")
    ph.dma("sync", pairs, sem)
    ph.flush()
    return sem


def emit_allgather(k, pairs):
    ph = Ph(k, gate=k.gate())
    sem = k.usem("cc")
    for ih, oh in pairs:
        def fn(g, ih=ih, oh=oh):
            return g.collective_compute(
                "AllGather", ALU.bypass,
                replica_groups=[list(range(NC))],
                ins=[ih.ap().opt()], outs=[oh.ap().opt()])
        sem.inc(1)
        ph._add("gpsimd", fn, [], [(sem, 1)])
    ph.flush()
    return sem


def emit_attn_f(k, xT, q_in, kg, vg, ksh, vsh, flags_sb, md_sb, wo_pack_in,
                hd, nh, bout_cols=None, clamp=True, tag="a"):
    nc = k.nc
    kv_rep = 4
    with (
        nc.psum_tensor(f"as0{tag}", [128, CH], F32) as ps0,
        nc.psum_tensor(f"as1{tag}", [128, CH], F32) as ps1,
        nc.psum_tensor(f"ac0{tag}", [128, CH], F32) as pc0,
        nc.psum_tensor(f"ac1{tag}", [128, CH], F32) as pc1,
        nc.psum_tensor(f"ar0{tag}", [128, CH], F32) as pr0,
        nc.psum_tensor(f"ar1{tag}", [128, CH], F32) as pr1,
        nc.psum_tensor(f"ab{tag}", [128, CH], F32) as pb,
        nc.sbuf_tensor(f"aq{tag}", [128 if hd == 128 else 64,
                                    (DCN if hd == 128 else nh) * CH],
                       BF16) as qw_sb,
        nc.sbuf_tensor(f"akl{tag}", [128, 2 * CH], BF16) as ksl,
        nc.sbuf_tensor(f"avl{tag}", [128, 2 * 4 * hd], BF16) as vsl,
        nc.sbuf_tensor(f"aet{tag}", [128, 2 * CH], F32) as et,
        nc.sbuf_tensor(f"apt{tag}", [128, 2 * CH], BF16) as pt,
        nc.sbuf_tensor(f"actx{tag}", [128, DCN * CH], BF16) as ctx_sb,
        nc.sbuf_tensor(f"abc{tag}", [128, CH], F32) as bcr,
        nc.sbuf_tensor(f"ascr{tag}", [128, CH], F32) as scrD,
        nc.sbuf_tensor(f"arr{tag}", [1, CH], F32) as rrow,
        nc.sbuf_tensor(f"awo{tag}", [128, 2 * D], BF16) as wos,
    ):
        psS = [ps0, ps1]
        psC = [pc0, pc1]
        psR = [pr0, pr1]
        NS = 17
        for w in range(NW):
            ph = Ph(k, gate=k.gate())
            qsem = k.usem(f"aq{tag}{w}")
            if hd == 128:
                qpairs = [(qw_sb[:, m * CH:(m + 1) * CH],
                           q_in[m * 128:(m + 1) * 128,
                                w * CH:(w + 1) * CH]) for m in range(DCN)]
            else:
                qpairs = [(qw_sb[0:64, h * CH:(h + 1) * CH],
                           q_in[h * 64:(h + 1) * 64,
                                w * CH:(w + 1) * CH]) for h in range(nh)]
            ph.dma("sync", qpairs, qsem)
            q_n = qsem.n
            kvsem = k.usem(f"akv{tag}{w}")
            exp_marks = {}
            pt_marks = {}
            den_marks = {}
            den_marks_fine = {}
            ctx_done = {}
            prev_bm = 0
            prev_am = 0
            for h in range(nh):
                kv = h // kv_rep
                if hd == 128:
                    qrhs = qw_sb[:, h * CH:(h + 1) * CH]
                else:
                    qrhs = qw_sb[0:64, h * CH:(h + 1) * CH]
                for s in range(NS):
                    diag = (s == NS - 1)
                    it = h * NS + s
                    b = it % 2
                    wwaits = [] if it < 2 else \
                        [(k.pe, den_marks[it - 2])]
                    if diag:
                        if hd == 128:
                            ksrc = ksh[kv * 128:(kv + 1) * 128,
                                       w * CH:(w + 1) * CH]
                        else:
                            ksrc = ksh[kv * 64:kv * 64 + 64,
                                       w * CH:(w + 1) * CH]
                        vsrcs = [vsh[w * CH + ks * 128:
                                     w * CH + (ks + 1) * 128,
                                     kv * hd:(kv + 1) * hd]
                                 for ks in range(4)]
                    else:
                        r, lc = s // 2, (s % 2) * CH
                        if hd == 128:
                            ksrc = kg[r * 512 + kv * 128:
                                      r * 512 + (kv + 1) * 128,
                                      lc:lc + CH]
                        else:
                            ksrc = kg[r * 512 + kv * 64:
                                      r * 512 + kv * 64 + 64,
                                      lc:lc + CH]
                        vsrcs = [vg[r * NTOK + lc + ks * 128:
                                    r * NTOK + lc + (ks + 1) * 128,
                                    kv * hd:(kv + 1) * hd]
                                 for ks in range(4)]
                    if hd == 128:
                        kpair = (ksl[:, b * CH:(b + 1) * CH], ksrc)
                    else:
                        kpair = (ksl[0:64, b * CH:(b + 1) * CH], ksrc)
                    vpairs = [(vsl[:, (b * 4 + ks) * hd:
                                   (b * 4 + ks + 1) * hd], vsrcs[ks])
                              for ks in range(4)]
                    ph.dma("sync", [kpair] + vpairs, kvsem, waits=wwaits)
                    kv_n = kvsem.n
                    for ks in range(4):
                        i4 = it * 4 + ks
                        sb = ks % 2
                        sw = [(kvsem, kv_n), (qsem, q_n)]
                        if i4 >= 2:
                            sw.append((k.act, exp_marks[i4 - 2]))

                        def mmS(t, b=b, sb=sb, ks=ks, qrhs=qrhs):
                            if hd == 128:
                                lhs = ksl[:, b * CH + ks * 128:
                                          b * CH + ks * 128 + 128]
                            else:
                                lhs = ksl[0:64, b * CH + ks * 128:
                                          b * CH + ks * 128 + 128]
                            return t.matmul(psS[sb][:], lhs, qrhs,
                                            start=True, stop=True)
                        sm = ph.t(mmS, waits=sw, inc=1)
                        eww = [] if i4 < 2 else [(k.dve, pt_marks[i4 - 2])]
                        if diag:
                            dv = ph.v(lambda v, sb=sb, ks=ks: v.tensor_tensor(
                                out=scrD[:], in0=psS[sb][:],
                                in1=md_sb[:, ks * CH:(ks + 1) * CH],
                                op=ALU.add),
                                waits=[(k.pe, sm), (k.act, k.act.n)])
                            em = ph.s(lambda s_, sb=sb: s_.activation(
                                out=et[:, sb * CH:(sb + 1) * CH],
                                in_=scrD[:], func=AF.Exp),
                                waits=[(k.dve, dv)] + eww)
                        else:
                            em = ph.s(lambda s_, sb=sb, fcol=w * NS + s:
                                      s_.activation(
                                          out=et[:, sb * CH:(sb + 1) * CH],
                                          in_=psS[sb][:], func=AF.Exp,
                                          bias=flags_sb[:, fcol:fcol + 1]),
                                      waits=[(k.pe, sm)] + eww)
                        exp_marks[i4] = em
                        ptw = [(k.act, em)]
                        if i4 >= 2:
                            ptw.append((k.pe, den_marks_fine[i4 - 2]))
                        if clamp:
                            if diag:
                                ph.v(lambda v, sb=sb: v.tensor_scalar(
                                    out=scrD[:],
                                    in0=et[:, sb * CH:(sb + 1) * CH],
                                    scalar1=E50P, scalar2=E50N, op0=ALU.min,
                                    op1=ALU.max), waits=ptw)
                                pm = ph.v(lambda v, sb=sb: v.tensor_scalar(
                                    out=pt[:, sb * CH:(sb + 1) * CH],
                                    in0=scrD[:], scalar1=E50N, scalar2=None,
                                    op0=ALU.subtract))
                            else:
                                pm = ph.v(lambda v, sb=sb: v.tensor_scalar(
                                    out=pt[:, sb * CH:(sb + 1) * CH],
                                    in0=et[:, sb * CH:(sb + 1) * CH],
                                    scalar1=E50P, scalar2=E50N, op0=ALU.min,
                                    op1=ALU.max), waits=ptw)
                        else:
                            pm = ph.v(lambda v, sb=sb: v.tensor_scalar(
                                out=pt[:, sb * CH:(sb + 1) * CH],
                                in0=et[:, sb * CH:(sb + 1) * CH],
                                scalar1=FBIG, scalar2=None, op0=ALU.min),
                                waits=ptw)
                        pt_marks[i4] = pm
                        cw = [(k.dve, pm), (kvsem, kv_n)]
                        if s == 0 and ks == 0 and h >= 2:
                            cw.append((k.dve, ctx_done[h - 2]))
                        st_ = (s == 0 and ks == 0)
                        sp_ = (s == NS - 1 and ks == 3)

                        def mmC(t, b=b, sb=sb, ks=ks, h=h, st_=st_, sp_=sp_):
                            t.matmul(psC[h % 2][0:hd, :],
                                     vsl[:, (b * 4 + ks) * hd:
                                         (b * 4 + ks + 1) * hd],
                                     pt[:, sb * CH:(sb + 1) * CH],
                                     start=st_, stop=sp_)
                            return t.matmul(
                                psR[h % 2][0:1, :], k.onec_b[:],
                                pt[:, sb * CH:(sb + 1) * CH],
                                start=st_, stop=sp_)
                        dm = ph.t(mmC, waits=cw, inc=1)
                        den_marks_fine[i4] = dm
                    den_marks[it] = den_marks_fine[it * 4 + 3]
                dpe = den_marks[h * NS + NS - 1]
                rw = [(k.pe, dpe)]
                if h >= 1:
                    rw.append((k.pe, prev_bm))
                ph.v(lambda v, h=h: v.reciprocal(
                    out=rrow[:], in_=psR[h % 2][0:1, :]), waits=rw)
                rv = k.dve.n
                bw = [(k.dve, rv)]
                if h >= 1:
                    bw.append((k.act, prev_am))
                bm = ph.t(lambda t: t.matmul(
                    pb[0:hd, :], k.oner_f[0:1, 0:hd], rrow[:],
                    start=True, stop=True), waits=bw, inc=1)
                prev_bm = bm
                aw = [(k.pe, bm)]
                if h >= 1:
                    aw.append((k.dve, ctx_done[h - 1]))
                am = ph.s(lambda s: s.activation(
                    out=bcr[0:hd, :], in_=pb[0:hd, :], func=AF.Copy),
                    waits=aw)
                prev_am = am
                if hd == 128:
                    cdst = ctx_sb[:, h * CH:(h + 1) * CH]
                    csrc = psC[h % 2][:]
                    bsrc = bcr[:]
                else:
                    cdst = ctx_sb[(h % 2) * 64:(h % 2) * 64 + 64,
                                  (h // 2) * CH:(h // 2 + 1) * CH]
                    csrc = psC[h % 2][0:64, :]
                    bsrc = bcr[0:64, :]
                ctx_done[h] = ph.v(lambda v, cdst=cdst, csrc=csrc,
                                   bsrc=bsrc: v.tensor_tensor(
                    out=cdst, in0=csrc, in1=bsrc, op=ALU.mult),
                    waits=[(k.act, am)])
            ctx_all = k.dve.n

            osem = k.usem(f"ao{tag}{w}")
            op_marks = {}
            x_marks = {}
            for mc in range(DCN):
                b = mc % 2
                wwaits = [] if mc < 2 else [(k.pe, op_marks[mc - 2])]
                ph.dma("gpsimd", [(wos[:, b * D:(b + 1) * D],
                                   wo_pack_in[mc, :, :])], osem,
                       waits=wwaits)
                ld_n = osem.n
                evw = [(k.dve, ctx_all)] if mc < 2 else \
                    [(k.dve, x_marks[mc - 2])]

                def mmO(t, b=b):
                    last = None
                    for kc in range(DCN):
                        last = t.matmul(
                            psS[b][:], wos[:, b * D + kc * 128:
                                           b * D + (kc + 1) * 128],
                            ctx_sb[:, kc * CH:(kc + 1) * CH],
                            start=(kc == 0), stop=(kc == DCN - 1))
                    return last
                op_marks[mc] = ph.t(mmO, waits=[(osem, ld_n)] + evw, inc=1)
                if bout_cols is None:
                    x_marks[mc] = ph.v(lambda v, mc=mc, b=b: v.tensor_tensor(
                        out=xT[:, col(mc, w)], in0=psS[b][:],
                        in1=xT[:, col(mc, w)], op=ALU.add),
                        waits=[(k.pe, op_marks[mc])])
                else:
                    x_marks[mc] = ph.v(
                        lambda v, mc=mc, b=b: v.scalar_tensor_tensor(
                            out=xT[:, col(mc, w)], in0=psS[b][:],
                            scalar=bout_cols[:, mc:mc + 1],
                            in1=xT[:, col(mc, w)], op0=ALU.add, op1=ALU.add),
                        waits=[(k.pe, op_marks[mc])])
            ph.flush()
    return k.dve.n



def build_p1():
    k = K()
    xin = k.din("xT_in", [D, NTOK], F32)
    wgp = k.din("wg0p", [HC0, 128, D], I32)
    wup = k.din("wu0p", [HC0, 128, D], I32)
    wdp = k.din("wd0p", [DCN, 128, HID0], I32)
    sg = k.din("s_gate", [HID0], F32)
    su = k.din("s_up", [HID0], F32)
    sd = k.din("s_down", [D], F32)
    wqp = k.din("wq0p", [DCN, 128, D], F32)
    wkp = k.din("wk0p", [4, 128, D], F32)
    wv = k.din("wv0", [D, CH], F32)
    x1out = k.dout("x1T_out", [D, NTOK], F32)
    qout = k.dout("q0T_out", [D, NTOK], BF16)
    kout = k.dout("k0T_out", [CH, NTOK], BF16)
    vout = k.dout("v0_out", [NTOK, CH], BF16)

    k.start(consts=(RT128,))
    xT = k.sb("xT", [128, DCN * NTOK], F32)
    yT = k.sb("yT", [128, DCN * NTOK], BF16)
    sgc = k.sb("sgc", [128, HC0], F32)
    suc = k.sb("suc", [128, HC0], F32)
    sdc = k.sb("sdc", [128, DCN], F32)
    csum = k.sb("csum", [1, 2 * CH], BF16)

    lx = emit_load_xT(k, xin, xT)
    emit_vec_cols(k, sg, HID0, sgc)
    emit_vec_cols(k, su, HID0, suc)
    emit_vec_cols(k, sd, D, sdc, scale=0.5)
    for w in range(NW):
        yw = yT
        emit_ln(k, xT, w, yw, lambda dc: wcol(dc), True,
                csum[0:1, w * CH:(w + 1) * CH], ld_sem=lx)
        emit_ffn0(k, xT, yw, csum[0:1, w * CH:(w + 1) * CH], wgp, wup, wdp,
                  sgc, suc, sdc, w, f"w{w}")
    emit_store_fm(k, xT, x1out, D, waits=k.gate())
    for w in range(NW):
        emit_ln(k, xT, w, yT, lambda dc, w=w: col(dc, w), False, None)
    emit_qkproj(k, yT, wqp, DCN, qout, k.constcol(RT128), "q0")
    emit_qkproj(k, yT, wkp, 4, kout, None, "k0")
    emit_vproj(k, yT, wv, vout, "v0")
    return k.finish()


def build_p2():
    k = K()
    xin = k.din("x1T_in", [D, NTOK], F32)
    qin = k.din("q0T_in", [D, NTOK], BF16)
    kw = [k.din(f"K0w{w}", [CH, NCH * CH], BF16) for w in range(NW)]
    vw = [k.din(f"V0w{w}", [T, CH], BF16) for w in range(NW)]
    flags = k.din("flags0", [128, 32], F32)
    md = k.din("md0", [128, 4 * CH], F32)
    wop = k.din("wo0p", [DCN, 128, D], F32)
    wgp = k.din("wg0p", [HC0, 128, D], I32)
    wup = k.din("wu0p", [HC0, 128, D], I32)
    wdp = k.din("wd0p", [DCN, 128, HID0], I32)
    sg = k.din("s_gate", [HID0], F32)
    su = k.din("s_up", [HID0], F32)
    sd = k.din("s_down", [D], F32)
    x3out = k.dout("x3T_out", [D, NTOK], F32)

    k.start()
    xT = k.sb("xT", [128, DCN * NTOK], F32)
    flags_sb = k.sb("flags_sb", [128, 32], F32)
    md_sb = k.sb("md_sb", [128, 4 * CH], F32)
    sgc = k.sb("sgc", [128, HC0], F32)
    suc = k.sb("suc", [128, HC0], F32)
    sdc = k.sb("sdc", [128, DCN], F32)

    lx = emit_load_xT(k, xin, xT)
    lm = emit_load_misc(k, [(flags_sb[:], flags), (md_sb[:], md)])
    emit_vec_cols(k, sg, HID0, sgc)
    emit_vec_cols(k, su, HID0, suc)
    emit_vec_cols(k, sd, D, sdc, scale=0.5)
    emit_attn(k, xT, qin, kw, vw, flags_sb, md_sb, wop, 128, 16,
              bout_cols=None, clamp=True, tag="a0")
    with k.nc.sbuf_tensor("yTw", [128, DCN * CH], BF16) as yw, \
            k.nc.sbuf_tensor("csum2", [1, 2 * CH], BF16) as csum:
        for w in range(NW):
            emit_ln(k, xT, w, yw, lambda dc: wcol(dc), True,
                    csum[0:1, w * CH:(w + 1) * CH])
            emit_ffn0(k, xT, yw, csum[0:1, w * CH:(w + 1) * CH], wgp, wup,
                      wdp, sgc, suc, sdc, w, f"b{w}")
    emit_store_fm(k, xT, x3out, D, waits=k.gate())
    return k.finish()


def build_p3():
    k = K()
    xin = k.din("x3T_in", [D, NTOK], F32)
    wgp = k.din("wg1p", [HC1, 128, D], F32)
    wup = k.din("wu1p", [HC1, 128, D], F32)
    wdp = k.din("wd1p", [DCN, 128, HID1], F32)
    wqp = k.din("wq1p", [DCN, 128, D], F32)
    wqsp = k.din("wq1sp", [DCN, 128, D], F32)
    wkp = k.din("wk1p", [4, 128, D], F32)
    wksp = k.din("wk1sp", [4, 128, D], F32)
    wv = k.din("wv1", [D, CH], F32)
    ctab_in = k.din("ropeC", [128, NTOK], F32)
    stab_in = k.din("ropeS", [128, NTOK], F32)
    x4out = k.dout("x4T_out", [D, NTOK], F32)
    qout = k.dout("q1T_out", [D, NTOK], BF16)
    kout = k.dout("k1T_out", [CH, NTOK], BF16)
    vout = k.dout("v1_out", [NTOK, CH], BF16)

    k.start(consts=(0.125,))
    xT = k.sb("xT", [128, DCN * NTOK], F32)
    ctab = k.sb("ctab", [128, NTOK], F32)
    stab = k.sb("stab", [128, NTOK], F32)

    lx = emit_load_xT(k, xin, xT)
    emit_load_misc(k, [(ctab[:], ctab_in), (stab[:], stab_in)])
    with k.nc.sbuf_tensor("yTw", [128, DCN * CH], BF16) as yw:
        for w in range(NW):
            emit_ln(k, xT, w, yw, lambda dc: wcol(dc), False, None,
                    ld_sem=lx)
            emit_ffn1(k, xT, yw, wgp, wup, wdp, w, f"w{w}")
    emit_store_fm(k, xT, x4out, D, waits=k.gate())
    yT = k.sb("yTf", [128, DCN * NTOK], BF16)
    for w in range(NW):
        emit_ln(k, xT, w, yT, lambda dc, w=w: col(dc, w), False, None)
    emit_qkproj(k, yT, wqp, DCN, qout, k.constcol(0.125), "q1", wpack2_in=wqsp,
                ctab=ctab, stab=stab)
    emit_qkproj(k, yT, wkp, 4, kout, None, "k1", wpack2_in=wksp,
                ctab=ctab, stab=stab)
    emit_vproj(k, yT, wv, vout, "v1")
    return k.finish()


def build_p4():
    k = K()
    xin = k.din("x4T_in", [D, NTOK], F32)
    qin = k.din("q1T_in", [D, NTOK], BF16)
    kw = [k.din(f"K1w{w}", [64, 8 * NCH * CH], BF16) for w in range(NW)]
    vw = [k.din(f"V1w{w}", [T, CH], BF16) for w in range(NW)]
    flags = k.din("flags1", [128, 32], F32)
    md = k.din("md1", [128, 4 * CH], F32)
    wop = k.din("wo1p", [DCN, 128, D], F32)
    bout = k.din("b_out", [D], F32)
    wgp = k.din("wg1p", [HC1, 128, D], F32)
    wup = k.din("wu1p", [HC1, 128, D], F32)
    wdp = k.din("wd1p", [DCN, 128, HID1], F32)
    x6out = k.dout("x6T_out", [D, NTOK], F32)

    k.start()
    xT = k.sb("xT", [128, DCN * NTOK], F32)
    flags_sb = k.sb("flags_sb", [128, 32], F32)
    md_sb = k.sb("md_sb", [128, 4 * CH], F32)
    bout_cols = k.sb("bout_cols", [128, DCN], F32)

    lx = emit_load_xT(k, xin, xT)
    lm = emit_load_misc(k, [(flags_sb[:], flags), (md_sb[:], md)])
    emit_vec_cols(k, bout, D, bout_cols)
    emit_attn(k, xT, qin, kw, vw, flags_sb, md_sb, wop, 64, 32,
              bout_cols=bout_cols, clamp=False, tag="a1")
    with k.nc.sbuf_tensor("yTw", [128, DCN * CH], BF16) as yw:
        for w in range(NW):
            emit_ln(k, xT, w, yw, lambda dc: wcol(dc), False, None)
            emit_ffn1(k, xT, yw, wgp, wup, wdp, w, f"b{w}")
    emit_store_fm(k, xT, x6out, D, waits=k.gate())
    return k.finish()


def build_fused():
    k = K()
    xin = k.din("xT_in", [D, NTOK], F32)
    wg0p = k.din("wg0p", [HC0, 128, D], I32)
    wu0p = k.din("wu0p", [HC0, 128, D], I32)
    wd0p = k.din("wd0p", [DCN, 128, HID0], I32)
    sg = k.din("s_gate", [HID0], F32)
    su = k.din("s_up", [HID0], F32)
    sd = k.din("s_down", [D], F32)
    wq0p = k.din("wq0p", [DCN, 128, D], F32)
    wk0p = k.din("wk0p", [4, 128, D], F32)
    wv0 = k.din("wv0", [D, CH], F32)
    wo0p = k.din("wo0p", [DCN, 128, D], F32)
    flags0 = k.din("flags0", [128, 34], F32)
    flags1 = k.din("flags1", [128, 34], F32)
    md = k.din("md", [128, 4 * CH], F32)
    wg1p = k.din("wg1p", [HC1, 128, D], F32)
    wu1p = k.din("wu1p", [HC1, 128, D], F32)
    wd1p = k.din("wd1p", [DCN, 128, HID1], F32)
    wq1p = k.din("wq1p", [DCN, 128, D], F32)
    wq1sp = k.din("wq1sp", [DCN, 128, D], F32)
    wk1p = k.din("wk1p", [4, 128, D], F32)
    wk1sp = k.din("wk1sp", [4, 128, D], F32)
    wv1 = k.din("wv1", [D, CH], F32)
    wo1p = k.din("wo1p", [DCN, 128, D], F32)
    bout = k.din("b_out", [D], F32)
    ctab_in = k.din("ropeC", [128, NTOK], F32)
    stab_in = k.din("ropeS", [128, NTOK], F32)
    outx = k.dout("outT", [D, NTOK], F32)

    nc = k.nc
    q0d_h = nc.dram_tensor("q0d", [D, NTOK], BF16)
    k0sh_h = nc.dram_tensor("k0sh", [CH, NTOK], BF16)
    v0sh_h = nc.dram_tensor("v0sh", [NTOK, CH], BF16)
    k0g_h = nc.dram_tensor("k0g", [NC * CH, NTOK], BF16)
    v0g_h = nc.dram_tensor("v0g", [T, CH], BF16)
    q1d_h = nc.dram_tensor("q1d", [D, NTOK], BF16)
    k1sh_h = nc.dram_tensor("k1sh", [CH, NTOK], BF16)
    v1sh_h = nc.dram_tensor("v1sh", [NTOK, CH], BF16)
    k1g_h = nc.dram_tensor("k1g", [NC * CH, NTOK], BF16)
    v1g_h = nc.dram_tensor("v1g", [T, CH], BF16)

    k.start(consts=(RT128, 0.125))
    xT = k.sb("xT", [128, DCN * NTOK], F32)
    flags0_sb = k.sb("flags0_sb", [128, 34], F32)
    flags1_sb = k.sb("flags1_sb", [128, 34], F32)
    md_sb = k.sb("md_sb", [128, 4 * CH], F32)
    ctab = k.sb("ctab", [128, NTOK], F32)
    stab = k.sb("stab", [128, NTOK], F32)
    bout_cols = k.sb("bout_cols", [128, DCN], F32)
    sgc = k.sb("sgc", [128, HC0], F32)
    suc = k.sb("suc", [128, HC0], F32)
    sdc = k.sb("sdc", [128, DCN], F32)

    lx = emit_load_xT(k, xin, xT)
    emit_load_misc(k, [(flags0_sb[:], flags0), (flags1_sb[:], flags1),
                       (md_sb[:], md), (ctab[:], ctab_in),
                       (stab[:], stab_in)])
    emit_vec_cols(k, sg, HID0, sgc)
    emit_vec_cols(k, su, HID0, suc)
    emit_vec_cols(k, sd, D, sdc, scale=0.5)
    emit_vec_cols(k, bout, D, bout_cols)

    with nc.sbuf_tensor("yw1", [128, DCN * CH], BF16) as yw, \
            nc.sbuf_tensor("cs1", [1, 2 * CH], BF16) as csum:
        for w in range(NW):
            emit_ln(k, xT, w, yw, lambda dc: wcol(dc), True,
                    csum[0:1, w * CH:(w + 1) * CH], ld_sem=lx)
            emit_ffn0(k, xT, yw, csum[0:1, w * CH:(w + 1) * CH], wg0p, wu0p,
                      wd0p, sgc, suc, sdc, w, f"fa{w}")
    with nc.sbuf_tensor("yf1", [128, DCN * NTOK], BF16) as yT:
        for w in range(NW):
            emit_ln(k, xT, w, yT, lambda dc, w=w: col(dc, w), False, None)
        emit_qkproj(k, yT, wq0p, DCN, q0d_h.ap(), k.constcol(RT128), "q0")
        emit_qkproj(k, yT, wk0p, 4, k0sh_h.ap(), None, "k0")
        emit_vproj(k, yT, wv0, v0sh_h.ap(), "v0")
    emit_allgather(k, [(k0sh_h, k0g_h), (v0sh_h, v0g_h)])
    emit_attn_f(k, xT, q0d_h.ap(), k0g_h.ap(), v0g_h.ap(), k0sh_h.ap(),
                v0sh_h.ap(), flags0_sb, md_sb, wo0p, 128, 16,
                bout_cols=None, clamp=True, tag="a0")
    with nc.sbuf_tensor("yw3", [128, DCN * CH], BF16) as yw, \
            nc.sbuf_tensor("cs3", [1, 2 * CH], BF16) as csum:
        for w in range(NW):
            emit_ln(k, xT, w, yw, lambda dc: wcol(dc), True,
                    csum[0:1, w * CH:(w + 1) * CH])
            emit_ffn0(k, xT, yw, csum[0:1, w * CH:(w + 1) * CH], wg0p, wu0p,
                      wd0p, sgc, suc, sdc, w, f"fb{w}")
    with nc.sbuf_tensor("yw4", [128, DCN * CH], BF16) as yw:
        for w in range(NW):
            emit_ln(k, xT, w, yw, lambda dc: wcol(dc), False, None)
            emit_ffn1(k, xT, yw, wg1p, wu1p, wd1p, w, f"fc{w}")
    with nc.sbuf_tensor("yf4", [128, DCN * NTOK], BF16) as yT:
        for w in range(NW):
            emit_ln(k, xT, w, yT, lambda dc, w=w: col(dc, w), False, None)
        emit_qkproj(k, yT, wq1p, DCN, q1d_h.ap(), k.constcol(0.125), "q1",
                    wpack2_in=wq1sp, ctab=ctab, stab=stab)
        emit_qkproj(k, yT, wk1p, 4, k1sh_h.ap(), None, "k1",
                    wpack2_in=wk1sp, ctab=ctab, stab=stab)
        emit_vproj(k, yT, wv1, v1sh_h.ap(), "v1")
    emit_allgather(k, [(k1sh_h, k1g_h), (v1sh_h, v1g_h)])
    emit_attn_f(k, xT, q1d_h.ap(), k1g_h.ap(), v1g_h.ap(), k1sh_h.ap(),
                v1sh_h.ap(), flags1_sb, md_sb, wo1p, 64, 32,
                bout_cols=bout_cols, clamp=False, tag="a1")
    with nc.sbuf_tensor("yw6", [128, DCN * CH], BF16) as yw:
        for w in range(NW):
            emit_ln(k, xT, w, yw, lambda dc: wcol(dc), False, None)
            emit_ffn1(k, xT, yw, wg1p, wu1p, wd1p, w, f"fd{w}")
    emit_store_fm(k, xT, outx, D, waits=k.gate())
    return k.finish()


def _flags_f(core):
    f = np.zeros((128, 34), np.float32)
    for w in range(NW):
        jd = 2 * core + w
        for s in range(16):
            if s >= jd:
                f[:, w * 17 + s] = NEG1E9
    return f



def _pack_lhsT(W):
    Kd, N = W.shape
    a = W.reshape(Kd // 128, 128, N // 128, 128).transpose(2, 1, 0, 3)
    return np.ascontiguousarray(a.reshape(N // 128, 128, Kd))


def _md_mask():
    m = np.zeros((128, 4 * CH), np.float32)
    for ks in range(4):
        i = np.arange(128)[:, None] + 128 * ks
        j = np.arange(CH)[None, :]
        m[:, ks * CH:(ks + 1) * CH] = np.where(i > j, NEG1E9, 0.0)
    return m


def _flags(core):
    f = np.zeros((128, 32), np.float32)
    for w in range(NW):
        jd = 2 * core + w
        for s in range(16):
            if s < jd or s == 15:
                f[:, w * 16 + s] = 0.0
            else:
                f[:, w * 16 + s] = NEG1E9
    return f


def _slot_order(jd):
    return [c for c in range(NCH) if c != jd] + [jd]


def _rope_tabs(core):
    pos = np.arange(core * NTOK, (core + 1) * NTOK, dtype=np.float64)
    inv = 1.0 / (10000.0 ** (np.arange(32, dtype=np.float64) / 32.0))
    fr = np.outer(inv, pos)
    cos = np.cos(fr).astype(np.float32)
    sin = np.sin(fr).astype(np.float32)
    C = np.tile(cos, (4, 1))
    S = np.concatenate([-sin, sin, -sin, sin], axis=0)
    return np.ascontiguousarray(C), np.ascontiguousarray(S)


def _swap_heads(W, head_w):
    N = W.shape[1]
    idx = np.arange(N)
    h = idx // head_w
    r = idx % head_w
    half = head_w // 2
    src = h * head_w + np.where(r < half, r + half, r - half)
    return W[:, src]


def _ffn1_perm():
    n = np.arange(HID1)
    c = n // 512
    r = (n % 512) // 128
    p = n % 128
    return 512 * c + 4 * p + r


_CACHE = {}


def _progs():
    if "p" not in _CACHE:
        _CACHE["p"] = (build_p1(), build_p2(), build_p3(), build_p4())
    return _CACHE["p"]


def kernel(**inputs):
    inputs = {kk: np.asarray(vv) for kk, vv in inputs.items()}
    x = np.ascontiguousarray(inputs["L0_attn_hidden_states"][0])
    xT = np.ascontiguousarray(x.T)
    if "f" not in _CACHE:
        _CACHE["f"] = build_fused()
    ncf = _CACHE["f"]

    wg0p = _pack_lhsT(np.ascontiguousarray(inputs["L0_ffn_wq_gate"].T))
    wu0p = _pack_lhsT(np.ascontiguousarray(inputs["L0_ffn_wq_up"].T))
    wd0p = _pack_lhsT(np.ascontiguousarray(inputs["L0_ffn_wq_down"].T))
    wq0p = _pack_lhsT(inputs["L0_attn_Wq"])
    wk0p = _pack_lhsT(inputs["L0_attn_Wk"])
    wo0p = _pack_lhsT(inputs["L0_attn_Wo"])
    perm = _ffn1_perm()
    wg1p = _pack_lhsT(np.ascontiguousarray(inputs["L1_ffn_w_gate"][:, perm]))
    wu1p = _pack_lhsT(np.ascontiguousarray(inputs["L1_ffn_w_up"][:, perm]))
    wd1p = _pack_lhsT(np.ascontiguousarray(inputs["L1_ffn_w_down"][perm, :]))
    wq1 = inputs["L1_attn_Wq"]
    wk1 = inputs["L1_attn_Wk"]
    wq1p = _pack_lhsT(wq1)
    wq1sp = _pack_lhsT(np.ascontiguousarray(_swap_heads(wq1, 64)))
    wk1p = _pack_lhsT(wk1)
    wk1sp = _pack_lhsT(np.ascontiguousarray(_swap_heads(wk1, 64)))
    wo1p = _pack_lhsT(inputs["L1_attn_Wo"])
    md = _md_mask()

    maps = []
    for c in range(NC):
        Ct, St = _rope_tabs(c)
        fl = _flags_f(c)
        maps.append({
            "xT_in": np.ascontiguousarray(xT[:, c * NTOK:(c + 1) * NTOK]),
            "wg0p": wg0p, "wu0p": wu0p, "wd0p": wd0p,
            "s_gate": inputs["L0_ffn_s_gate"],
            "s_up": inputs["L0_ffn_s_up"],
            "s_down": inputs["L0_ffn_s_down"],
            "wq0p": wq0p, "wk0p": wk0p, "wv0": inputs["L0_attn_Wv"],
            "wo0p": wo0p, "flags0": fl, "flags1": fl, "md": md,
            "wg1p": wg1p, "wu1p": wu1p, "wd1p": wd1p,
            "wq1p": wq1p, "wq1sp": wq1sp, "wk1p": wk1p, "wk1sp": wk1sp,
            "wv1": inputs["L1_attn_Wv"], "wo1p": wo1p,
            "b_out": inputs["L1_attn_b_out"],
            "ropeC": Ct, "ropeS": St,
        })
    res = run_bass_kernel_spmd(ncf, maps, core_ids=list(range(NC))).results
    outT = np.concatenate([np.asarray(res[c]["outT"], np.float32)
                           for c in range(NC)], axis=1)
    return np.ascontiguousarray(outT.T)[None, :, :].astype(np.float32)


def kernel_multi(**inputs):
    inputs = {kk: np.asarray(vv) for kk, vv in inputs.items()}
    x = np.ascontiguousarray(inputs["L0_attn_hidden_states"][0])
    xT = np.ascontiguousarray(x.T)
    nc1, nc2, nc3, nc4 = _progs()

    wg0p = _pack_lhsT(np.ascontiguousarray(inputs["L0_ffn_wq_gate"].T))
    wu0p = _pack_lhsT(np.ascontiguousarray(inputs["L0_ffn_wq_up"].T))
    wd0p = _pack_lhsT(np.ascontiguousarray(inputs["L0_ffn_wq_down"].T))
    wq0p = _pack_lhsT(inputs["L0_attn_Wq"])
    wk0p = _pack_lhsT(inputs["L0_attn_Wk"])
    wo0p = _pack_lhsT(inputs["L0_attn_Wo"])
    perm = _ffn1_perm()
    wg1p = _pack_lhsT(np.ascontiguousarray(inputs["L1_ffn_w_gate"][:, perm]))
    wu1p = _pack_lhsT(np.ascontiguousarray(inputs["L1_ffn_w_up"][:, perm]))
    wd1p = _pack_lhsT(np.ascontiguousarray(inputs["L1_ffn_w_down"][perm, :]))
    wq1 = inputs["L1_attn_Wq"]
    wk1 = inputs["L1_attn_Wk"]
    wq1p = _pack_lhsT(wq1)
    wq1sp = _pack_lhsT(np.ascontiguousarray(_swap_heads(wq1, 64)))
    wk1p = _pack_lhsT(wk1)
    wk1sp = _pack_lhsT(np.ascontiguousarray(_swap_heads(wk1, 64)))
    wo1p = _pack_lhsT(inputs["L1_attn_Wo"])
    md = _md_mask()

    maps = []
    for c in range(NC):
        maps.append({
            "xT_in": np.ascontiguousarray(xT[:, c * NTOK:(c + 1) * NTOK]),
            "wg0p": wg0p, "wu0p": wu0p, "wd0p": wd0p,
            "s_gate": inputs["L0_ffn_s_gate"],
            "s_up": inputs["L0_ffn_s_up"],
            "s_down": inputs["L0_ffn_s_down"],
            "wq0p": wq0p, "wk0p": wk0p, "wv0": inputs["L0_attn_Wv"],
        })
    r1 = run_bass_kernel_spmd(nc1, maps, core_ids=list(range(NC))).results
    k0 = np.concatenate([r1[c]["k0T_out"] for c in range(NC)], axis=1)
    v0 = np.concatenate([r1[c]["v0_out"] for c in range(NC)], axis=0)

    maps = []
    for c in range(NC):
        m = {
            "x1T_in": r1[c]["x1T_out"],
            "q0T_in": r1[c]["q0T_out"],
            "flags0": _flags(c), "md0": md, "wo0p": wo0p,
            "wg0p": wg0p, "wu0p": wu0p, "wd0p": wd0p,
            "s_gate": inputs["L0_ffn_s_gate"],
            "s_up": inputs["L0_ffn_s_up"],
            "s_down": inputs["L0_ffn_s_down"],
        }
        for w in range(NW):
            order = _slot_order(2 * c + w)
            m[f"K0w{w}"] = np.ascontiguousarray(np.concatenate(
                [k0[:, ch * CH:(ch + 1) * CH] for ch in order], axis=1))
            m[f"V0w{w}"] = np.ascontiguousarray(np.concatenate(
                [v0[ch * CH:(ch + 1) * CH] for ch in order], axis=0))
        maps.append(m)
    r2 = run_bass_kernel_spmd(nc2, maps, core_ids=list(range(NC))).results

    maps = []
    for c in range(NC):
        Ct, St = _rope_tabs(c)
        maps.append({
            "x3T_in": r2[c]["x3T_out"],
            "wg1p": wg1p, "wu1p": wu1p, "wd1p": wd1p,
            "wq1p": wq1p, "wq1sp": wq1sp, "wk1p": wk1p, "wk1sp": wk1sp,
            "wv1": inputs["L1_attn_Wv"],
            "ropeC": Ct, "ropeS": St,
        })
    r3 = run_bass_kernel_spmd(nc3, maps, core_ids=list(range(NC))).results
    k1 = np.concatenate([r3[c]["k1T_out"] for c in range(NC)], axis=1)
    v1 = np.concatenate([r3[c]["v1_out"] for c in range(NC)], axis=0)
    k1r = k1.reshape(8, 64, NCH, CH)

    maps = []
    for c in range(NC):
        m = {
            "x4T_in": r3[c]["x4T_out"],
            "q1T_in": r3[c]["q1T_out"],
            "flags1": _flags(c), "md1": md, "wo1p": wo1p,
            "b_out": inputs["L1_attn_b_out"],
            "wg1p": wg1p, "wu1p": wu1p, "wd1p": wd1p,
        }
        for w in range(NW):
            order = _slot_order(2 * c + w)
            kk = k1r[:, :, order, :]
            kk = kk.transpose(1, 0, 2, 3).reshape(64, 8 * NCH * CH)
            m[f"K1w{w}"] = np.ascontiguousarray(kk)
            m[f"V1w{w}"] = np.ascontiguousarray(np.concatenate(
                [v1[ch * CH:(ch + 1) * CH] for ch in order], axis=0))
        maps.append(m)
    r4 = run_bass_kernel_spmd(nc4, maps, core_ids=list(range(NC))).results

    outT = np.concatenate([np.asarray(r4[c]["x6T_out"], np.float32)
                           for c in range(NC)], axis=1)
    return np.ascontiguousarray(outT.T)[None, :, :].astype(np.float32)


# version 05 (final)
# speedup vs baseline: 1.5029x; 1.3595x over previous
import contextlib

import numpy as np
import ml_dtypes

import concourse.bass as bass
import concourse.mybir as mybir
from concourse.bass_utils import run_bass_kernel_spmd

F32 = mybir.dt.float32
BF16 = mybir.dt.bfloat16
I32 = mybir.dt.int32
AF = mybir.ActivationFunctionType
ALU = mybir.AluOpType
BF = ml_dtypes.bfloat16

NC = 8
T = 8192
D = 2048
NTOK = 1024
CH = 512
NW = NTOK // CH
DCN = D // 128
NCH = T // CH
HID0 = 5632
HC0 = HID0 // 128
HID1 = 8192
HC1 = HID1 // 128
E50P = float(np.exp(np.float64(50.0)))
E50N = float(np.exp(np.float64(-50.0)))
RT128 = float(np.sqrt(np.float64(128.0)))
FBIG = 3.0e38
NEG1E9 = -1.0e9
EPS = 1e-5


def col(dc, w):
    return slice(dc * NTOK + w * CH, dc * NTOK + w * CH + CH)


def wcol(dc):
    return slice(dc * CH, dc * CH + CH)


class Sem:
    def __init__(self, h):
        self.h = h
        self.n = 0

    def inc(self, amt=1):
        self.n += amt
        return self.n


class K:

    def __init__(self):
        self.nc = bass.Bass("TRN2", target_bir_lowering=False)
        self.es = contextlib.ExitStack()
        self._sems = {}
        self.block = None
        self.store_sems = []
        self._useq = 0

    def sem(self, name) -> Sem:
        if name not in self._sems:
            self._sems[name] = Sem(self.es.enter_context(
                self.nc.semaphore(name)))
        return self._sems[name]

    def usem(self, prefix) -> Sem:
        self._useq += 1
        return self.sem(f"{prefix}{self._useq}")

    def sb(self, name, shape, dtype):
        return self.es.enter_context(self.nc.sbuf_tensor(name, shape, dtype))

    def din(self, name, shape, dtype):
        return self.nc.dram_tensor(name, shape, dtype,
                                   kind="ExternalInput").ap()

    def dout(self, name, shape, dtype):
        return self.nc.dram_tensor(name, shape, dtype,
                                   kind="ExternalOutput").ap()

    def start(self, consts=()):
        self.block = self.es.enter_context(self.nc.Block())
        self.pe = self.sem("pe")
        self.act = self.sem("act")
        self.dve = self.sem("dve")
        self.onec_f = self.sb("onec_f", [128, 1], F32)
        self.onec_b = self.sb("onec_b", [128, 1], BF16)
        self.oner_f = self.sb("oner_f", [1, 128], F32)
        self.neg128_b = self.sb("neg128_b", [1, 128], BF16)
        consts = (EPS,) + tuple(consts)
        self._cct = self.sb("ccols", [128, max(1, len(consts))], F32)
        self._cc = {}
        ph = Ph(self)
        ph.v(lambda v: v.memset(self.onec_f[:], 1.0))
        ph.v(lambda v: v.memset(self.onec_b[:], 1.0))
        ph.v(lambda v: v.memset(self.oner_f[:], 1.0))
        ph.v(lambda v: v.memset(self.neg128_b[:], -128.0))
        for i, cv in enumerate(consts):
            ap = self._cct[:, i:i + 1]
            ph.v(lambda v, ap=ap, cv=cv: v.memset(ap, float(cv)))
            self._cc[float(cv)] = ap
        ph.flush()

    def constcol(self, v):
        return self._cc[float(v)]

    def gate(self):
        return [(s, s.n) for s in self._sems.values() if s.n > 0]

    def finish(self):
        blk = self.block
        stores = list(self.store_sems)

        def fin(sy):
            for s in stores:
                sy.wait_ge(s.h, s.n)
        blk.sync(fin)
        self.es.close()
        return self.nc


class Ph:

    ENGS = ("sync", "gpsimd", "scalar", "vector", "tensor")

    def __init__(self, k: K, gate=None):
        self.k = k
        self.q = {e: [] for e in Ph.ENGS}
        self._gated = set()
        self.gate = list(gate) if gate else []

    def _add(self, eng, fn, waits, incs):
        waits = list(waits)
        if eng not in self._gated:
            self._gated.add(eng)
            waits = self.gate + waits
        self.q[eng].append((waits, fn, incs))

    def t(self, fn, waits=(), inc=0):
        incs = []
        if inc:
            self.k.pe.inc(inc)
            incs = [(self.k.pe, inc)]
        self._add("tensor", fn, waits, incs)
        return self.k.pe.n

    def s(self, fn, waits=()):
        a = self.k.act
        w = list(waits) + [(a, a.n)]
        a.inc(1)
        self._add("scalar", fn, w, [(a, 1)])
        return a.n

    def v(self, fn, waits=()):
        d = self.k.dve
        w = list(waits) + [(d, d.n)]
        d.inc(1)
        self._add("vector", fn, w, [(d, 1)])
        return d.n

    def dma(self, queue, pairs, sem: Sem, waits=()):
        h = sem.h

        def fn(e, pairs=tuple(pairs)):
            last = None
            for o, i in pairs:
                last = e.dma_start(out=o, in_=i).then_inc(h, 16)
            return last
        sem.inc(16 * len(pairs))
        self._add(queue, fn, waits, [])
        return sem.n

    def flush(self):
        blk = self.k.block
        for eng in Ph.ENGS:
            entries = self.q[eng]
            if not entries:
                continue

            def section(e, entries=entries):
                for waits, fn, incs in entries:
                    for s, val in waits:
                        if val > 0:
                            e.wait_ge(s.h, val)
                    inst = fn(e)
                    for s, amt in incs:
                        inst.then_inc(s.h, amt)
            getattr(blk, eng)(section)
        self.q = {e: [] for e in Ph.ENGS}
        self._gated = set()



def emit_load_xT(k, x_in, xT):
    ph = Ph(k, gate=k.gate())
    sem = k.usem("lx")
    ph.dma("sync", [(xT[:, dc * NTOK:(dc + 1) * NTOK],
                     x_in[dc * 128:(dc + 1) * 128, :]) for dc in range(DCN)],
           sem)
    ph.flush()
    return sem


def emit_store_fm(k, src, out_hbm, nrows, waits):
    ph = Ph(k, gate=k.gate())
    sem = k.usem("st")
    k.store_sems.append(sem)
    ph.dma("sync", [(out_hbm[r * 128:(r + 1) * 128, :],
                     src[:, r * NTOK:(r + 1) * NTOK])
                    for r in range(nrows // 128)], sem, waits=waits)
    ph.flush()
    return sem


def emit_vec_cols(k, src, n, dst, scale=None):
    ph = Ph(k, gate=k.gate())
    sem = k.usem("vc")
    nch = n // 128
    ph.dma("sync", [(dst[:, c:c + 1],
                     src[c * 128:(c + 1) * 128].rearrange("(p o) -> p o", o=1))
                    for c in range(nch)], sem)
    if scale is not None:
        ph.v(lambda v: v.tensor_scalar(out=dst[:], in0=dst[:], scalar1=scale,
                                       scalar2=None, op0=ALU.mult),
             waits=[(sem, sem.n)])
    ph.flush()
    return sem


def emit_ln(k, xT, w, ytgt, ycol, want_csum, csum_bf, ld_sem=None):
    nc = k.nc
    ph = Ph(k, gate=k.gate())
    k._useq += 1
    u = str(k._useq)
    with (
        nc.psum_tensor("ln_pa" + u, [128, CH], F32) as pa,
        nc.psum_tensor("ln_pb" + u, [128, CH], F32) as pb,
        nc.psum_tensor("ln_pc1" + u, [128, CH], F32) as pc1,
        nc.psum_tensor("ln_pc2" + u, [128, CH], F32) as pc2,
        nc.sbuf_tensor("ln_sq" + u, [128, 2 * CH], F32) as sq,
        nc.sbuf_tensor("ln_rows" + u, [1, 4 * CH], F32) as rows,
        nc.sbuf_tensor("ln_bc" + u, [128, 2 * CH], F32) as bc,
        nc.sbuf_tensor("ln_scr" + u, [128, CH], F32) as scr,
    ):
        ldw = [(ld_sem, ld_sem.n)] if ld_sem is not None else []
        sqb_marks = {}
        for dc in range(DCN):
            sw = list(ldw)
            if dc >= 2:
                sw.append((k.pe, sqb_marks[dc - 2]))
            am = ph.s(lambda s, dc=dc: s.activation(
                out=sq[:, (dc % 2) * CH:(dc % 2 + 1) * CH],
                in_=xT[:, col(dc, w)], func=AF.Square), waits=sw)
            ph.t(lambda t, dc=dc: t.matmul(
                pa[0:1, :], k.onec_f[:], xT[:, col(dc, w)],
                start=(dc == 0), stop=(dc == DCN - 1)), waits=ldw)
            sqb_marks[dc] = ph.t(lambda t, dc=dc: t.matmul(
                pb[0:1, :], k.onec_f[:],
                sq[:, (dc % 2) * CH:(dc % 2 + 1) * CH],
                start=(dc == 0), stop=(dc == DCN - 1)),
                waits=[(k.act, am)], inc=1)
        sums_pe = k.pe.n
        rm = rows[0:1, 0:CH]
        rt = rows[0:1, CH:2 * CH]
        rv = rows[0:1, 2 * CH:3 * CH]
        rr = rows[0:1, 3 * CH:4 * CH]
        ph.v(lambda v: v.tensor_scalar(out=rm, in0=pa[0:1, :],
                                       scalar1=1.0 / D, scalar2=None,
                                       op0=ALU.mult),
             waits=[(k.pe, sums_pe)])
        ph.v(lambda v: v.tensor_scalar(out=rt, in0=pb[0:1, :],
                                       scalar1=1.0 / D, scalar2=None,
                                       op0=ALU.mult))
        ph.v(lambda v: v.tensor_tensor(out=rv, in0=rm, in1=rm, op=ALU.mult))
        ph.v(lambda v: v.tensor_tensor(out=rv, in0=rt, in1=rv,
                                       op=ALU.subtract))
        vd = k.dve.n
        am = ph.s(lambda s: s.activation(out=rr, in_=rv, func=AF.Sqrt,
                                         bias=k.constcol(EPS)[0:1, 0:1]),
                  waits=[(k.dve, vd)])
        ph.v(lambda v: v.reciprocal(out=rv, in_=rr), waits=[(k.act, am)])
        ph.v(lambda v: v.scalar_tensor_tensor(out=rr, in0=rm, scalar=-1.0,
                                              in1=rv, op0=ALU.mult,
                                              op1=ALU.mult))
        vd2 = k.dve.n
        ph.t(lambda t: t.matmul(pc1[:], k.oner_f[:], rr, start=True,
                                stop=True), waits=[(k.dve, vd2)])
        ph.t(lambda t: t.matmul(pc2[:], k.oner_f[:], rv, start=True,
                                stop=True), inc=1)
        bc_pe = k.pe.n
        a1 = ph.s(lambda s: s.activation(out=bc[:, 0:CH], in_=pc1[:],
                                         func=AF.Copy),
                  waits=[(k.pe, bc_pe)])
        a2 = ph.s(lambda s: s.activation(out=bc[:, CH:2 * CH], in_=pc2[:],
                                         func=AF.Copy))
        y_marks = []
        for dc in range(DCN):
            ph.v(lambda v, dc=dc: v.tensor_tensor(
                out=scr[:], in0=xT[:, col(dc, w)], in1=bc[:, 0:CH],
                op=ALU.add), waits=[(k.act, a2)])
            y_marks.append(ph.v(lambda v, dc=dc: v.tensor_tensor(
                out=ytgt[:, ycol(dc)], in0=scr[:], in1=bc[:, CH:2 * CH],
                op=ALU.mult)))
        if want_csum:
            for dc in range(DCN):
                ph.t(lambda t, dc=dc: t.matmul(
                    pa[0:1, :], k.onec_b[:], ytgt[:, ycol(dc)],
                    start=(dc == 0), stop=(dc == DCN - 1)),
                    waits=[(k.dve, y_marks[-1])],
                    inc=(1 if dc == DCN - 1 else 0))
            cs_pe = k.pe.n
            ph.v(lambda v: v.tensor_copy(out=csum_bf, in_=pa[0:1, :]),
                 waits=[(k.pe, cs_pe)])
        ph.flush()
    return k.dve.n


def emit_ffn0(k, xT, yT, csum_bf, wgp_in, wup_in, wdp_in, sg_cols, su_cols,
              halfsd_cols, w, tag):
    nc = k.nc
    ph = Ph(k, gate=k.gate())
    with (
        nc.psum_tensor(f"f0g{tag}", [128, CH], F32) as pg,
        nc.psum_tensor(f"f0u{tag}", [128, CH], F32) as pu,
        nc.psum_tensor(f"f0g2{tag}", [128, CH], F32) as pg2,
        nc.psum_tensor(f"f0u2{tag}", [128, CH], F32) as pu2,
        nc.sbuf_tensor(f"f0wg{tag}", [128, 2 * D], BF16) as wgs,
        nc.sbuf_tensor(f"f0wu{tag}", [128, 2 * D], BF16) as wus,
        nc.sbuf_tensor(f"f0h{tag}", [128, HC0 * CH], BF16) as h_sb,
        nc.sbuf_tensor(f"f0sg{tag}", [128, 2 * CH], F32) as gsc,
        nc.sbuf_tensor(f"f0su{tag}", [128, 2 * CH], F32) as usc,
    ):
        lsem = k.usem(f"f0l{tag}")
        psG = [pg, pg2]
        psU = [pu, pu2]
        mm_marks = {}
        h_marks = {}
        for hc in range(HC0):
            b = hc % 2
            wwaits = [] if hc < 2 else [(k.pe, mm_marks[hc - 2])]
            ph.dma("gpsimd",
                   [(wgs[:, b * D:(b + 1) * D],
                     wgp_in[hc, :, :]),
                    (wus[:, b * D:(b + 1) * D],
                     wup_in[hc, :, :])], lsem, waits=wwaits)
            ld_n = lsem.n
            evw = [] if hc < 2 else [(k.act, h_marks[hc - 2][1])]

            def mm(t, hc=hc, b=b):
                for dc in range(DCN):
                    t.matmul(psG[b][:], wgs[:, b * D + dc * 128:
                                            b * D + (dc + 1) * 128],
                             yT[:, wcol(dc)], start=(dc == 0), stop=False)
                    t.matmul(psU[b][:], wus[:, b * D + dc * 128:
                                            b * D + (dc + 1) * 128],
                             yT[:, wcol(dc)], start=(dc == 0), stop=False)
                t.matmul(psG[b][:], k.neg128_b[:], csum_bf,
                         start=False, stop=True)
                return t.matmul(psU[b][:], k.neg128_b[:], csum_bf,
                                start=False, stop=True)
            mm_marks[hc] = ph.t(mm, waits=[(lsem, ld_n)] + evw, inc=1)
            gw = [(k.pe, mm_marks[hc])]
            if hc >= 2:
                gw.append((k.dve, h_marks[hc - 2][0]))
            a1 = ph.s(lambda s, hc=hc, b=b: s.activation(
                out=gsc[:, b * CH:(b + 1) * CH], in_=psG[b][:], func=AF.Silu,
                scale=sg_cols[:, hc:hc + 1]), waits=gw)
            a2 = ph.s(lambda s, hc=hc, b=b: s.mul(
                out=usc[:, b * CH:(b + 1) * CH], in_=psU[b][:],
                mul=su_cols[:, hc:hc + 1]))
            hm = ph.v(lambda v, hc=hc, b=b: v.tensor_tensor(
                out=h_sb[:, hc * CH:(hc + 1) * CH],
                in0=gsc[:, b * CH:(b + 1) * CH],
                in1=usc[:, b * CH:(b + 1) * CH], op=ALU.mult),
                waits=[(k.act, a2)])
            h_marks[hc] = (hm, a2)
        h_done = k.dve.n

        for hc in range(HC0):
            ph.t(lambda t, hc=hc: t.matmul(
                pg[0:1, :], k.onec_b[:], h_sb[:, hc * CH:(hc + 1) * CH],
                start=(hc == 0), stop=(hc == HC0 - 1)),
                waits=[(k.dve, h_done)], inc=(1 if hc == HC0 - 1 else 0))
        hs_pe = k.pe.n
        hsr = gsc[0:1, 0:CH]
        ph.v(lambda v: v.tensor_scalar(out=hsr, in0=pg[0:1, :],
                                       scalar1=-128.0, scalar2=None,
                                       op0=ALU.mult), waits=[(k.pe, hs_pe)])
        hs_v = k.dve.n
        ph.t(lambda t: t.matmul(pg2[:], k.oner_f[:], hsr, start=True,
                                stop=True), waits=[(k.dve, hs_v)], inc=1)
        hb_pe = k.pe.n
        hbm = ph.s(lambda s: s.activation(out=usc[:, 0:CH], in_=pg2[:],
                                          func=AF.Copy),
                   waits=[(k.pe, hb_pe)])

        dsem = k.usem(f"f0d{tag}")
        with nc.sbuf_tensor(f"f0wd{tag}", [128, 2 * HID0], BF16) as wds:
            dn_marks = {}
            x_marks = {}
            for dc in range(DCN):
                b = dc % 2
                wwaits = [] if dc < 2 else [(k.pe, dn_marks[dc - 2])]
                ph.dma("gpsimd", [(wds[:, b * HID0:(b + 1) * HID0],
                                   wdp_in[dc, :, :])], dsem, waits=wwaits)
                ld_n = dsem.n
                if dc < 2:
                    evw = [(k.dve, hs_v), (k.act, hbm)]
                else:
                    evw = [(k.dve, x_marks[dc - 2])]

                def mmd(t, dc=dc, b=b):
                    last = None
                    for hc in range(HC0):
                        last = t.matmul(
                            psG[b][:], wds[:, b * HID0 + hc * 128:
                                           b * HID0 + (hc + 1) * 128],
                            h_sb[:, hc * CH:(hc + 1) * CH],
                            start=(hc == 0), stop=(hc == HC0 - 1))
                    return last
                dn_marks[dc] = ph.t(mmd, waits=[(dsem, ld_n)] + evw, inc=1)
                ph.v(lambda v, b=b: v.tensor_tensor(
                    out=gsc[:, CH:2 * CH], in0=psG[b][:], in1=usc[:, 0:CH],
                    op=ALU.add), waits=[(k.pe, dn_marks[dc]), (k.act, hbm)])
                x_marks[dc] = ph.v(lambda v, dc=dc: v.scalar_tensor_tensor(
                    out=xT[:, col(dc, w)], in0=gsc[:, CH:2 * CH],
                    scalar=halfsd_cols[:, dc:dc + 1], in1=xT[:, col(dc, w)],
                    op0=ALU.mult, op1=ALU.add))
            ph.flush()
    return k.dve.n


def emit_ffn1(k, xT, yT, wgp_in, wup_in, wdp_in, w, tag):
    nc = k.nc
    ph = Ph(k, gate=k.gate())
    gsc_es = contextlib.ExitStack()
    EG = gsc_es.enter_context
    with (
        nc.psum_tensor(f"f1a{tag}", [128, CH], F32) as p0,
        nc.psum_tensor(f"f1b{tag}", [128, CH], F32) as p1,
        nc.psum_tensor(f"f1c{tag}", [128, CH], F32) as p2,
        nc.psum_tensor(f"f1d{tag}", [128, CH], F32) as p3,
        nc.sbuf_tensor(f"f1h{tag}", [128, HC1 * CH], BF16) as h_sb,
    ):
        wgs = EG(nc.sbuf_tensor(f"f1wg{tag}", [128, 2 * D], BF16))
        wus = EG(nc.sbuf_tensor(f"f1wu{tag}", [128, 2 * D], BF16))
        h4 = EG(nc.sbuf_tensor(f"f1h4{tag}", [128, 4 * CH], F32))
        sq = EG(nc.sbuf_tensor(f"f1sq{tag}", [128, 4 * CH], F32))
        pr = EG(nc.sbuf_tensor(f"f1pr{tag}", [128, 3 * CH], F32))
        geb = EG(nc.sbuf_tensor(f"f1ge{tag}", [128, CH], F32))
        u1 = EG(nc.sbuf_tensor(f"f1u1{tag}", [128, CH], F32))
        lsem = k.usem(f"f1l{tag}")
        psG = [p0, p1]
        psU = [p2, p3]
        mm_marks = {}
        h4_marks = {}
        for c in range(DCN):
            for r in range(4):
                it = c * 4 + r
                b = it % 2
                wwaits = [] if it < 2 else [(k.pe, mm_marks[it - 2])]
                ph.dma("gpsimd",
                       [(wgs[:, b * D:(b + 1) * D], wgp_in[it, :, :]),
                        (wus[:, b * D:(b + 1) * D], wup_in[it, :, :])],
                       lsem, waits=wwaits)
                ld_n = lsem.n
                evw = [] if it < 2 else [(k.dve, h4_marks[it - 2])]

                def mm(t, b=b):
                    for dc in range(DCN):
                        t.matmul(psG[b][:], wgs[:, b * D + dc * 128:
                                                b * D + (dc + 1) * 128],
                                 yT[:, wcol(dc)], start=(dc == 0),
                                 stop=(dc == DCN - 1))
                        last = t.matmul(
                            psU[b][:], wus[:, b * D + dc * 128:
                                           b * D + (dc + 1) * 128],
                            yT[:, wcol(dc)], start=(dc == 0),
                            stop=(dc == DCN - 1))
                    return last
                mm_marks[it] = ph.t(mm, waits=[(lsem, ld_n)] + evw, inc=1)
                gw = [(k.pe, mm_marks[it])]
                if c >= 1:
                    gw.append((k.dve, h4_marks[(c - 1) * 4 + 3]))
                a1 = ph.s(lambda s, b=b, r=r: s.activation(
                    out=h4[:, r * CH:(r + 1) * CH], in_=psG[b][:],
                    func=AF.Silu), waits=gw)
                uw = [] if it == 0 else [(k.dve, h4_marks[it - 1])]
                a2 = ph.s(lambda s, b=b: s.activation(
                    out=u1[:], in_=psU[b][:], func=AF.Copy), waits=uw)
                h4_marks[it] = ph.v(lambda v, r=r: v.tensor_tensor(
                    out=h4[:, r * CH:(r + 1) * CH],
                    in0=h4[:, r * CH:(r + 1) * CH], in1=u1[:], op=ALU.mult),
                    waits=[(k.act, a2)])
            hs = [h4[:, r * CH:(r + 1) * CH] for r in range(4)]
            sqs = [sq[:, r * CH:(r + 1) * CH] for r in range(4)]
            for r in range(4):
                ph.v(lambda v, r=r: v.tensor_tensor(
                    out=sqs[r], in0=hs[r], in1=hs[r], op=ALU.mult))
            pr0 = pr[:, 0:CH]
            pr1 = pr[:, CH:2 * CH]
            pr2 = pr[:, 2 * CH:3 * CH]
            ph.v(lambda v: v.tensor_tensor(out=pr0, in0=sqs[0], in1=sqs[1],
                                           op=ALU.max))
            ph.v(lambda v: v.tensor_tensor(out=pr1, in0=sqs[2], in1=sqs[3],
                                           op=ALU.max))
            ph.v(lambda v: v.tensor_tensor(out=pr0, in0=pr0, in1=pr1,
                                           op=ALU.min))
            ph.v(lambda v: v.tensor_tensor(out=pr1, in0=sqs[0], in1=sqs[1],
                                           op=ALU.min))
            ph.v(lambda v: v.tensor_tensor(out=pr2, in0=sqs[2], in1=sqs[3],
                                           op=ALU.min))
            ph.v(lambda v: v.tensor_tensor(out=pr1, in0=pr1, in1=pr2,
                                           op=ALU.max))
            ph.v(lambda v: v.tensor_tensor(out=pr0, in0=pr0, in1=pr1,
                                           op=ALU.max))
            for r in range(4):
                ph.v(lambda v, r=r: v.tensor_tensor(
                    out=geb[:], in0=sqs[r], in1=pr0, op=ALU.is_ge))
                m = ph.v(lambda v, r=r, c=c: v.tensor_tensor(
                    out=h_sb[:, (c * 4 + r) * CH:(c * 4 + r + 1) * CH],
                    in0=hs[r], in1=geb[:], op=ALU.mult))
                h4_marks[c * 4 + r] = m
        h_done = k.dve.n
        last_mm = mm_marks[DCN * 4 - 1]
        gsc_es.close()

        dsem = k.usem(f"f1ds{tag}")
        with nc.sbuf_tensor(f"f1wd{tag}", [128, 2 * HID1], BF16) as wds:
            dn_marks = {}
            x_marks = {}
            for dc in range(DCN):
                b = dc % 2
                wwaits = [(k.pe, last_mm), (k.dve, h_done)] if dc < 2 \
                    else [(k.pe, dn_marks[dc - 2])]
                ph.dma("gpsimd", [(wds[:, b * HID1:(b + 1) * HID1],
                                   wdp_in[dc, :, :])], dsem, waits=wwaits)
                ld_n = dsem.n
                evw = [(k.dve, h_done)] if dc < 2 else \
                    [(k.dve, x_marks[dc - 2])]

                def mmd(t, b=b):
                    last = None
                    for hc in range(HC1):
                        last = t.matmul(
                            psG[b][:], wds[:, b * HID1 + hc * 128:
                                           b * HID1 + (hc + 1) * 128],
                            h_sb[:, hc * CH:(hc + 1) * CH],
                            start=(hc == 0), stop=(hc == HC1 - 1))
                    return last
                dn_marks[dc] = ph.t(mmd, waits=[(dsem, ld_n)] + evw, inc=1)
                x_marks[dc] = ph.v(lambda v, dc=dc, b=b: v.scalar_tensor_tensor(
                    out=xT[:, col(dc, w)], in0=psG[b][:], scalar=0.5,
                    in1=xT[:, col(dc, w)], op0=ALU.mult, op1=ALU.add),
                    waits=[(k.pe, dn_marks[dc])])
            ph.flush()
    return k.dve.n


def emit_qkproj(k, yT, wpack_in, mcn, out_hbm, scale_col, tag, wpack2_in=None,
                ctab=None, stab=None):
    skw = {} if scale_col is None else {"scale": scale_col}
    """out_hbm[m*128.., w*CH..] = bf16(scale * W.T @ y)  (feature-major).
    If wpack2_in given: RoPE combine  qa*C + qb*S  (both scaled)."""
    nc = k.nc
    ph = Ph(k, gate=k.gate())
    rope = wpack2_in is not None
    with (
        nc.psum_tensor(f"qp0{tag}", [128, CH], F32) as p0,
        nc.psum_tensor(f"qp1{tag}", [128, CH], F32) as p1,
        nc.psum_tensor(f"qp2{tag}", [128, CH], F32) as p2,
        nc.psum_tensor(f"qp3{tag}", [128, CH], F32) as p3,
        nc.sbuf_tensor(f"qw{tag}", [128, 2 * D], BF16) as ws,
        nc.sbuf_tensor(f"qw2{tag}", [128, 2 * D], BF16) as ws2,
        nc.sbuf_tensor(f"qstg{tag}", [128, 2 * CH], BF16) as qstg,
        nc.sbuf_tensor(f"qsc{tag}", [128, 3 * CH], F32) as qsc,
    ):
        lsem = k.usem(f"qpl{tag}")
        ssem = k.usem(f"qps{tag}")
        k.store_sems.append(ssem)
        psA = [p0, p1]
        psB = [p2, p3]
        mm_marks = {}
        prev_vm = 0
        for m in range(mcn):
            b = m % 2
            wwaits = [] if m < 2 else [(k.pe, mm_marks[(m - 2, NW - 1)])]
            pairs = [(ws[:, b * D:(b + 1) * D], wpack_in[m, :, :])]
            if rope:
                pairs.append((ws2[:, b * D:(b + 1) * D], wpack2_in[m, :, :]))
            ph.dma("gpsimd", pairs, lsem, waits=wwaits)
            ld_n = lsem.n
            for w in range(NW):
                i = m * NW + w
                bb = i % 2

                def mm(t, b=b, bb=bb, w=w):
                    last = None
                    for dc in range(DCN):
                        last = t.matmul(
                            psA[bb][:], ws[:, b * D + dc * 128:
                                           b * D + (dc + 1) * 128],
                            yT[:, col(dc, w)], start=(dc == 0),
                            stop=(dc == DCN - 1))
                        if rope:
                            last = t.matmul(
                                psB[bb][:], ws2[:, b * D + dc * 128:
                                                b * D + (dc + 1) * 128],
                                yT[:, col(dc, w)], start=(dc == 0),
                                stop=(dc == DCN - 1))
                    return last
                evw = [] if i < 2 else [(ssem, 16 * (i - 1))]
                mm_marks[(m, w)] = ph.t(mm, waits=[(lsem, ld_n)] + evw,
                                        inc=1)
                if not rope:
                    am = ph.s(lambda s, bb=bb, i=i: s.activation(
                        out=qstg[:, (i % 2) * CH:(i % 2 + 1) * CH],
                        in_=psA[bb][:], func=AF.Copy, **skw),
                        waits=[(k.pe, mm_marks[(m, w)])])
                    gate_done = (k.act, am)
                else:
                    ph.s(lambda s, bb=bb: s.activation(
                        out=qsc[:, 0:CH], in_=psA[bb][:], func=AF.Copy,
                        **skw),
                        waits=[(k.pe, mm_marks[(m, w)]), (k.dve, prev_vm)])
                    am = ph.s(lambda s, bb=bb: s.activation(
                        out=qsc[:, CH:2 * CH], in_=psB[bb][:], func=AF.Copy,
                        **skw))
                    ph.v(lambda v, w=w: v.tensor_tensor(
                        out=qsc[:, 0:CH], in0=qsc[:, 0:CH],
                        in1=ctab[:, w * CH:(w + 1) * CH], op=ALU.mult),
                        waits=[(k.act, am)])
                    ph.v(lambda v, w=w: v.tensor_tensor(
                        out=qsc[:, 2 * CH:3 * CH], in0=qsc[:, CH:2 * CH],
                        in1=stab[:, w * CH:(w + 1) * CH], op=ALU.mult))
                    vm = ph.v(lambda v, i=i: v.tensor_tensor(
                        out=qstg[:, (i % 2) * CH:(i % 2 + 1) * CH],
                        in0=qsc[:, 0:CH], in1=qsc[:, 2 * CH:3 * CH],
                        op=ALU.add))
                    prev_vm = vm
                    gate_done = (k.dve, vm)
                ph.dma("sync", [(out_hbm[m * 128:(m + 1) * 128,
                                         w * CH:(w + 1) * CH],
                                 qstg[:, (i % 2) * CH:(i % 2 + 1) * CH])],
                       ssem, waits=[gate_done])
        ph.flush()
    return ssem


def emit_vproj(k, yT, wv_in, out_hbm, tag):
    nc = k.nc
    ph = Ph(k, gate=k.gate())
    with (
        nc.psum_tensor(f"vp0{tag}", [128, CH], F32) as p0,
        nc.psum_tensor(f"vp1{tag}", [128, CH], F32) as p1,
        nc.sbuf_tensor(f"vw{tag}", [128, DCN * CH], BF16) as wv_sb,
        nc.sbuf_tensor(f"vstg{tag}", [128, 2 * CH], BF16) as vstg,
    ):
        lsem = k.usem(f"vpl{tag}")
        ssem = k.usem(f"vps{tag}")
        k.store_sems.append(ssem)
        ph.dma("gpsimd", [(wv_sb[:, dc * CH:(dc + 1) * CH],
                           wv_in[dc * 128:(dc + 1) * 128, :])
                          for dc in range(DCN)], lsem)
        ld_n = lsem.n
        ps = [p0, p1]
        mm_marks = {}
        for tt in range(NTOK // 128):
            b = tt % 2
            evw = [] if tt < 2 else [(k.act, mm_marks[tt - 2])]

            def mm(t, tt=tt, b=b):
                last = None
                for dc in range(DCN):
                    last = t.matmul(
                        ps[b][:], yT[:, dc * NTOK + tt * 128:
                                     dc * NTOK + (tt + 1) * 128],
                        wv_sb[:, dc * CH:(dc + 1) * CH],
                        start=(dc == 0), stop=(dc == DCN - 1))
                return last
            pm = ph.t(mm, waits=[(lsem, ld_n)] + evw, inc=1)
            stw = [] if tt < 2 else [(ssem, 16 * (tt - 1))]
            am = ph.s(lambda s, tt=tt, b=b: s.activation(
                out=vstg[:, b * CH:(b + 1) * CH], in_=ps[b][:],
                func=AF.Copy), waits=[(k.pe, pm)] + stw)
            mm_marks[tt] = am
            ph.dma("sync", [(out_hbm[tt * 128:(tt + 1) * 128, :],
                             vstg[:, b * CH:(b + 1) * CH])], ssem,
                   waits=[(k.act, am)])
        ph.flush()
    return ssem


def emit_attn(k, xT, q_in, k_ins, v_ins, flags_sb, md_sb, wo_pack_in,
              hd, nh, bout_cols=None, clamp=True, tag="a"):
    nc = k.nc
    kv_rep = 4
    with (
        nc.psum_tensor(f"as0{tag}", [128, CH], F32) as ps0,
        nc.psum_tensor(f"as1{tag}", [128, CH], F32) as ps1,
        nc.psum_tensor(f"ac0{tag}", [128, CH], F32) as pc0,
        nc.psum_tensor(f"ac1{tag}", [128, CH], F32) as pc1,
        nc.psum_tensor(f"ar0{tag}", [128, CH], F32) as pr0,
        nc.psum_tensor(f"ar1{tag}", [128, CH], F32) as pr1,
        nc.psum_tensor(f"ab{tag}", [128, CH], F32) as pb,
        nc.sbuf_tensor(f"aq{tag}", [128 if hd == 128 else 64,
                                     (DCN if hd == 128 else nh) * CH],
                       BF16) as qw_sb,
        nc.sbuf_tensor(f"akl{tag}", [128, 2 * CH], BF16) as ksl,
        nc.sbuf_tensor(f"avl{tag}", [128, 2 * 4 * hd], BF16) as vsl,
        nc.sbuf_tensor(f"aet{tag}", [128, 2 * CH], F32) as et,
        nc.sbuf_tensor(f"apt{tag}", [128, 2 * CH], BF16) as pt,
        nc.sbuf_tensor(f"actx{tag}", [128, DCN * CH], BF16) as ctx_sb,
        nc.sbuf_tensor(f"abc{tag}", [128, CH], F32) as bcr,
        nc.sbuf_tensor(f"ascr{tag}", [128, CH], F32) as scrD,
        nc.sbuf_tensor(f"arr{tag}", [1, CH], F32) as rrow,
        nc.sbuf_tensor(f"awo{tag}", [128, 2 * D], BF16) as wos,
    ):
        psS = [ps0, ps1]
        psC = [pc0, pc1]
        psR = [pr0, pr1]
        for w in range(NW):
            ph = Ph(k, gate=k.gate())
            qsem = k.usem(f"aq{tag}{w}")
            if hd == 128:
                qpairs = [(qw_sb[:, m * CH:(m + 1) * CH],
                           q_in[m * 128:(m + 1) * 128,
                                w * CH:(w + 1) * CH]) for m in range(DCN)]
            else:
                qpairs = [(qw_sb[0:64, h * CH:(h + 1) * CH],
                           q_in[h * 64:(h + 1) * 64,
                                w * CH:(w + 1) * CH]) for h in range(nh)]
            ph.dma("sync", qpairs, qsem)
            q_n = qsem.n
            kvsem = k.usem(f"akv{tag}{w}")
            exp_marks = {}
            pt_marks = {}
            den_marks = {}
            den_marks_fine = {}
            ctx_done = {}
            prev_bm = 0
            prev_am = 0
            for h in range(nh):
                kv = h // kv_rep
                if hd == 128:
                    qrhs = qw_sb[:, h * CH:(h + 1) * CH]
                else:
                    qrhs = qw_sb[0:64, h * CH:(h + 1) * CH]
                for s in range(16):
                    b = s % 2
                    it = h * 16 + s
                    wwaits = [] if it < 2 else \
                        [(k.pe, den_marks[it - 2])]
                    if hd == 128:
                        kpair = (ksl[:, b * CH:(b + 1) * CH],
                                 k_ins[w][kv * 128:(kv + 1) * 128,
                                          s * CH:(s + 1) * CH])
                    else:
                        kpair = (ksl[0:64, b * CH:(b + 1) * CH],
                                 k_ins[w][0:64, (kv * 16 + s) * CH:
                                          (kv * 16 + s + 1) * CH])
                    vpairs = [(vsl[:, (b * 4 + ks) * hd:
                                   (b * 4 + ks + 1) * hd],
                               v_ins[w][s * CH + ks * 128:
                                        s * CH + (ks + 1) * 128,
                                        kv * hd:(kv + 1) * hd])
                              for ks in range(4)]
                    ph.dma("sync", [kpair] + vpairs, kvsem, waits=wwaits)
                    kv_n = kvsem.n
                    for ks in range(4):
                        i4 = it * 4 + ks
                        sb = ks % 2
                        sw = [(kvsem, kv_n), (qsem, q_n)]
                        if i4 >= 2:
                            sw.append((k.act, exp_marks[i4 - 2]))

                        def mmS(t, b=b, sb=sb, ks=ks, qrhs=qrhs):
                            if hd == 128:
                                lhs = ksl[:, b * CH + ks * 128:
                                          b * CH + ks * 128 + 128]
                            else:
                                lhs = ksl[0:64, b * CH + ks * 128:
                                          b * CH + ks * 128 + 128]
                            return t.matmul(psS[sb][:], lhs, qrhs,
                                            start=True, stop=True)
                        sm = ph.t(mmS, waits=sw, inc=1)
                        eww = [] if i4 < 2 else [(k.dve, pt_marks[i4 - 2])]
                        if s == 15:
                            dv = ph.v(lambda v, sb=sb, ks=ks: v.tensor_tensor(
                                out=scrD[:], in0=psS[sb][:],
                                in1=md_sb[:, ks * CH:(ks + 1) * CH],
                                op=ALU.add),
                                waits=[(k.pe, sm), (k.act, k.act.n)])
                            em = ph.s(lambda s_, sb=sb: s_.activation(
                                out=et[:, sb * CH:(sb + 1) * CH],
                                in_=scrD[:], func=AF.Exp),
                                waits=[(k.dve, dv)] + eww)
                        else:
                            em = ph.s(lambda s_, sb=sb, fcol=w * 16 + s:
                                      s_.activation(
                                          out=et[:, sb * CH:(sb + 1) * CH],
                                          in_=psS[sb][:], func=AF.Exp,
                                          bias=flags_sb[:, fcol:fcol + 1]),
                                      waits=[(k.pe, sm)] + eww)
                        exp_marks[i4] = em
                        ptw = [(k.act, em)]
                        if i4 >= 2:
                            ptw.append((k.pe, den_marks_fine[i4 - 2]))
                        if clamp:
                            pm = ph.v(lambda v, sb=sb: v.tensor_scalar(
                                out=pt[:, sb * CH:(sb + 1) * CH],
                                in0=et[:, sb * CH:(sb + 1) * CH],
                                scalar1=E50P, scalar2=E50N, op0=ALU.min,
                                op1=ALU.max), waits=ptw)
                        else:
                            pm = ph.v(lambda v, sb=sb: v.tensor_scalar(
                                out=pt[:, sb * CH:(sb + 1) * CH],
                                in0=et[:, sb * CH:(sb + 1) * CH],
                                scalar1=FBIG, scalar2=None, op0=ALU.min),
                                waits=ptw)
                        pt_marks[i4] = pm
                        cw = [(k.dve, pm), (kvsem, kv_n)]
                        if s == 0 and ks == 0 and h >= 2:
                            cw.append((k.dve, ctx_done[h - 2]))
                        st_ = (s == 0 and ks == 0)
                        sp_ = (s == 15 and ks == 3)

                        def mmC(t, b=b, sb=sb, ks=ks, h=h, st_=st_, sp_=sp_):
                            t.matmul(psC[h % 2][0:hd, :],
                                     vsl[:, (b * 4 + ks) * hd:
                                         (b * 4 + ks + 1) * hd],
                                     pt[:, sb * CH:(sb + 1) * CH],
                                     start=st_, stop=sp_)
                            return t.matmul(
                                psR[h % 2][0:1, :], k.onec_b[:],
                                pt[:, sb * CH:(sb + 1) * CH],
                                start=st_, stop=sp_)
                        dm = ph.t(mmC, waits=cw, inc=1)
                        den_marks_fine[i4] = dm
                    den_marks[it] = den_marks_fine[it * 4 + 3]
                dpe = den_marks[h * 16 + 15]
                rw = [(k.pe, dpe)]
                if h >= 1:
                    rw.append((k.pe, prev_bm))
                ph.v(lambda v, h=h: v.reciprocal(
                    out=rrow[:], in_=psR[h % 2][0:1, :]), waits=rw)
                rv = k.dve.n
                bw = [(k.dve, rv)]
                if h >= 1:
                    bw.append((k.act, prev_am))
                bm = ph.t(lambda t: t.matmul(
                    pb[0:hd, :], k.oner_f[0:1, 0:hd], rrow[:],
                    start=True, stop=True), waits=bw, inc=1)
                prev_bm = bm
                aw = [(k.pe, bm)]
                if h >= 1:
                    aw.append((k.dve, ctx_done[h - 1]))
                am = ph.s(lambda s: s.activation(
                    out=bcr[0:hd, :], in_=pb[0:hd, :], func=AF.Copy),
                    waits=aw)
                prev_am = am
                if hd == 128:
                    cdst = ctx_sb[:, h * CH:(h + 1) * CH]
                    csrc = psC[h % 2][:]
                    bsrc = bcr[:]
                else:
                    cdst = ctx_sb[(h % 2) * 64:(h % 2) * 64 + 64,
                                  (h // 2) * CH:(h // 2 + 1) * CH]
                    csrc = psC[h % 2][0:64, :]
                    bsrc = bcr[0:64, :]
                ctx_done[h] = ph.v(lambda v, cdst=cdst, csrc=csrc,
                                   bsrc=bsrc: v.tensor_tensor(
                    out=cdst, in0=csrc, in1=bsrc, op=ALU.mult),
                    waits=[(k.act, am)])
            ctx_all = k.dve.n

            osem = k.usem(f"ao{tag}{w}")
            op_marks = {}
            x_marks = {}
            for mc in range(DCN):
                b = mc % 2
                wwaits = [] if mc < 2 else [(k.pe, op_marks[mc - 2])]
                ph.dma("gpsimd", [(wos[:, b * D:(b + 1) * D],
                                   wo_pack_in[mc, :, :])], osem,
                       waits=wwaits)
                ld_n = osem.n
                evw = [(k.dve, ctx_all)] if mc < 2 else \
                    [(k.dve, x_marks[mc - 2])]

                def mmO(t, b=b):
                    last = None
                    for kc in range(DCN):
                        last = t.matmul(
                            psS[b][:], wos[:, b * D + kc * 128:
                                           b * D + (kc + 1) * 128],
                            ctx_sb[:, kc * CH:(kc + 1) * CH],
                            start=(kc == 0), stop=(kc == DCN - 1))
                    return last
                op_marks[mc] = ph.t(mmO, waits=[(osem, ld_n)] + evw, inc=1)
                if bout_cols is None:
                    x_marks[mc] = ph.v(lambda v, mc=mc, b=b: v.tensor_tensor(
                        out=xT[:, col(mc, w)], in0=psS[b][:],
                        in1=xT[:, col(mc, w)], op=ALU.add),
                        waits=[(k.pe, op_marks[mc])])
                else:
                    x_marks[mc] = ph.v(
                        lambda v, mc=mc, b=b: v.scalar_tensor_tensor(
                            out=xT[:, col(mc, w)], in0=psS[b][:],
                            scalar=bout_cols[:, mc:mc + 1],
                            in1=xT[:, col(mc, w)], op0=ALU.add, op1=ALU.add),
                        waits=[(k.pe, op_marks[mc])])
            ph.flush()
    return k.dve.n


def emit_load_misc(k, pairs):
    ph = Ph(k, gate=k.gate())
    sem = k.usem("misc")
    ph.dma("sync", pairs, sem)
    ph.flush()
    return sem


def emit_allgather(k, pairs):
    ph = Ph(k, gate=k.gate())
    sem = k.usem("cc")
    for ih, oh in pairs:
        def fn(g, ih=ih, oh=oh):
            return g.collective_compute(
                "AllGather", ALU.bypass,
                replica_groups=[list(range(NC))],
                ins=[ih.ap().opt()], outs=[oh.ap().opt()])
        sem.inc(1)
        ph._add("gpsimd", fn, [], [(sem, 1)])
    ph.flush()
    return sem


def emit_attn_f(k, xT, q_in, kg, vg, ksh, vsh, flags_sb, md_sb, wo_pack_in,
                hd, nh, bout_cols=None, clamp=True, tag="a"):
    nc = k.nc
    kv_rep = 4
    NS = 17
    vstr = hd + (1 if hd == 64 else 0)
    with (
        nc.psum_tensor(f"asb{tag}", [128, 4 * CH], F32) as psSb,
        nc.psum_tensor(f"ac0{tag}", [128, CH], F32) as pc0,
        nc.psum_tensor(f"ac1{tag}", [128, CH], F32) as pc1,
        nc.psum_tensor(f"ab{tag}", [128, CH], F32) as pb,
        nc.sbuf_tensor(f"aq{tag}", [128 if hd == 128 else 64,
                                    (DCN if hd == 128 else nh) * CH],
                       BF16) as qw_sb,
        nc.sbuf_tensor(f"akl{tag}", [128, 2 * CH], BF16) as ksl,
        nc.sbuf_tensor(f"avl{tag}", [128, 2 * 4 * vstr], BF16) as vsl,
        nc.sbuf_tensor(f"aet{tag}", [128, 2 * CH], F32) as et,
        nc.sbuf_tensor(f"apt{tag}", [128, 8 * CH], BF16) as pt,
        nc.sbuf_tensor(f"actx{tag}", [128, DCN * CH], BF16) as ctx_sb,
        nc.sbuf_tensor(f"abc{tag}", [128, CH], F32) as bcr,
        nc.sbuf_tensor(f"ascr{tag}", [128, CH], F32) as scrD,
        nc.sbuf_tensor(f"ascr2{tag}", [128, CH], F32) as scrE,
        nc.sbuf_tensor(f"arr{tag}", [128, CH], F32) as rrow,
        nc.sbuf_tensor(f"aon64{tag}", [128, 64], F32) as on64,
        nc.sbuf_tensor(f"awo{tag}", [128, 2 * D], BF16) as wos,
    ):
        psC = [pc0, pc1]
        if hd == 64:
            psR = None
        else:
            psR_cm = nc.psum_tensor(f"ar0{tag}", [128, CH], F32)
            psR = psR_cm.__enter__()
        ph0 = Ph(k, gate=k.gate())
        ph0.v(lambda v: v.memset(on64[64:65, :], 1.0))
        if hd == 64:
            ph0.v(lambda v: v.memset(vsl[:, :], 1.0))
        ph0.flush()
        init_done = k.dve.n
        for w in range(NW):
            ph = Ph(k, gate=k.gate())
            qsem = k.usem(f"aq{tag}{w}")
            if hd == 128:
                qpairs = [(qw_sb[:, m * CH:(m + 1) * CH],
                           q_in[m * 128:(m + 1) * 128,
                                w * CH:(w + 1) * CH]) for m in range(DCN)]
            else:
                qpairs = [(qw_sb[0:64, h * CH:(h + 1) * CH],
                           q_in[h * 64:(h + 1) * 64,
                                w * CH:(w + 1) * CH]) for h in range(nh)]
            ph.dma("sync", qpairs, qsem)
            q_n = qsem.n
            kvsem = k.usem(f"akv{tag}{w}")
            sm_marks = {}
            exp_marks = {}
            pt_marks = {}
            cd_marks = {}
            kvn_marks = {}
            ctx_done = {}
            recip_marks = {}
            prev_bm = 0
            prev_am = 0

            def emit_ctx(h, s, ph=ph):
                it = h * NS + s
                b = it % 2
                st_ = (s == 0)
                sp_ = (s == NS - 1)
                for ks in range(4):
                    cw = [(k.dve, pt_marks[(it, ks)]),
                          (kvsem, kvn_marks[it])]
                    if st_ and ks == 0:
                        if h >= 2:
                            cw.append((k.dve, ctx_done[h - 2]))
                        if hd == 128 and h >= 1:
                            cw.append((k.dve, recip_marks[h - 1]))
                    psrc = pt[:, ((it % 2) * 4 + ks) * CH:
                              ((it % 2) * 4 + ks + 1) * CH]

                    def mmC(t, b=b, ks=ks, h=h, psrc=psrc,
                            st2=(st_ and ks == 0), sp2=(sp_ and ks == 3)):
                        m = t.matmul(psC[h % 2][0:vstr, :],
                                     vsl[:, (b * 4 + ks) * vstr:
                                         (b * 4 + ks) * vstr + vstr],
                                     psrc, start=st2, stop=sp2)
                        if hd == 128:
                            m = t.matmul(psR[0:1, :], k.onec_b[:],
                                         psrc, start=st2, stop=sp2)
                        return m
                    cd_marks[(it, ks)] = ph.t(mmC, waits=cw, inc=1)

            for h in range(nh):
                kv = h // kv_rep
                if hd == 128:
                    qrhs = qw_sb[:, h * CH:(h + 1) * CH]
                else:
                    qrhs = qw_sb[0:64, h * CH:(h + 1) * CH]
                for s in range(NS):
                    diag = (s == NS - 1)
                    it = h * NS + s
                    b = it % 2
                    wwaits = []
                    if it >= 2:
                        wwaits = [(k.pe, sm_marks[(it - 2, 3)]),
                                  (k.pe, cd_marks[(it - 2, 3)])]
                    elif hd == 64:
                        wwaits = [(k.dve, init_done)]
                    if diag:
                        if hd == 128:
                            ksrc = ksh[kv * 128:(kv + 1) * 128,
                                       w * CH:(w + 1) * CH]
                        else:
                            ksrc = ksh[kv * 64:kv * 64 + 64,
                                       w * CH:(w + 1) * CH]
                        vsrcs = [vsh[w * CH + ks * 128:
                                     w * CH + (ks + 1) * 128,
                                     kv * hd:(kv + 1) * hd]
                                 for ks in range(4)]
                    else:
                        r, lc = s // 2, (s % 2) * CH
                        if hd == 128:
                            ksrc = kg[r * 512 + kv * 128:
                                      r * 512 + (kv + 1) * 128,
                                      lc:lc + CH]
                        else:
                            ksrc = kg[r * 512 + kv * 64:
                                      r * 512 + kv * 64 + 64,
                                      lc:lc + CH]
                        vsrcs = [vg[r * NTOK + lc + ks * 128:
                                    r * NTOK + lc + (ks + 1) * 128,
                                    kv * hd:(kv + 1) * hd]
                                 for ks in range(4)]
                    if hd == 128:
                        kpair = (ksl[:, b * CH:(b + 1) * CH], ksrc)
                    else:
                        kpair = (ksl[0:64, b * CH:(b + 1) * CH], ksrc)
                    vpairs = [(vsl[:, (b * 4 + ks) * vstr:
                                   (b * 4 + ks) * vstr + hd], vsrcs[ks])
                              for ks in range(4)]
                    ph.dma("sync", [kpair] + vpairs, kvsem, waits=wwaits)
                    kvn_marks[it] = kvsem.n
                    for ks in range(4):
                        sw = [(kvsem, kvn_marks[it]), (qsem, q_n)]
                        if it >= 1:
                            sw.append((k.act, exp_marks[(it - 1, ks)]))

                        def mmS(t, b=b, ks=ks, qrhs=qrhs):
                            if hd == 128:
                                lhs = ksl[:, b * CH + ks * 128:
                                          b * CH + ks * 128 + 128]
                            else:
                                lhs = ksl[0:64, b * CH + ks * 128:
                                          b * CH + ks * 128 + 128]
                            return t.matmul(psSb[:, ks * CH:(ks + 1) * CH],
                                            lhs, qrhs, start=True, stop=True)
                        sm_marks[(it, ks)] = ph.t(mmS, waits=sw, inc=1)
                    for ks in range(4):
                        eww = []
                        if it * 4 + ks >= 2:
                            pi = (it * 4 + ks) - 2
                            eww = [(k.dve, pt_marks[(pi // 4, pi % 4)])]
                        if diag:
                            dv = ph.v(lambda v, ks=ks: v.tensor_tensor(
                                out=scrD[:],
                                in0=psSb[:, ks * CH:(ks + 1) * CH],
                                in1=md_sb[:, ks * CH:(ks + 1) * CH],
                                op=ALU.add),
                                waits=[(k.pe, sm_marks[(it, ks)]),
                                       (k.act, k.act.n)])
                            em = ph.s(lambda s_, ks=ks: s_.activation(
                                out=et[:, (ks % 2) * CH:(ks % 2 + 1) * CH],
                                in_=scrD[:], func=AF.Exp),
                                waits=[(k.dve, dv)] + eww)
                        else:
                            em = ph.s(lambda s_, ks=ks, fcol=w * NS + s:
                                      s_.activation(
                                          out=et[:, (ks % 2) * CH:
                                                 (ks % 2 + 1) * CH],
                                          in_=psSb[:, ks * CH:(ks + 1) * CH],
                                          func=AF.Exp,
                                          bias=flags_sb[:, fcol:fcol + 1]),
                                      waits=[(k.pe, sm_marks[(it, ks)])] +
                                      eww)
                        exp_marks[(it, ks)] = em
                        pdst = pt[:, ((it % 2) * 4 + ks) * CH:
                                  ((it % 2) * 4 + ks + 1) * CH]
                        ptw = [(k.act, em)]
                        if it >= 2:
                            ptw.append((k.pe, cd_marks[(it - 2, ks)]))
                        if clamp:
                            if diag:
                                ph.v(lambda v, ks=ks: v.tensor_scalar(
                                    out=scrE[:],
                                    in0=et[:, (ks % 2) * CH:
                                           (ks % 2 + 1) * CH],
                                    scalar1=E50P, scalar2=E50N, op0=ALU.min,
                                    op1=ALU.max), waits=ptw)
                                pm = ph.v(lambda v, pdst=pdst: v.tensor_scalar(
                                    out=pdst, in0=scrE[:], scalar1=E50N,
                                    scalar2=None, op0=ALU.subtract))
                            else:
                                pm = ph.v(lambda v, ks=ks, pdst=pdst:
                                          v.tensor_scalar(
                                              out=pdst,
                                              in0=et[:, (ks % 2) * CH:
                                                     (ks % 2 + 1) * CH],
                                              scalar1=E50P, scalar2=E50N,
                                              op0=ALU.min, op1=ALU.max),
                                          waits=ptw)
                        else:
                            pm = ph.v(lambda v, ks=ks, pdst=pdst:
                                      v.tensor_scalar(
                                          out=pdst,
                                          in0=et[:, (ks % 2) * CH:
                                                 (ks % 2 + 1) * CH],
                                          scalar1=FBIG, scalar2=None,
                                          op0=ALU.min), waits=ptw)
                        pt_marks[(it, ks)] = pm
                    if s >= 1:
                        emit_ctx(h, s - 1)
                emit_ctx(h, NS - 1)
                dpe = cd_marks[(h * NS + NS - 1, 3)]
                rw = [(k.pe, dpe)]
                if h >= 1:
                    rw.append((k.pe, prev_bm))
                if hd == 128:
                    recip_marks[h] = ph.v(lambda v: v.reciprocal(
                        out=rrow[0:1, :], in_=psR[0:1, :]), waits=rw)
                    rsrc = rrow[0:1, :]
                    blhs = k.oner_f[0:1, 0:hd]
                else:
                    recip_marks[h] = ph.v(lambda v, h=h: v.reciprocal(
                        out=rrow[64:65, :], in_=psC[h % 2][64:65, :]),
                        waits=rw)
                    rsrc = rrow[64:65, :]
                    blhs = on64[64:65, :]
                rv = k.dve.n
                bw = [(k.dve, rv)]
                if h >= 1:
                    bw.append((k.act, prev_am))
                bm = ph.t(lambda t, rsrc=rsrc, blhs=blhs: t.matmul(
                    pb[0:hd, :], blhs, rsrc,
                    start=True, stop=True), waits=bw, inc=1)
                prev_bm = bm
                aw = [(k.pe, bm)]
                if h >= 1:
                    aw.append((k.dve, ctx_done[h - 1]))
                am = ph.s(lambda s: s.activation(
                    out=bcr[0:hd, :], in_=pb[0:hd, :], func=AF.Copy),
                    waits=aw)
                prev_am = am
                if hd == 128:
                    cdst = ctx_sb[:, h * CH:(h + 1) * CH]
                    csrc = psC[h % 2][:]
                    bsrc = bcr[:]
                else:
                    cdst = ctx_sb[(h % 2) * 64:(h % 2) * 64 + 64,
                                  (h // 2) * CH:(h // 2 + 1) * CH]
                    csrc = psC[h % 2][0:64, :]
                    bsrc = bcr[0:64, :]
                ctx_done[h] = ph.v(lambda v, cdst=cdst, csrc=csrc,
                                   bsrc=bsrc: v.tensor_tensor(
                    out=cdst, in0=csrc, in1=bsrc, op=ALU.mult),
                    waits=[(k.act, am)])
            ctx_all = k.dve.n

            osem = k.usem(f"ao{tag}{w}")
            op_marks = {}
            x_marks = {}
            for mc in range(DCN):
                b = mc % 2
                wwaits = [] if mc < 2 else [(k.pe, op_marks[mc - 2])]
                ph.dma("gpsimd", [(wos[:, b * D:(b + 1) * D],
                                   wo_pack_in[mc, :, :])], osem,
                       waits=wwaits)
                ld_n = osem.n
                evw = [(k.dve, ctx_all)] if mc < 2 else \
                    [(k.dve, x_marks[mc - 2])]

                def mmO(t, b=b):
                    last = None
                    for kc in range(DCN):
                        last = t.matmul(
                            psSb[:, (b % 2) * CH:(b % 2 + 1) * CH],
                            wos[:, b * D + kc * 128:b * D + (kc + 1) * 128],
                            ctx_sb[:, kc * CH:(kc + 1) * CH],
                            start=(kc == 0), stop=(kc == DCN - 1))
                    return last
                op_marks[mc] = ph.t(mmO, waits=[(osem, ld_n)] + evw, inc=1)
                if bout_cols is None:
                    x_marks[mc] = ph.v(lambda v, mc=mc, b=b: v.tensor_tensor(
                        out=xT[:, col(mc, w)],
                        in0=psSb[:, (b % 2) * CH:(b % 2 + 1) * CH],
                        in1=xT[:, col(mc, w)], op=ALU.add),
                        waits=[(k.pe, op_marks[mc])])
                else:
                    x_marks[mc] = ph.v(
                        lambda v, mc=mc, b=b: v.scalar_tensor_tensor(
                            out=xT[:, col(mc, w)],
                            in0=psSb[:, (b % 2) * CH:(b % 2 + 1) * CH],
                            scalar=bout_cols[:, mc:mc + 1],
                            in1=xT[:, col(mc, w)], op0=ALU.add, op1=ALU.add),
                        waits=[(k.pe, op_marks[mc])])
            ph.flush()
        if hd != 64:
            psR_cm.__exit__(None, None, None)
    return k.dve.n


def build_fused():
    k = K()
    xin = k.din("xT_in", [D, NTOK], F32)
    wg0p = k.din("wg0p", [HC0, 128, D], I32)
    wu0p = k.din("wu0p", [HC0, 128, D], I32)
    wd0p = k.din("wd0p", [DCN, 128, HID0], I32)
    sg = k.din("s_gate", [HID0], F32)
    su = k.din("s_up", [HID0], F32)
    sd = k.din("s_down", [D], F32)
    wq0p = k.din("wq0p", [DCN, 128, D], F32)
    wk0p = k.din("wk0p", [4, 128, D], F32)
    wv0 = k.din("wv0", [D, CH], F32)
    wo0p = k.din("wo0p", [DCN, 128, D], F32)
    flags0 = k.din("flags0", [128, 34], F32)
    flags1 = k.din("flags1", [128, 34], F32)
    md = k.din("md", [128, 4 * CH], F32)
    wg1p = k.din("wg1p", [HC1, 128, D], F32)
    wu1p = k.din("wu1p", [HC1, 128, D], F32)
    wd1p = k.din("wd1p", [DCN, 128, HID1], F32)
    wq1p = k.din("wq1p", [DCN, 128, D], F32)
    wq1sp = k.din("wq1sp", [DCN, 128, D], F32)
    wk1p = k.din("wk1p", [4, 128, D], F32)
    wk1sp = k.din("wk1sp", [4, 128, D], F32)
    wv1 = k.din("wv1", [D, CH], F32)
    wo1p = k.din("wo1p", [DCN, 128, D], F32)
    bout = k.din("b_out", [D], F32)
    ctab_in = k.din("ropeC", [128, NTOK], F32)
    stab_in = k.din("ropeS", [128, NTOK], F32)
    outx = k.dout("outT", [D, NTOK], F32)

    nc = k.nc
    q0d_h = nc.dram_tensor("q0d", [D, NTOK], BF16)
    k0sh_h = nc.dram_tensor("k0sh", [CH, NTOK], BF16)
    v0sh_h = nc.dram_tensor("v0sh", [NTOK, CH], BF16)
    k0g_h = nc.dram_tensor("k0g", [NC * CH, NTOK], BF16)
    v0g_h = nc.dram_tensor("v0g", [T, CH], BF16)
    q1d_h = nc.dram_tensor("q1d", [D, NTOK], BF16)
    k1sh_h = nc.dram_tensor("k1sh", [CH, NTOK], BF16)
    v1sh_h = nc.dram_tensor("v1sh", [NTOK, CH], BF16)
    k1g_h = nc.dram_tensor("k1g", [NC * CH, NTOK], BF16)
    v1g_h = nc.dram_tensor("v1g", [T, CH], BF16)

    k.start(consts=(RT128, 0.125))
    xT = k.sb("xT", [128, DCN * NTOK], F32)
    flags0_sb = k.sb("flags0_sb", [128, 34], F32)
    flags1_sb = k.sb("flags1_sb", [128, 34], F32)
    md_sb = k.sb("md_sb", [128, 4 * CH], F32)
    ctab = k.sb("ctab", [128, NTOK], F32)
    stab = k.sb("stab", [128, NTOK], F32)
    bout_cols = k.sb("bout_cols", [128, DCN], F32)
    sgc = k.sb("sgc", [128, HC0], F32)
    suc = k.sb("suc", [128, HC0], F32)
    sdc = k.sb("sdc", [128, DCN], F32)

    lx = emit_load_xT(k, xin, xT)
    emit_load_misc(k, [(flags0_sb[:], flags0), (flags1_sb[:], flags1),
                       (md_sb[:], md), (ctab[:], ctab_in),
                       (stab[:], stab_in)])
    emit_vec_cols(k, sg, HID0, sgc)
    emit_vec_cols(k, su, HID0, suc)
    emit_vec_cols(k, sd, D, sdc, scale=0.5)
    emit_vec_cols(k, bout, D, bout_cols)

    with nc.sbuf_tensor("yw1", [128, DCN * CH], BF16) as yw, \
            nc.sbuf_tensor("cs1", [1, 2 * CH], BF16) as csum:
        for w in range(NW):
            emit_ln(k, xT, w, yw, lambda dc: wcol(dc), True,
                    csum[0:1, w * CH:(w + 1) * CH], ld_sem=lx)
            emit_ffn0(k, xT, yw, csum[0:1, w * CH:(w + 1) * CH], wg0p, wu0p,
                      wd0p, sgc, suc, sdc, w, f"fa{w}")
    with nc.sbuf_tensor("yf1", [128, DCN * NTOK], BF16) as yT:
        for w in range(NW):
            emit_ln(k, xT, w, yT, lambda dc, w=w: col(dc, w), False, None)
        emit_qkproj(k, yT, wq0p, DCN, q0d_h.ap(), k.constcol(RT128), "q0")
        emit_qkproj(k, yT, wk0p, 4, k0sh_h.ap(), None, "k0")
        emit_vproj(k, yT, wv0, v0sh_h.ap(), "v0")
    emit_allgather(k, [(k0sh_h, k0g_h), (v0sh_h, v0g_h)])
    emit_attn_f(k, xT, q0d_h.ap(), k0g_h.ap(), v0g_h.ap(), k0sh_h.ap(),
                v0sh_h.ap(), flags0_sb, md_sb, wo0p, 128, 16,
                bout_cols=None, clamp=True, tag="a0")
    with nc.sbuf_tensor("yw3", [128, DCN * CH], BF16) as yw, \
            nc.sbuf_tensor("cs3", [1, 2 * CH], BF16) as csum:
        for w in range(NW):
            emit_ln(k, xT, w, yw, lambda dc: wcol(dc), True,
                    csum[0:1, w * CH:(w + 1) * CH])
            emit_ffn0(k, xT, yw, csum[0:1, w * CH:(w + 1) * CH], wg0p, wu0p,
                      wd0p, sgc, suc, sdc, w, f"fb{w}")
    with nc.sbuf_tensor("yw4", [128, DCN * CH], BF16) as yw:
        for w in range(NW):
            emit_ln(k, xT, w, yw, lambda dc: wcol(dc), False, None)
            emit_ffn1(k, xT, yw, wg1p, wu1p, wd1p, w, f"fc{w}")
    with nc.sbuf_tensor("yf4", [128, DCN * NTOK], BF16) as yT:
        for w in range(NW):
            emit_ln(k, xT, w, yT, lambda dc, w=w: col(dc, w), False, None)
        emit_qkproj(k, yT, wq1p, DCN, q1d_h.ap(), k.constcol(0.125), "q1",
                    wpack2_in=wq1sp, ctab=ctab, stab=stab)
        emit_qkproj(k, yT, wk1p, 4, k1sh_h.ap(), None, "k1",
                    wpack2_in=wk1sp, ctab=ctab, stab=stab)
        emit_vproj(k, yT, wv1, v1sh_h.ap(), "v1")
    emit_allgather(k, [(k1sh_h, k1g_h), (v1sh_h, v1g_h)])
    emit_attn_f(k, xT, q1d_h.ap(), k1g_h.ap(), v1g_h.ap(), k1sh_h.ap(),
                v1sh_h.ap(), flags1_sb, md_sb, wo1p, 64, 32,
                bout_cols=bout_cols, clamp=False, tag="a1")
    with nc.sbuf_tensor("yw6", [128, DCN * CH], BF16) as yw:
        for w in range(NW):
            emit_ln(k, xT, w, yw, lambda dc: wcol(dc), False, None)
            emit_ffn1(k, xT, yw, wg1p, wu1p, wd1p, w, f"fd{w}")
    emit_store_fm(k, xT, outx, D, waits=k.gate())
    return k.finish()


def _flags_f(core):
    f = np.zeros((128, 34), np.float32)
    for w in range(NW):
        jd = 2 * core + w
        for s in range(16):
            if s >= jd:
                f[:, w * 17 + s] = NEG1E9
    return f



def _pack_lhsT(W):
    Kd, N = W.shape
    a = W.reshape(Kd // 128, 128, N // 128, 128).transpose(2, 1, 0, 3)
    return np.ascontiguousarray(a.reshape(N // 128, 128, Kd))


def _md_mask():
    m = np.zeros((128, 4 * CH), np.float32)
    for ks in range(4):
        i = np.arange(128)[:, None] + 128 * ks
        j = np.arange(CH)[None, :]
        m[:, ks * CH:(ks + 1) * CH] = np.where(i > j, NEG1E9, 0.0)
    return m


def _flags(core):
    f = np.zeros((128, 32), np.float32)
    for w in range(NW):
        jd = 2 * core + w
        for s in range(16):
            if s < jd or s == 15:
                f[:, w * 16 + s] = 0.0
            else:
                f[:, w * 16 + s] = NEG1E9
    return f


def _slot_order(jd):
    return [c for c in range(NCH) if c != jd] + [jd]


def _rope_tabs(core):
    pos = np.arange(core * NTOK, (core + 1) * NTOK, dtype=np.float64)
    inv = 1.0 / (10000.0 ** (np.arange(32, dtype=np.float64) / 32.0))
    fr = np.outer(inv, pos)
    cos = np.cos(fr).astype(np.float32)
    sin = np.sin(fr).astype(np.float32)
    C = np.tile(cos, (4, 1))
    S = np.concatenate([-sin, sin, -sin, sin], axis=0)
    return np.ascontiguousarray(C), np.ascontiguousarray(S)


def _swap_heads(W, head_w):
    N = W.shape[1]
    idx = np.arange(N)
    h = idx // head_w
    r = idx % head_w
    half = head_w // 2
    src = h * head_w + np.where(r < half, r + half, r - half)
    return W[:, src]


def _ffn1_perm():
    n = np.arange(HID1)
    c = n // 512
    r = (n % 512) // 128
    p = n % 128
    return 512 * c + 4 * p + r


_CACHE = {}


def _progs():
    if "p" not in _CACHE:
        _CACHE["p"] = (build_p1(), build_p2(), build_p3(), build_p4())
    return _CACHE["p"]


def kernel(**inputs):
    inputs = {kk: np.asarray(vv) for kk, vv in inputs.items()}
    x = np.ascontiguousarray(inputs["L0_attn_hidden_states"][0])
    xT = np.ascontiguousarray(x.T)
    if "f" not in _CACHE:
        _CACHE["f"] = build_fused()
    ncf = _CACHE["f"]

    wg0p = _pack_lhsT(np.ascontiguousarray(inputs["L0_ffn_wq_gate"].T))
    wu0p = _pack_lhsT(np.ascontiguousarray(inputs["L0_ffn_wq_up"].T))
    wd0p = _pack_lhsT(np.ascontiguousarray(inputs["L0_ffn_wq_down"].T))
    wq0p = _pack_lhsT(inputs["L0_attn_Wq"])
    wk0p = _pack_lhsT(inputs["L0_attn_Wk"])
    wo0p = _pack_lhsT(inputs["L0_attn_Wo"])
    perm = _ffn1_perm()
    wg1p = _pack_lhsT(np.ascontiguousarray(inputs["L1_ffn_w_gate"][:, perm]))
    wu1p = _pack_lhsT(np.ascontiguousarray(inputs["L1_ffn_w_up"][:, perm]))
    wd1p = _pack_lhsT(np.ascontiguousarray(inputs["L1_ffn_w_down"][perm, :]))
    wq1 = inputs["L1_attn_Wq"]
    wk1 = inputs["L1_attn_Wk"]
    wq1p = _pack_lhsT(wq1)
    wq1sp = _pack_lhsT(np.ascontiguousarray(_swap_heads(wq1, 64)))
    wk1p = _pack_lhsT(wk1)
    wk1sp = _pack_lhsT(np.ascontiguousarray(_swap_heads(wk1, 64)))
    wo1p = _pack_lhsT(inputs["L1_attn_Wo"])
    md = _md_mask()

    maps = []
    for c in range(NC):
        Ct, St = _rope_tabs(c)
        fl = _flags_f(c)
        maps.append({
            "xT_in": np.ascontiguousarray(xT[:, c * NTOK:(c + 1) * NTOK]),
            "wg0p": wg0p, "wu0p": wu0p, "wd0p": wd0p,
            "s_gate": inputs["L0_ffn_s_gate"],
            "s_up": inputs["L0_ffn_s_up"],
            "s_down": inputs["L0_ffn_s_down"],
            "wq0p": wq0p, "wk0p": wk0p, "wv0": inputs["L0_attn_Wv"],
            "wo0p": wo0p, "flags0": fl, "flags1": fl, "md": md,
            "wg1p": wg1p, "wu1p": wu1p, "wd1p": wd1p,
            "wq1p": wq1p, "wq1sp": wq1sp, "wk1p": wk1p, "wk1sp": wk1sp,
            "wv1": inputs["L1_attn_Wv"], "wo1p": wo1p,
            "b_out": inputs["L1_attn_b_out"],
            "ropeC": Ct, "ropeS": St,
        })
    res = run_bass_kernel_spmd(ncf, maps, core_ids=list(range(NC))).results
    outT = np.concatenate([np.asarray(res[c]["outT"], np.float32)
                           for c in range(NC)], axis=1)
    return np.ascontiguousarray(outT.T)[None, :, :].astype(np.float32)


def kernel_multi(**inputs):
    inputs = {kk: np.asarray(vv) for kk, vv in inputs.items()}
    x = np.ascontiguousarray(inputs["L0_attn_hidden_states"][0])
    xT = np.ascontiguousarray(x.T)
    nc1, nc2, nc3, nc4 = _progs()

    wg0p = _pack_lhsT(np.ascontiguousarray(inputs["L0_ffn_wq_gate"].T))
    wu0p = _pack_lhsT(np.ascontiguousarray(inputs["L0_ffn_wq_up"].T))
    wd0p = _pack_lhsT(np.ascontiguousarray(inputs["L0_ffn_wq_down"].T))
    wq0p = _pack_lhsT(inputs["L0_attn_Wq"])
    wk0p = _pack_lhsT(inputs["L0_attn_Wk"])
    wo0p = _pack_lhsT(inputs["L0_attn_Wo"])
    perm = _ffn1_perm()
    wg1p = _pack_lhsT(np.ascontiguousarray(inputs["L1_ffn_w_gate"][:, perm]))
    wu1p = _pack_lhsT(np.ascontiguousarray(inputs["L1_ffn_w_up"][:, perm]))
    wd1p = _pack_lhsT(np.ascontiguousarray(inputs["L1_ffn_w_down"][perm, :]))
    wq1 = inputs["L1_attn_Wq"]
    wk1 = inputs["L1_attn_Wk"]
    wq1p = _pack_lhsT(wq1)
    wq1sp = _pack_lhsT(np.ascontiguousarray(_swap_heads(wq1, 64)))
    wk1p = _pack_lhsT(wk1)
    wk1sp = _pack_lhsT(np.ascontiguousarray(_swap_heads(wk1, 64)))
    wo1p = _pack_lhsT(inputs["L1_attn_Wo"])
    md = _md_mask()

    maps = []
    for c in range(NC):
        maps.append({
            "xT_in": np.ascontiguousarray(xT[:, c * NTOK:(c + 1) * NTOK]),
            "wg0p": wg0p, "wu0p": wu0p, "wd0p": wd0p,
            "s_gate": inputs["L0_ffn_s_gate"],
            "s_up": inputs["L0_ffn_s_up"],
            "s_down": inputs["L0_ffn_s_down"],
            "wq0p": wq0p, "wk0p": wk0p, "wv0": inputs["L0_attn_Wv"],
        })
    r1 = run_bass_kernel_spmd(nc1, maps, core_ids=list(range(NC))).results
    k0 = np.concatenate([r1[c]["k0T_out"] for c in range(NC)], axis=1)
    v0 = np.concatenate([r1[c]["v0_out"] for c in range(NC)], axis=0)

    maps = []
    for c in range(NC):
        m = {
            "x1T_in": r1[c]["x1T_out"],
            "q0T_in": r1[c]["q0T_out"],
            "flags0": _flags(c), "md0": md, "wo0p": wo0p,
            "wg0p": wg0p, "wu0p": wu0p, "wd0p": wd0p,
            "s_gate": inputs["L0_ffn_s_gate"],
            "s_up": inputs["L0_ffn_s_up"],
            "s_down": inputs["L0_ffn_s_down"],
        }
        for w in range(NW):
            order = _slot_order(2 * c + w)
            m[f"K0w{w}"] = np.ascontiguousarray(np.concatenate(
                [k0[:, ch * CH:(ch + 1) * CH] for ch in order], axis=1))
            m[f"V0w{w}"] = np.ascontiguousarray(np.concatenate(
                [v0[ch * CH:(ch + 1) * CH] for ch in order], axis=0))
        maps.append(m)
    r2 = run_bass_kernel_spmd(nc2, maps, core_ids=list(range(NC))).results

    maps = []
    for c in range(NC):
        Ct, St = _rope_tabs(c)
        maps.append({
            "x3T_in": r2[c]["x3T_out"],
            "wg1p": wg1p, "wu1p": wu1p, "wd1p": wd1p,
            "wq1p": wq1p, "wq1sp": wq1sp, "wk1p": wk1p, "wk1sp": wk1sp,
            "wv1": inputs["L1_attn_Wv"],
            "ropeC": Ct, "ropeS": St,
        })
    r3 = run_bass_kernel_spmd(nc3, maps, core_ids=list(range(NC))).results
    k1 = np.concatenate([r3[c]["k1T_out"] for c in range(NC)], axis=1)
    v1 = np.concatenate([r3[c]["v1_out"] for c in range(NC)], axis=0)
    k1r = k1.reshape(8, 64, NCH, CH)

    maps = []
    for c in range(NC):
        m = {
            "x4T_in": r3[c]["x4T_out"],
            "q1T_in": r3[c]["q1T_out"],
            "flags1": _flags(c), "md1": md, "wo1p": wo1p,
            "b_out": inputs["L1_attn_b_out"],
            "wg1p": wg1p, "wu1p": wu1p, "wd1p": wd1p,
        }
        for w in range(NW):
            order = _slot_order(2 * c + w)
            kk = k1r[:, :, order, :]
            kk = kk.transpose(1, 0, 2, 3).reshape(64, 8 * NCH * CH)
            m[f"K1w{w}"] = np.ascontiguousarray(kk)
            m[f"V1w{w}"] = np.ascontiguousarray(np.concatenate(
                [v1[ch * CH:(ch + 1) * CH] for ch in order], axis=0))
        maps.append(m)
    r4 = run_bass_kernel_spmd(nc4, maps, core_ids=list(range(NC))).results

    outT = np.concatenate([np.asarray(r4[c]["x6T_out"], np.float32)
                           for c in range(NC)], axis=1)
    return np.ascontiguousarray(outT.T)[None, :, :].astype(np.float32)


# version 07
# speedup vs baseline: 2.1685x; 1.4429x over previous
import contextlib

import numpy as np
import ml_dtypes

import concourse.bass as bass
import concourse.mybir as mybir
from concourse.bass_utils import run_bass_kernel_spmd

F32 = mybir.dt.float32
BF16 = mybir.dt.bfloat16
I32 = mybir.dt.int32
AF = mybir.ActivationFunctionType
ALU = mybir.AluOpType
BF = ml_dtypes.bfloat16

NC = 8
T = 8192
D = 2048
NTOK = 1024
CH = 512
NW = NTOK // CH
DCN = D // 128
NCH = T // CH
HID0 = 5632
HC0 = HID0 // 128
HID1 = 8192
HC1 = HID1 // 128
E50P = float(np.exp(np.float64(50.0)))
E50N = float(np.exp(np.float64(-50.0)))
RT128 = float(np.sqrt(np.float64(128.0)))
FBIG = 3.0e38
NEG1E9 = -1.0e9
EPS = 1e-5


def col(dc, w):
    return slice(dc * NTOK + w * CH, dc * NTOK + w * CH + CH)


def wcol(dc):
    return slice(dc * CH, dc * CH + CH)


class Sem:
    def __init__(self, h):
        self.h = h
        self.n = 0

    def inc(self, amt=1):
        self.n += amt
        return self.n


class K:

    def __init__(self):
        self.nc = bass.Bass("TRN2", target_bir_lowering=False)
        self.es = contextlib.ExitStack()
        self._sems = {}
        self.block = None
        self.store_sems = []
        self._useq = 0

    def sem(self, name) -> Sem:
        if name not in self._sems:
            self._sems[name] = Sem(self.es.enter_context(
                self.nc.semaphore(name)))
        return self._sems[name]

    def usem(self, prefix) -> Sem:
        self._useq += 1
        return self.sem(f"{prefix}{self._useq}")

    def sb(self, name, shape, dtype):
        return self.es.enter_context(self.nc.sbuf_tensor(name, shape, dtype))

    def din(self, name, shape, dtype):
        return self.nc.dram_tensor(name, shape, dtype,
                                   kind="ExternalInput").ap()

    def dout(self, name, shape, dtype):
        return self.nc.dram_tensor(name, shape, dtype,
                                   kind="ExternalOutput").ap()

    def start(self, consts=()):
        self.block = self.es.enter_context(self.nc.Block())
        self.pe = self.sem("pe")
        self.act = self.sem("act")
        self.dve = self.sem("dve")
        self.onec_f = self.sb("onec_f", [128, 1], F32)
        self.onec_b = self.sb("onec_b", [128, 1], BF16)
        self.oner_f = self.sb("oner_f", [1, 128], F32)
        self.neg128_b = self.sb("neg128_b", [1, 128], BF16)
        consts = (EPS,) + tuple(consts)
        self._cct = self.sb("ccols", [128, max(1, len(consts))], F32)
        self._cc = {}
        ph = Ph(self)
        ph.v(lambda v: v.memset(self.onec_f[:], 1.0))
        ph.v(lambda v: v.memset(self.onec_b[:], 1.0))
        ph.v(lambda v: v.memset(self.oner_f[:], 1.0))
        ph.v(lambda v: v.memset(self.neg128_b[:], -128.0))
        for i, cv in enumerate(consts):
            ap = self._cct[:, i:i + 1]
            ph.v(lambda v, ap=ap, cv=cv: v.memset(ap, float(cv)))
            self._cc[float(cv)] = ap
        ph.flush()

    def constcol(self, v):
        return self._cc[float(v)]

    def gate(self):
        return [(s, s.n) for s in self._sems.values() if s.n > 0]

    def finish(self):
        blk = self.block
        stores = list(self.store_sems)

        def fin(sy):
            for s in stores:
                sy.wait_ge(s.h, s.n)
        blk.sync(fin)
        self.es.close()
        return self.nc


class Ph:

    ENGS = ("sync", "gpsimd", "scalar", "vector", "tensor")

    def __init__(self, k: K, gate=None):
        self.k = k
        self.q = {e: [] for e in Ph.ENGS}
        self._gated = set()
        self.gate = list(gate) if gate else []

    def _add(self, eng, fn, waits, incs):
        waits = list(waits)
        if eng not in self._gated:
            self._gated.add(eng)
            waits = self.gate + waits
        self.q[eng].append((waits, fn, incs))

    def t(self, fn, waits=(), inc=0):
        incs = []
        if inc:
            self.k.pe.inc(inc)
            incs = [(self.k.pe, inc)]
        self._add("tensor", fn, waits, incs)
        return self.k.pe.n

    def s(self, fn, waits=()):
        a = self.k.act
        w = list(waits) + [(a, a.n)]
        a.inc(1)
        self._add("scalar", fn, w, [(a, 1)])
        return a.n

    def v(self, fn, waits=()):
        d = self.k.dve
        w = list(waits) + [(d, d.n)]
        d.inc(1)
        self._add("vector", fn, w, [(d, 1)])
        return d.n

    def g(self, fn, waits=()):
        p = self.k.sem("pool")
        w = list(waits) + [(p, p.n)]
        p.inc(1)
        self._add("gpsimd", fn, w, [(p, 1)])
        return p.n

    def dma(self, queue, pairs, sem: Sem, waits=()):
        h = sem.h

        def fn(e, pairs=tuple(pairs)):
            last = None
            for o, i in pairs:
                last = e.dma_start(out=o, in_=i).then_inc(h, 16)
            return last
        sem.inc(16 * len(pairs))
        self._add(queue, fn, waits, [])
        return sem.n

    def flush(self):
        blk = self.k.block
        for eng in Ph.ENGS:
            entries = self.q[eng]
            if not entries:
                continue

            def section(e, entries=entries):
                for waits, fn, incs in entries:
                    for s, val in waits:
                        if val > 0:
                            e.wait_ge(s.h, val)
                    inst = fn(e)
                    for s, amt in incs:
                        inst.then_inc(s.h, amt)
            getattr(blk, eng)(section)
        self.q = {e: [] for e in Ph.ENGS}
        self._gated = set()



def emit_load_xT(k, x_in, xT):
    ph = Ph(k, gate=k.gate())
    sem = k.usem("lx")
    ph.dma("sync", [(xT[:, dc * NTOK:(dc + 1) * NTOK],
                     x_in[dc * 128:(dc + 1) * 128, :]) for dc in range(DCN)],
           sem)
    ph.flush()
    return sem


def emit_store_fm(k, src, out_hbm, nrows, waits):
    ph = Ph(k, gate=k.gate())
    sem = k.usem("st")
    k.store_sems.append(sem)
    ph.dma("sync", [(out_hbm[r * 128:(r + 1) * 128, :],
                     src[:, r * NTOK:(r + 1) * NTOK])
                    for r in range(nrows // 128)], sem, waits=waits)
    ph.flush()
    return sem


def emit_vec_cols(k, src, n, dst, scale=None):
    ph = Ph(k, gate=k.gate())
    sem = k.usem("vc")
    nch = n // 128
    ph.dma("sync", [(dst[:, c:c + 1],
                     src[c * 128:(c + 1) * 128].rearrange("(p o) -> p o", o=1))
                    for c in range(nch)], sem)
    if scale is not None:
        ph.v(lambda v: v.tensor_scalar(out=dst[:], in0=dst[:], scalar1=scale,
                                       scalar2=None, op0=ALU.mult),
             waits=[(sem, sem.n)])
    ph.flush()
    return sem


def emit_ln(k, xT, w, ytgt, ycol, want_csum, csum_bf, ld_sem=None):
    nc = k.nc
    ph = Ph(k, gate=k.gate())
    k._useq += 1
    u = str(k._useq)
    with (
        nc.psum_tensor("ln_pa" + u, [128, CH], F32) as pa,
        nc.psum_tensor("ln_pb" + u, [128, CH], F32) as pb,
        nc.psum_tensor("ln_pc1" + u, [128, CH], F32) as pc1,
        nc.psum_tensor("ln_pc2" + u, [128, CH], F32) as pc2,
        nc.sbuf_tensor("ln_sq" + u, [128, 2 * CH], F32) as sq,
        nc.sbuf_tensor("ln_rows" + u, [1, 4 * CH], F32) as rows,
        nc.sbuf_tensor("ln_bc" + u, [128, 2 * CH], F32) as bc,
        nc.sbuf_tensor("ln_scr" + u, [128, CH], F32) as scr,
    ):
        ldw = [(ld_sem, ld_sem.n)] if ld_sem is not None else []
        sqb_marks = {}
        for dc in range(DCN):
            sw = list(ldw)
            if dc >= 2:
                sw.append((k.pe, sqb_marks[dc - 2]))
            am = ph.s(lambda s, dc=dc: s.activation(
                out=sq[:, (dc % 2) * CH:(dc % 2 + 1) * CH],
                in_=xT[:, col(dc, w)], func=AF.Square), waits=sw)
            ph.t(lambda t, dc=dc: t.matmul(
                pa[0:1, :], k.onec_f[:], xT[:, col(dc, w)],
                start=(dc == 0), stop=(dc == DCN - 1)), waits=ldw)
            sqb_marks[dc] = ph.t(lambda t, dc=dc: t.matmul(
                pb[0:1, :], k.onec_f[:],
                sq[:, (dc % 2) * CH:(dc % 2 + 1) * CH],
                start=(dc == 0), stop=(dc == DCN - 1)),
                waits=[(k.act, am)], inc=1)
        sums_pe = k.pe.n
        rm = rows[0:1, 0:CH]
        rt = rows[0:1, CH:2 * CH]
        rv = rows[0:1, 2 * CH:3 * CH]
        rr = rows[0:1, 3 * CH:4 * CH]
        ph.v(lambda v: v.tensor_scalar(out=rm, in0=pa[0:1, :],
                                       scalar1=1.0 / D, scalar2=None,
                                       op0=ALU.mult),
             waits=[(k.pe, sums_pe)])
        ph.v(lambda v: v.tensor_scalar(out=rt, in0=pb[0:1, :],
                                       scalar1=1.0 / D, scalar2=None,
                                       op0=ALU.mult))
        ph.v(lambda v: v.tensor_tensor(out=rv, in0=rm, in1=rm, op=ALU.mult))
        ph.v(lambda v: v.tensor_tensor(out=rv, in0=rt, in1=rv,
                                       op=ALU.subtract))
        vd = k.dve.n
        am = ph.s(lambda s: s.activation(out=rr, in_=rv, func=AF.Sqrt,
                                         bias=k.constcol(EPS)[0:1, 0:1]),
                  waits=[(k.dve, vd)])
        ph.v(lambda v: v.reciprocal(out=rv, in_=rr), waits=[(k.act, am)])
        ph.v(lambda v: v.scalar_tensor_tensor(out=rr, in0=rm, scalar=-1.0,
                                              in1=rv, op0=ALU.mult,
                                              op1=ALU.mult))
        vd2 = k.dve.n
        ph.t(lambda t: t.matmul(pc1[:], k.oner_f[:], rr, start=True,
                                stop=True), waits=[(k.dve, vd2)])
        ph.t(lambda t: t.matmul(pc2[:], k.oner_f[:], rv, start=True,
                                stop=True), inc=1)
        bc_pe = k.pe.n
        a1 = ph.s(lambda s: s.activation(out=bc[:, 0:CH], in_=pc1[:],
                                         func=AF.Copy),
                  waits=[(k.pe, bc_pe)])
        a2 = ph.s(lambda s: s.activation(out=bc[:, CH:2 * CH], in_=pc2[:],
                                         func=AF.Copy))
        y_marks = []
        for dc in range(DCN):
            ph.v(lambda v, dc=dc: v.tensor_tensor(
                out=scr[:], in0=xT[:, col(dc, w)], in1=bc[:, 0:CH],
                op=ALU.add), waits=[(k.act, a2)])
            y_marks.append(ph.v(lambda v, dc=dc: v.tensor_tensor(
                out=ytgt[:, ycol(dc)], in0=scr[:], in1=bc[:, CH:2 * CH],
                op=ALU.mult)))
        if want_csum:
            for dc in range(DCN):
                ph.t(lambda t, dc=dc: t.matmul(
                    pa[0:1, :], k.onec_b[:], ytgt[:, ycol(dc)],
                    start=(dc == 0), stop=(dc == DCN - 1)),
                    waits=[(k.dve, y_marks[-1])],
                    inc=(1 if dc == DCN - 1 else 0))
            cs_pe = k.pe.n
            ph.v(lambda v: v.tensor_copy(out=csum_bf, in_=pa[0:1, :]),
                 waits=[(k.pe, cs_pe)])
        ph.flush()
    return k.dve.n


def emit_ffn0(k, xT, yT, csum_bf, wgp_in, wup_in, wdp_in, sg_cols, su_cols,
              halfsd_cols, w, tag):
    nc = k.nc
    ph = Ph(k, gate=k.gate())
    with (
        nc.psum_tensor(f"f0g{tag}", [128, CH], F32) as pg,
        nc.psum_tensor(f"f0u{tag}", [128, CH], F32) as pu,
        nc.psum_tensor(f"f0g2{tag}", [128, CH], F32) as pg2,
        nc.psum_tensor(f"f0u2{tag}", [128, CH], F32) as pu2,
        nc.sbuf_tensor(f"f0wg{tag}", [128, 2 * D], BF16) as wgs,
        nc.sbuf_tensor(f"f0wu{tag}", [128, 2 * D], BF16) as wus,
        nc.sbuf_tensor(f"f0h{tag}", [128, HC0 * CH], BF16) as h_sb,
        nc.sbuf_tensor(f"f0sg{tag}", [128, 2 * CH], F32) as gsc,
        nc.sbuf_tensor(f"f0su{tag}", [128, 2 * CH], F32) as usc,
    ):
        lsem = k.usem(f"f0l{tag}")
        psG = [pg, pg2]
        psU = [pu, pu2]
        mm_marks = {}
        h_marks = {}
        for hc in range(HC0):
            b = hc % 2
            wwaits = [] if hc < 2 else [(k.pe, mm_marks[hc - 2])]
            ph.dma("gpsimd",
                   [(wgs[:, b * D:(b + 1) * D],
                     wgp_in[hc, :, :]),
                    (wus[:, b * D:(b + 1) * D],
                     wup_in[hc, :, :])], lsem, waits=wwaits)
            ld_n = lsem.n
            evw = [] if hc < 2 else [(k.act, h_marks[hc - 2][1])]

            def mm(t, hc=hc, b=b):
                for dc in range(DCN):
                    t.matmul(psG[b][:], wgs[:, b * D + dc * 128:
                                            b * D + (dc + 1) * 128],
                             yT[:, wcol(dc)], start=(dc == 0), stop=False)
                    t.matmul(psU[b][:], wus[:, b * D + dc * 128:
                                            b * D + (dc + 1) * 128],
                             yT[:, wcol(dc)], start=(dc == 0), stop=False)
                t.matmul(psG[b][:], k.neg128_b[:], csum_bf,
                         start=False, stop=True)
                return t.matmul(psU[b][:], k.neg128_b[:], csum_bf,
                                start=False, stop=True)
            mm_marks[hc] = ph.t(mm, waits=[(lsem, ld_n)] + evw, inc=1)
            gw = [(k.pe, mm_marks[hc])]
            if hc >= 2:
                gw.append((k.dve, h_marks[hc - 2][0]))
            a1 = ph.s(lambda s, hc=hc, b=b: s.activation(
                out=gsc[:, b * CH:(b + 1) * CH], in_=psG[b][:], func=AF.Silu,
                scale=sg_cols[:, hc:hc + 1]), waits=gw)
            a2 = ph.s(lambda s, hc=hc, b=b: s.mul(
                out=usc[:, b * CH:(b + 1) * CH], in_=psU[b][:],
                mul=su_cols[:, hc:hc + 1]))
            hm = ph.v(lambda v, hc=hc, b=b: v.tensor_tensor(
                out=h_sb[:, hc * CH:(hc + 1) * CH],
                in0=gsc[:, b * CH:(b + 1) * CH],
                in1=usc[:, b * CH:(b + 1) * CH], op=ALU.mult),
                waits=[(k.act, a2)])
            h_marks[hc] = (hm, a2)
        h_done = k.dve.n

        for hc in range(HC0):
            ph.t(lambda t, hc=hc: t.matmul(
                pg[0:1, :], k.onec_b[:], h_sb[:, hc * CH:(hc + 1) * CH],
                start=(hc == 0), stop=(hc == HC0 - 1)),
                waits=[(k.dve, h_done)], inc=(1 if hc == HC0 - 1 else 0))
        hs_pe = k.pe.n
        hsr = gsc[0:1, 0:CH]
        ph.v(lambda v: v.tensor_scalar(out=hsr, in0=pg[0:1, :],
                                       scalar1=-128.0, scalar2=None,
                                       op0=ALU.mult), waits=[(k.pe, hs_pe)])
        hs_v = k.dve.n
        ph.t(lambda t: t.matmul(pg2[:], k.oner_f[:], hsr, start=True,
                                stop=True), waits=[(k.dve, hs_v)], inc=1)
        hb_pe = k.pe.n
        hbm = ph.s(lambda s: s.activation(out=usc[:, 0:CH], in_=pg2[:],
                                          func=AF.Copy),
                   waits=[(k.pe, hb_pe)])

        dsem = k.usem(f"f0d{tag}")
        with nc.sbuf_tensor(f"f0wd{tag}", [128, 2 * HID0], BF16) as wds:
            dn_marks = {}
            x_marks = {}
            for dc in range(DCN):
                b = dc % 2
                wwaits = [] if dc < 2 else [(k.pe, dn_marks[dc - 2])]
                ph.dma("gpsimd", [(wds[:, b * HID0:(b + 1) * HID0],
                                   wdp_in[dc, :, :])], dsem, waits=wwaits)
                ld_n = dsem.n
                if dc < 2:
                    evw = [(k.dve, hs_v), (k.act, hbm)]
                else:
                    evw = [(k.dve, x_marks[dc - 2])]

                def mmd(t, dc=dc, b=b):
                    last = None
                    for hc in range(HC0):
                        last = t.matmul(
                            psG[b][:], wds[:, b * HID0 + hc * 128:
                                           b * HID0 + (hc + 1) * 128],
                            h_sb[:, hc * CH:(hc + 1) * CH],
                            start=(hc == 0), stop=(hc == HC0 - 1))
                    return last
                dn_marks[dc] = ph.t(mmd, waits=[(dsem, ld_n)] + evw, inc=1)
                ph.v(lambda v, b=b: v.tensor_tensor(
                    out=gsc[:, CH:2 * CH], in0=psG[b][:], in1=usc[:, 0:CH],
                    op=ALU.add), waits=[(k.pe, dn_marks[dc]), (k.act, hbm)])
                x_marks[dc] = ph.v(lambda v, dc=dc: v.scalar_tensor_tensor(
                    out=xT[:, col(dc, w)], in0=gsc[:, CH:2 * CH],
                    scalar=halfsd_cols[:, dc:dc + 1], in1=xT[:, col(dc, w)],
                    op0=ALU.mult, op1=ALU.add))
            ph.flush()
    return k.dve.n


def emit_ffn1(k, xT, yT, wgp_in, wup_in, wdp_in, w, tag):
    nc = k.nc
    ph = Ph(k, gate=k.gate())
    gsc_es = contextlib.ExitStack()
    EG = gsc_es.enter_context
    with (
        nc.psum_tensor(f"f1a{tag}", [128, CH], F32) as p0,
        nc.psum_tensor(f"f1b{tag}", [128, CH], F32) as p1,
        nc.psum_tensor(f"f1c{tag}", [128, CH], F32) as p2,
        nc.psum_tensor(f"f1d{tag}", [128, CH], F32) as p3,
        nc.sbuf_tensor(f"f1h{tag}", [128, HC1 * CH], BF16) as h_sb,
    ):
        wgs = EG(nc.sbuf_tensor(f"f1wg{tag}", [128, 2 * D], BF16))
        wus = EG(nc.sbuf_tensor(f"f1wu{tag}", [128, 2 * D], BF16))
        h4 = EG(nc.sbuf_tensor(f"f1h4{tag}", [128, 4 * CH], F32))
        sq = EG(nc.sbuf_tensor(f"f1sq{tag}", [128, 4 * CH], F32))
        pr = EG(nc.sbuf_tensor(f"f1pr{tag}", [128, 3 * CH], F32))
        geb = EG(nc.sbuf_tensor(f"f1ge{tag}", [128, CH], F32))
        u1 = EG(nc.sbuf_tensor(f"f1u1{tag}", [128, CH], F32))
        lsem = k.usem(f"f1l{tag}")
        psG = [p0, p1]
        psU = [p2, p3]
        mm_marks = {}
        h4_marks = {}
        for c in range(DCN):
            for r in range(4):
                it = c * 4 + r
                b = it % 2
                wwaits = [] if it < 2 else [(k.pe, mm_marks[it - 2])]
                ph.dma("gpsimd",
                       [(wgs[:, b * D:(b + 1) * D], wgp_in[it, :, :]),
                        (wus[:, b * D:(b + 1) * D], wup_in[it, :, :])],
                       lsem, waits=wwaits)
                ld_n = lsem.n
                evw = [] if it < 2 else [(k.dve, h4_marks[it - 2])]

                def mm(t, b=b):
                    for dc in range(DCN):
                        t.matmul(psG[b][:], wgs[:, b * D + dc * 128:
                                                b * D + (dc + 1) * 128],
                                 yT[:, wcol(dc)], start=(dc == 0),
                                 stop=(dc == DCN - 1))
                        last = t.matmul(
                            psU[b][:], wus[:, b * D + dc * 128:
                                           b * D + (dc + 1) * 128],
                            yT[:, wcol(dc)], start=(dc == 0),
                            stop=(dc == DCN - 1))
                    return last
                mm_marks[it] = ph.t(mm, waits=[(lsem, ld_n)] + evw, inc=1)
                gw = [(k.pe, mm_marks[it])]
                if c >= 1:
                    gw.append((k.dve, h4_marks[(c - 1) * 4 + 3]))
                a1 = ph.s(lambda s, b=b, r=r: s.activation(
                    out=h4[:, r * CH:(r + 1) * CH], in_=psG[b][:],
                    func=AF.Silu), waits=gw)
                uw = [] if it == 0 else [(k.dve, h4_marks[it - 1])]
                a2 = ph.s(lambda s, b=b: s.activation(
                    out=u1[:], in_=psU[b][:], func=AF.Copy), waits=uw)
                h4_marks[it] = ph.v(lambda v, r=r: v.tensor_tensor(
                    out=h4[:, r * CH:(r + 1) * CH],
                    in0=h4[:, r * CH:(r + 1) * CH], in1=u1[:], op=ALU.mult),
                    waits=[(k.act, a2)])
            hs = [h4[:, r * CH:(r + 1) * CH] for r in range(4)]
            sqs = [sq[:, r * CH:(r + 1) * CH] for r in range(4)]
            for r in range(4):
                ph.v(lambda v, r=r: v.tensor_tensor(
                    out=sqs[r], in0=hs[r], in1=hs[r], op=ALU.mult))
            pr0 = pr[:, 0:CH]
            pr1 = pr[:, CH:2 * CH]
            pr2 = pr[:, 2 * CH:3 * CH]
            ph.v(lambda v: v.tensor_tensor(out=pr0, in0=sqs[0], in1=sqs[1],
                                           op=ALU.max))
            ph.v(lambda v: v.tensor_tensor(out=pr1, in0=sqs[2], in1=sqs[3],
                                           op=ALU.max))
            ph.v(lambda v: v.tensor_tensor(out=pr0, in0=pr0, in1=pr1,
                                           op=ALU.min))
            ph.v(lambda v: v.tensor_tensor(out=pr1, in0=sqs[0], in1=sqs[1],
                                           op=ALU.min))
            ph.v(lambda v: v.tensor_tensor(out=pr2, in0=sqs[2], in1=sqs[3],
                                           op=ALU.min))
            ph.v(lambda v: v.tensor_tensor(out=pr1, in0=pr1, in1=pr2,
                                           op=ALU.max))
            ph.v(lambda v: v.tensor_tensor(out=pr0, in0=pr0, in1=pr1,
                                           op=ALU.max))
            for r in range(4):
                ph.v(lambda v, r=r: v.tensor_tensor(
                    out=geb[:], in0=sqs[r], in1=pr0, op=ALU.is_ge))
                m = ph.v(lambda v, r=r, c=c: v.tensor_tensor(
                    out=h_sb[:, (c * 4 + r) * CH:(c * 4 + r + 1) * CH],
                    in0=hs[r], in1=geb[:], op=ALU.mult))
                h4_marks[c * 4 + r] = m
        h_done = k.dve.n
        last_mm = mm_marks[DCN * 4 - 1]
        gsc_es.close()

        dsem = k.usem(f"f1ds{tag}")
        with nc.sbuf_tensor(f"f1wd{tag}", [128, 2 * HID1], BF16) as wds:
            dn_marks = {}
            x_marks = {}
            for dc in range(DCN):
                b = dc % 2
                wwaits = [(k.pe, last_mm), (k.dve, h_done)] if dc < 2 \
                    else [(k.pe, dn_marks[dc - 2])]
                ph.dma("gpsimd", [(wds[:, b * HID1:(b + 1) * HID1],
                                   wdp_in[dc, :, :])], dsem, waits=wwaits)
                ld_n = dsem.n
                evw = [(k.dve, h_done)] if dc < 2 else \
                    [(k.dve, x_marks[dc - 2])]

                def mmd(t, b=b):
                    last = None
                    for hc in range(HC1):
                        last = t.matmul(
                            psG[b][:], wds[:, b * HID1 + hc * 128:
                                           b * HID1 + (hc + 1) * 128],
                            h_sb[:, hc * CH:(hc + 1) * CH],
                            start=(hc == 0), stop=(hc == HC1 - 1))
                    return last
                dn_marks[dc] = ph.t(mmd, waits=[(dsem, ld_n)] + evw, inc=1)
                x_marks[dc] = ph.v(lambda v, dc=dc, b=b: v.scalar_tensor_tensor(
                    out=xT[:, col(dc, w)], in0=psG[b][:], scalar=0.5,
                    in1=xT[:, col(dc, w)], op0=ALU.mult, op1=ALU.add),
                    waits=[(k.pe, dn_marks[dc])])
            ph.flush()
    return k.dve.n


def emit_qkproj(k, yT, wpack_in, mcn, out_hbm, scale_col, tag, wpack2_in=None,
                ctab=None, stab=None):
    skw = {} if scale_col is None else {"scale": scale_col}
    """out_hbm[m*128.., w*CH..] = bf16(scale * W.T @ y)  (feature-major).
    If wpack2_in given: RoPE combine  qa*C + qb*S  (both scaled)."""
    nc = k.nc
    ph = Ph(k, gate=k.gate())
    rope = wpack2_in is not None
    with (
        nc.psum_tensor(f"qp0{tag}", [128, CH], F32) as p0,
        nc.psum_tensor(f"qp1{tag}", [128, CH], F32) as p1,
        nc.psum_tensor(f"qp2{tag}", [128, CH], F32) as p2,
        nc.psum_tensor(f"qp3{tag}", [128, CH], F32) as p3,
        nc.sbuf_tensor(f"qw{tag}", [128, 2 * D], BF16) as ws,
        nc.sbuf_tensor(f"qw2{tag}", [128, 2 * D], BF16) as ws2,
        nc.sbuf_tensor(f"qstg{tag}", [128, 2 * CH], BF16) as qstg,
        nc.sbuf_tensor(f"qsc{tag}", [128, 3 * CH], F32) as qsc,
    ):
        lsem = k.usem(f"qpl{tag}")
        ssem = k.usem(f"qps{tag}")
        k.store_sems.append(ssem)
        psA = [p0, p1]
        psB = [p2, p3]
        mm_marks = {}
        prev_vm = 0
        for m in range(mcn):
            b = m % 2
            wwaits = [] if m < 2 else [(k.pe, mm_marks[(m - 2, NW - 1)])]
            pairs = [(ws[:, b * D:(b + 1) * D], wpack_in[m, :, :])]
            if rope:
                pairs.append((ws2[:, b * D:(b + 1) * D], wpack2_in[m, :, :]))
            ph.dma("gpsimd", pairs, lsem, waits=wwaits)
            ld_n = lsem.n
            for w in range(NW):
                i = m * NW + w
                bb = i % 2

                def mm(t, b=b, bb=bb, w=w):
                    last = None
                    for dc in range(DCN):
                        last = t.matmul(
                            psA[bb][:], ws[:, b * D + dc * 128:
                                           b * D + (dc + 1) * 128],
                            yT[:, col(dc, w)], start=(dc == 0),
                            stop=(dc == DCN - 1))
                        if rope:
                            last = t.matmul(
                                psB[bb][:], ws2[:, b * D + dc * 128:
                                                b * D + (dc + 1) * 128],
                                yT[:, col(dc, w)], start=(dc == 0),
                                stop=(dc == DCN - 1))
                    return last
                evw = [] if i < 2 else [(ssem, 16 * (i - 1))]
                mm_marks[(m, w)] = ph.t(mm, waits=[(lsem, ld_n)] + evw,
                                        inc=1)
                if not rope:
                    am = ph.s(lambda s, bb=bb, i=i: s.activation(
                        out=qstg[:, (i % 2) * CH:(i % 2 + 1) * CH],
                        in_=psA[bb][:], func=AF.Copy, **skw),
                        waits=[(k.pe, mm_marks[(m, w)])])
                    gate_done = (k.act, am)
                else:
                    ph.s(lambda s, bb=bb: s.activation(
                        out=qsc[:, 0:CH], in_=psA[bb][:], func=AF.Copy,
                        **skw),
                        waits=[(k.pe, mm_marks[(m, w)]), (k.dve, prev_vm)])
                    am = ph.s(lambda s, bb=bb: s.activation(
                        out=qsc[:, CH:2 * CH], in_=psB[bb][:], func=AF.Copy,
                        **skw))
                    ph.v(lambda v, w=w: v.tensor_tensor(
                        out=qsc[:, 0:CH], in0=qsc[:, 0:CH],
                        in1=ctab[:, w * CH:(w + 1) * CH], op=ALU.mult),
                        waits=[(k.act, am)])
                    ph.v(lambda v, w=w: v.tensor_tensor(
                        out=qsc[:, 2 * CH:3 * CH], in0=qsc[:, CH:2 * CH],
                        in1=stab[:, w * CH:(w + 1) * CH], op=ALU.mult))
                    vm = ph.v(lambda v, i=i: v.tensor_tensor(
                        out=qstg[:, (i % 2) * CH:(i % 2 + 1) * CH],
                        in0=qsc[:, 0:CH], in1=qsc[:, 2 * CH:3 * CH],
                        op=ALU.add))
                    prev_vm = vm
                    gate_done = (k.dve, vm)
                ph.dma("sync", [(out_hbm[m * 128:(m + 1) * 128,
                                         w * CH:(w + 1) * CH],
                                 qstg[:, (i % 2) * CH:(i % 2 + 1) * CH])],
                       ssem, waits=[gate_done])
        ph.flush()
    return ssem


def emit_vproj(k, yT, wv_in, out_hbm, tag):
    nc = k.nc
    ph = Ph(k, gate=k.gate())
    with (
        nc.psum_tensor(f"vp0{tag}", [128, CH], F32) as p0,
        nc.psum_tensor(f"vp1{tag}", [128, CH], F32) as p1,
        nc.sbuf_tensor(f"vw{tag}", [128, DCN * CH], BF16) as wv_sb,
        nc.sbuf_tensor(f"vstg{tag}", [128, 2 * CH], BF16) as vstg,
    ):
        lsem = k.usem(f"vpl{tag}")
        ssem = k.usem(f"vps{tag}")
        k.store_sems.append(ssem)
        ph.dma("gpsimd", [(wv_sb[:, dc * CH:(dc + 1) * CH],
                           wv_in[dc * 128:(dc + 1) * 128, :])
                          for dc in range(DCN)], lsem)
        ld_n = lsem.n
        ps = [p0, p1]
        mm_marks = {}
        for tt in range(NTOK // 128):
            b = tt % 2
            evw = [] if tt < 2 else [(k.act, mm_marks[tt - 2])]

            def mm(t, tt=tt, b=b):
                last = None
                for dc in range(DCN):
                    last = t.matmul(
                        ps[b][:], yT[:, dc * NTOK + tt * 128:
                                     dc * NTOK + (tt + 1) * 128],
                        wv_sb[:, dc * CH:(dc + 1) * CH],
                        start=(dc == 0), stop=(dc == DCN - 1))
                return last
            pm = ph.t(mm, waits=[(lsem, ld_n)] + evw, inc=1)
            stw = [] if tt < 2 else [(ssem, 16 * (tt - 1))]
            am = ph.s(lambda s, tt=tt, b=b: s.activation(
                out=vstg[:, b * CH:(b + 1) * CH], in_=ps[b][:],
                func=AF.Copy), waits=[(k.pe, pm)] + stw)
            mm_marks[tt] = am
            ph.dma("sync", [(out_hbm[tt * 128:(tt + 1) * 128, :],
                             vstg[:, b * CH:(b + 1) * CH])], ssem,
                   waits=[(k.act, am)])
        ph.flush()
    return ssem


def emit_attn(k, xT, q_in, k_ins, v_ins, flags_sb, md_sb, wo_pack_in,
              hd, nh, bout_cols=None, clamp=True, tag="a"):
    nc = k.nc
    kv_rep = 4
    with (
        nc.psum_tensor(f"as0{tag}", [128, CH], F32) as ps0,
        nc.psum_tensor(f"as1{tag}", [128, CH], F32) as ps1,
        nc.psum_tensor(f"ac0{tag}", [128, CH], F32) as pc0,
        nc.psum_tensor(f"ac1{tag}", [128, CH], F32) as pc1,
        nc.psum_tensor(f"ar0{tag}", [128, CH], F32) as pr0,
        nc.psum_tensor(f"ar1{tag}", [128, CH], F32) as pr1,
        nc.psum_tensor(f"ab{tag}", [128, CH], F32) as pb,
        nc.sbuf_tensor(f"aq{tag}", [128 if hd == 128 else 64,
                                     (DCN if hd == 128 else nh) * CH],
                       BF16) as qw_sb,
        nc.sbuf_tensor(f"akl{tag}", [128, 2 * CH], BF16) as ksl,
        nc.sbuf_tensor(f"avl{tag}", [128, 2 * 4 * hd], BF16) as vsl,
        nc.sbuf_tensor(f"aet{tag}", [128, 2 * CH], F32) as et,
        nc.sbuf_tensor(f"apt{tag}", [128, 2 * CH], BF16) as pt,
        nc.sbuf_tensor(f"actx{tag}", [128, DCN * CH], BF16) as ctx_sb,
        nc.sbuf_tensor(f"abc{tag}", [128, CH], F32) as bcr,
        nc.sbuf_tensor(f"ascr{tag}", [128, CH], F32) as scrD,
        nc.sbuf_tensor(f"arr{tag}", [1, CH], F32) as rrow,
        nc.sbuf_tensor(f"awo{tag}", [128, 2 * D], BF16) as wos,
    ):
        psS = [ps0, ps1]
        psC = [pc0, pc1]
        psR = [pr0, pr1]
        for w in range(NW):
            ph = Ph(k, gate=k.gate())
            qsem = k.usem(f"aq{tag}{w}")
            if hd == 128:
                qpairs = [(qw_sb[:, m * CH:(m + 1) * CH],
                           q_in[m * 128:(m + 1) * 128,
                                w * CH:(w + 1) * CH]) for m in range(DCN)]
            else:
                qpairs = [(qw_sb[0:64, h * CH:(h + 1) * CH],
                           q_in[h * 64:(h + 1) * 64,
                                w * CH:(w + 1) * CH]) for h in range(nh)]
            ph.dma("sync", qpairs, qsem)
            q_n = qsem.n
            kvsem = k.usem(f"akv{tag}{w}")
            exp_marks = {}
            pt_marks = {}
            den_marks = {}
            den_marks_fine = {}
            ctx_done = {}
            prev_bm = 0
            prev_am = 0
            for h in range(nh):
                kv = h // kv_rep
                if hd == 128:
                    qrhs = qw_sb[:, h * CH:(h + 1) * CH]
                else:
                    qrhs = qw_sb[0:64, h * CH:(h + 1) * CH]
                for s in range(16):
                    b = s % 2
                    it = h * 16 + s
                    wwaits = [] if it < 2 else \
                        [(k.pe, den_marks[it - 2])]
                    if hd == 128:
                        kpair = (ksl[:, b * CH:(b + 1) * CH],
                                 k_ins[w][kv * 128:(kv + 1) * 128,
                                          s * CH:(s + 1) * CH])
                    else:
                        kpair = (ksl[0:64, b * CH:(b + 1) * CH],
                                 k_ins[w][0:64, (kv * 16 + s) * CH:
                                          (kv * 16 + s + 1) * CH])
                    vpairs = [(vsl[:, (b * 4 + ks) * hd:
                                   (b * 4 + ks + 1) * hd],
                               v_ins[w][s * CH + ks * 128:
                                        s * CH + (ks + 1) * 128,
                                        kv * hd:(kv + 1) * hd])
                              for ks in range(4)]
                    ph.dma("sync", [kpair] + vpairs, kvsem, waits=wwaits)
                    kv_n = kvsem.n
                    for ks in range(4):
                        i4 = it * 4 + ks
                        sb = ks % 2
                        sw = [(kvsem, kv_n), (qsem, q_n)]
                        if i4 >= 2:
                            sw.append((k.act, exp_marks[i4 - 2]))

                        def mmS(t, b=b, sb=sb, ks=ks, qrhs=qrhs):
                            if hd == 128:
                                lhs = ksl[:, b * CH + ks * 128:
                                          b * CH + ks * 128 + 128]
                            else:
                                lhs = ksl[0:64, b * CH + ks * 128:
                                          b * CH + ks * 128 + 128]
                            return t.matmul(psS[sb][:], lhs, qrhs,
                                            start=True, stop=True)
                        sm = ph.t(mmS, waits=sw, inc=1)
                        eww = [] if i4 < 2 else [(k.dve, pt_marks[i4 - 2])]
                        if s == 15:
                            dv = ph.v(lambda v, sb=sb, ks=ks: v.tensor_tensor(
                                out=scrD[:], in0=psS[sb][:],
                                in1=md_sb[:, ks * CH:(ks + 1) * CH],
                                op=ALU.add),
                                waits=[(k.pe, sm), (k.act, k.act.n)])
                            em = ph.s(lambda s_, sb=sb: s_.activation(
                                out=et[:, sb * CH:(sb + 1) * CH],
                                in_=scrD[:], func=AF.Exp),
                                waits=[(k.dve, dv)] + eww)
                        else:
                            em = ph.s(lambda s_, sb=sb, fcol=w * 16 + s:
                                      s_.activation(
                                          out=et[:, sb * CH:(sb + 1) * CH],
                                          in_=psS[sb][:], func=AF.Exp,
                                          bias=flags_sb[:, fcol:fcol + 1]),
                                      waits=[(k.pe, sm)] + eww)
                        exp_marks[i4] = em
                        ptw = [(k.act, em)]
                        if i4 >= 2:
                            ptw.append((k.pe, den_marks_fine[i4 - 2]))
                        if clamp:
                            pm = ph.v(lambda v, sb=sb: v.tensor_scalar(
                                out=pt[:, sb * CH:(sb + 1) * CH],
                                in0=et[:, sb * CH:(sb + 1) * CH],
                                scalar1=E50P, scalar2=E50N, op0=ALU.min,
                                op1=ALU.max), waits=ptw)
                        else:
                            pm = ph.v(lambda v, sb=sb: v.tensor_scalar(
                                out=pt[:, sb * CH:(sb + 1) * CH],
                                in0=et[:, sb * CH:(sb + 1) * CH],
                                scalar1=FBIG, scalar2=None, op0=ALU.min),
                                waits=ptw)
                        pt_marks[i4] = pm
                        cw = [(k.dve, pm), (kvsem, kv_n)]
                        if s == 0 and ks == 0 and h >= 2:
                            cw.append((k.dve, ctx_done[h - 2]))
                        st_ = (s == 0 and ks == 0)
                        sp_ = (s == 15 and ks == 3)

                        def mmC(t, b=b, sb=sb, ks=ks, h=h, st_=st_, sp_=sp_):
                            t.matmul(psC[h % 2][0:hd, :],
                                     vsl[:, (b * 4 + ks) * hd:
                                         (b * 4 + ks + 1) * hd],
                                     pt[:, sb * CH:(sb + 1) * CH],
                                     start=st_, stop=sp_)
                            return t.matmul(
                                psR[h % 2][0:1, :], k.onec_b[:],
                                pt[:, sb * CH:(sb + 1) * CH],
                                start=st_, stop=sp_)
                        dm = ph.t(mmC, waits=cw, inc=1)
                        den_marks_fine[i4] = dm
                    den_marks[it] = den_marks_fine[it * 4 + 3]
                dpe = den_marks[h * 16 + 15]
                rw = [(k.pe, dpe)]
                if h >= 1:
                    rw.append((k.pe, prev_bm))
                ph.v(lambda v, h=h: v.reciprocal(
                    out=rrow[:], in_=psR[h % 2][0:1, :]), waits=rw)
                rv = k.dve.n
                bw = [(k.dve, rv)]
                if h >= 1:
                    bw.append((k.act, prev_am))
                bm = ph.t(lambda t: t.matmul(
                    pb[0:hd, :], k.oner_f[0:1, 0:hd], rrow[:],
                    start=True, stop=True), waits=bw, inc=1)
                prev_bm = bm
                aw = [(k.pe, bm)]
                if h >= 1:
                    aw.append((k.dve, ctx_done[h - 1]))
                am = ph.s(lambda s: s.activation(
                    out=bcr[0:hd, :], in_=pb[0:hd, :], func=AF.Copy),
                    waits=aw)
                prev_am = am
                if hd == 128:
                    cdst = ctx_sb[:, h * CH:(h + 1) * CH]
                    csrc = psC[h % 2][:]
                    bsrc = bcr[:]
                else:
                    cdst = ctx_sb[(h % 2) * 64:(h % 2) * 64 + 64,
                                  (h // 2) * CH:(h // 2 + 1) * CH]
                    csrc = psC[h % 2][0:64, :]
                    bsrc = bcr[0:64, :]
                ctx_done[h] = ph.v(lambda v, cdst=cdst, csrc=csrc,
                                   bsrc=bsrc: v.tensor_tensor(
                    out=cdst, in0=csrc, in1=bsrc, op=ALU.mult),
                    waits=[(k.act, am)])
            ctx_all = k.dve.n

            osem = k.usem(f"ao{tag}{w}")
            op_marks = {}
            x_marks = {}
            for mc in range(DCN):
                b = mc % 2
                wwaits = [] if mc < 2 else [(k.pe, op_marks[mc - 2])]
                ph.dma("gpsimd", [(wos[:, b * D:(b + 1) * D],
                                   wo_pack_in[mc, :, :])], osem,
                       waits=wwaits)
                ld_n = osem.n
                evw = [(k.dve, ctx_all)] if mc < 2 else \
                    [(k.dve, x_marks[mc - 2])]

                def mmO(t, b=b):
                    last = None
                    for kc in range(DCN):
                        last = t.matmul(
                            psS[b][:], wos[:, b * D + kc * 128:
                                           b * D + (kc + 1) * 128],
                            ctx_sb[:, kc * CH:(kc + 1) * CH],
                            start=(kc == 0), stop=(kc == DCN - 1))
                    return last
                op_marks[mc] = ph.t(mmO, waits=[(osem, ld_n)] + evw, inc=1)
                if bout_cols is None:
                    x_marks[mc] = ph.v(lambda v, mc=mc, b=b: v.tensor_tensor(
                        out=xT[:, col(mc, w)], in0=psS[b][:],
                        in1=xT[:, col(mc, w)], op=ALU.add),
                        waits=[(k.pe, op_marks[mc])])
                else:
                    x_marks[mc] = ph.v(
                        lambda v, mc=mc, b=b: v.scalar_tensor_tensor(
                            out=xT[:, col(mc, w)], in0=psS[b][:],
                            scalar=bout_cols[:, mc:mc + 1],
                            in1=xT[:, col(mc, w)], op0=ALU.add, op1=ALU.add),
                        waits=[(k.pe, op_marks[mc])])
            ph.flush()
    return k.dve.n


def emit_load_misc(k, pairs):
    ph = Ph(k, gate=k.gate())
    sem = k.usem("misc")
    ph.dma("sync", pairs, sem)
    ph.flush()
    return sem


def emit_allgather(k, pairs):
    ph = Ph(k, gate=k.gate())
    sem = k.usem("cc")
    for ih, oh in pairs:
        def fn(g, ih=ih, oh=oh):
            return g.collective_compute(
                "AllGather", ALU.bypass,
                replica_groups=[list(range(NC))],
                ins=[ih.ap().opt()], outs=[oh.ap().opt()])
        sem.inc(1)
        ph._add("gpsimd", fn, [], [(sem, 1)])
    ph.flush()
    return sem


def emit_attn_f(k, xT, q_in, kg, vg, ksh, vsh, flags_sb, md_sb, wo_pack_in,
                hd, nh, bout_cols=None, clamp=True, tag="a"):
    nc = k.nc
    kv_rep = 4
    NS = 17
    vstr = hd + (1 if hd == 64 else 0)
    with (
        nc.psum_tensor(f"asb{tag}", [128, 4 * CH], F32) as psSb,
        nc.psum_tensor(f"ac0{tag}", [128, CH], F32) as pc0,
        nc.psum_tensor(f"ac1{tag}", [128, CH], F32) as pc1,
        nc.psum_tensor(f"ab{tag}", [128, CH], F32) as pb,
        nc.sbuf_tensor(f"aq{tag}", [128 if hd == 128 else 64,
                                    (DCN if hd == 128 else nh) * CH],
                       BF16) as qw_sb,
        nc.sbuf_tensor(f"akl{tag}", [128, 4 * CH], BF16) as ksl,
        nc.sbuf_tensor(f"avl{tag}", [128, 4 * 4 * vstr], BF16) as vsl,
        nc.sbuf_tensor(f"aet{tag}", [128, 2 * CH], BF16) as et,
        nc.sbuf_tensor(f"apt{tag}", [128, 8 * CH], BF16) as pt,
        nc.sbuf_tensor(f"actx{tag}", [128, DCN * CH], BF16) as ctx_sb,
        nc.sbuf_tensor(f"abc{tag}", [128, CH], F32) as bcr,
        nc.sbuf_tensor(f"ascr{tag}", [128, CH], F32) as scrD,
        nc.sbuf_tensor(f"ascr2{tag}", [128, CH], BF16) as scrE,
        nc.sbuf_tensor(f"arr{tag}", [128, CH], F32) as rrow,
        nc.sbuf_tensor(f"aon64{tag}", [128, 64], F32) as on64,
        nc.sbuf_tensor(f"awo{tag}", [128, 2 * D], BF16) as wos,
    ):
        psC = [pc0, pc1]
        if hd == 64:
            psR = None
        else:
            psR_cm = nc.psum_tensor(f"ar0{tag}", [128, CH], F32)
            psR = psR_cm.__enter__()
        ph0 = Ph(k, gate=k.gate())
        ph0.v(lambda v: v.memset(on64[64:65, :], 1.0))
        if hd == 64:
            ph0.v(lambda v: v.memset(vsl[:, :], 1.0))
        ph0.flush()
        init_done = k.dve.n
        for w in range(NW):
            ph = Ph(k, gate=k.gate())
            qsem = k.usem(f"aq{tag}{w}")
            if hd == 128:
                qpairs = [(qw_sb[:, m * CH:(m + 1) * CH],
                           q_in[m * 128:(m + 1) * 128,
                                w * CH:(w + 1) * CH]) for m in range(DCN)]
            else:
                qpairs = [(qw_sb[0:64, h * CH:(h + 1) * CH],
                           q_in[h * 64:(h + 1) * 64,
                                w * CH:(w + 1) * CH]) for h in range(nh)]
            ph.dma("sync", qpairs, qsem)
            q_n = qsem.n
            kvsem = k.usem(f"akv{tag}{w}")
            sm_marks = {}
            exp_marks = {}
            pt_marks = {}
            cd_marks = {}
            kvn_marks = {}
            ctx_done = {}
            recip_marks = {}
            prev_bm = 0
            prev_am = 0

            def emit_ctx(h, s, ph=ph):
                it = h * NS + s
                b = it % 4
                st_ = (s == 0)
                sp_ = (s == NS - 1)
                for ks in range(4):
                    cw = [pt_marks[(it, ks)],
                          (kvsem, kvn_marks[it])]
                    if st_ and ks == 0:
                        if h >= 2:
                            cw.append((k.dve, ctx_done[h - 2]))
                        if hd == 128 and h >= 1:
                            cw.append((k.dve, recip_marks[h - 1]))
                    psrc = pt[:, ((it % 2) * 4 + ks) * CH:
                              ((it % 2) * 4 + ks + 1) * CH]

                    def mmC(t, b=b, ks=ks, h=h, psrc=psrc,
                            st2=(st_ and ks == 0), sp2=(sp_ and ks == 3)):
                        m = t.matmul(psC[h % 2][0:vstr, :],
                                     vsl[:, (b * 4 + ks) * vstr:
                                         (b * 4 + ks) * vstr + vstr],
                                     psrc, start=st2, stop=sp2)
                        if hd == 128:
                            m = t.matmul(psR[0:1, :], k.onec_b[:],
                                         psrc, start=st2, stop=sp2)
                        return m
                    cd_marks[(it, ks)] = ph.t(mmC, waits=cw, inc=1)

            for h in range(nh):
                kv = h // kv_rep
                if hd == 128:
                    qrhs = qw_sb[:, h * CH:(h + 1) * CH]
                else:
                    qrhs = qw_sb[0:64, h * CH:(h + 1) * CH]
                for s in range(NS):
                    diag = (s == NS - 1)
                    it = h * NS + s
                    b = it % 4
                    wwaits = []
                    if it >= 4:
                        wwaits = [(k.pe, sm_marks[(it - 4, 3)]),
                                  (k.pe, cd_marks[(it - 4, 3)])]
                    elif hd == 64:
                        wwaits = [(k.dve, init_done)]
                    if diag:
                        if hd == 128:
                            ksrc = ksh[kv * 128:(kv + 1) * 128,
                                       w * CH:(w + 1) * CH]
                        else:
                            ksrc = ksh[kv * 64:kv * 64 + 64,
                                       w * CH:(w + 1) * CH]
                        vsrcs = [vsh[w * CH + ks * 128:
                                     w * CH + (ks + 1) * 128,
                                     kv * hd:(kv + 1) * hd]
                                 for ks in range(4)]
                    else:
                        r, lc = s // 2, (s % 2) * CH
                        if hd == 128:
                            ksrc = kg[r * 512 + kv * 128:
                                      r * 512 + (kv + 1) * 128,
                                      lc:lc + CH]
                        else:
                            ksrc = kg[r * 512 + kv * 64:
                                      r * 512 + kv * 64 + 64,
                                      lc:lc + CH]
                        vsrcs = [vg[r * NTOK + lc + ks * 128:
                                    r * NTOK + lc + (ks + 1) * 128,
                                    kv * hd:(kv + 1) * hd]
                                 for ks in range(4)]
                    if hd == 128:
                        kpair = (ksl[:, b * CH:(b + 1) * CH], ksrc)
                    else:
                        kpair = (ksl[0:64, b * CH:(b + 1) * CH], ksrc)
                    vpairs = [(vsl[:, (b * 4 + ks) * vstr:
                                   (b * 4 + ks) * vstr + hd], vsrcs[ks])
                              for ks in range(4)]
                    ph.dma("sync", [kpair] + vpairs, kvsem, waits=wwaits)
                    kvn_marks[it] = kvsem.n
                    for ks in range(4):
                        sw = [(kvsem, kvn_marks[it]), (qsem, q_n)]
                        if it >= 1:
                            sw.append((k.act, exp_marks[(it - 1, ks)]))

                        def mmS(t, b=b, ks=ks, qrhs=qrhs):
                            if hd == 128:
                                lhs = ksl[:, b * CH + ks * 128:
                                          b * CH + ks * 128 + 128]
                            else:
                                lhs = ksl[0:64, b * CH + ks * 128:
                                          b * CH + ks * 128 + 128]
                            return t.matmul(psSb[:, ks * CH:(ks + 1) * CH],
                                            lhs, qrhs, start=True, stop=True)
                        sm_marks[(it, ks)] = ph.t(mmS, waits=sw, inc=1)
                    for ks in range(4):
                        pdst = pt[:, ((it % 2) * 4 + ks) * CH:
                                  ((it % 2) * 4 + ks + 1) * CH]
                        ptwar = []
                        if it >= 2:
                            ptwar = [(k.pe, cd_marks[(it - 2, ks)])]
                        if clamp:
                            eww = []
                            if it * 4 + ks >= 2:
                                pi = (it * 4 + ks) - 2
                                eww = [pt_marks[(pi // 4, pi % 4)]]
                            if diag:
                                dv = ph.v(lambda v, ks=ks: v.tensor_tensor(
                                    out=scrD[:],
                                    in0=psSb[:, ks * CH:(ks + 1) * CH],
                                    in1=md_sb[:, ks * CH:(ks + 1) * CH],
                                    op=ALU.add),
                                    waits=[(k.pe, sm_marks[(it, ks)]),
                                           (k.act, k.act.n)])
                                em = ph.s(lambda s_, ks=ks: s_.activation(
                                    out=et[:, (ks % 2) * CH:
                                           (ks % 2 + 1) * CH],
                                    in_=scrD[:], func=AF.Exp),
                                    waits=[(k.dve, dv)] + eww)
                            else:
                                em = ph.s(lambda s_, ks=ks, fcol=w * NS + s:
                                          s_.activation(
                                              out=et[:, (ks % 2) * CH:
                                                     (ks % 2 + 1) * CH],
                                              in_=psSb[:,
                                                       ks * CH:(ks + 1) * CH],
                                              func=AF.Exp,
                                              bias=flags_sb[:,
                                                            fcol:fcol + 1]),
                                          waits=[(k.pe, sm_marks[(it, ks)])]
                                          + eww)
                            exp_marks[(it, ks)] = em
                            ptw = [(k.act, em)] + ptwar
                            if diag:
                                ph.g(lambda g, ks=ks: g.tensor_scalar(
                                    out=scrE[:],
                                    in0=et[:, (ks % 2) * CH:
                                           (ks % 2 + 1) * CH],
                                    scalar1=E50P, scalar2=E50N, op0=ALU.min,
                                    op1=ALU.max), waits=ptw)
                                pm = ph.g(lambda g, pdst=pdst:
                                          g.tensor_scalar(
                                              out=pdst, in0=scrE[:],
                                              scalar1=E50N, scalar2=None,
                                              op0=ALU.subtract))
                            else:
                                pm = ph.g(lambda g, ks=ks, pdst=pdst:
                                          g.tensor_scalar(
                                              out=pdst,
                                              in0=et[:, (ks % 2) * CH:
                                                     (ks % 2 + 1) * CH],
                                              scalar1=E50P, scalar2=E50N,
                                              op0=ALU.min, op1=ALU.max),
                                          waits=ptw)
                            pt_marks[(it, ks)] = (k.sem("pool"), pm)
                        else:
                            if diag:
                                dv = ph.v(lambda v, ks=ks: v.tensor_tensor(
                                    out=scrD[:],
                                    in0=psSb[:, ks * CH:(ks + 1) * CH],
                                    in1=md_sb[:, ks * CH:(ks + 1) * CH],
                                    op=ALU.add),
                                    waits=[(k.pe, sm_marks[(it, ks)]),
                                           (k.act, k.act.n)])
                                em = ph.s(lambda s_, pdst=pdst: s_.activation(
                                    out=pdst, in_=scrD[:], func=AF.Exp),
                                    waits=[(k.dve, dv)] + ptwar)
                            else:
                                em = ph.s(lambda s_, ks=ks, pdst=pdst,
                                          fcol=w * NS + s: s_.activation(
                                              out=pdst,
                                              in_=psSb[:,
                                                       ks * CH:(ks + 1) * CH],
                                              func=AF.Exp,
                                              bias=flags_sb[:,
                                                            fcol:fcol + 1]),
                                          waits=[(k.pe, sm_marks[(it, ks)])]
                                          + ptwar)
                            exp_marks[(it, ks)] = em
                            pt_marks[(it, ks)] = (k.act, em)
                    if s >= 1:
                        emit_ctx(h, s - 1)
                emit_ctx(h, NS - 1)
                dpe = cd_marks[(h * NS + NS - 1, 3)]
                rw = [(k.pe, dpe)]
                if h >= 1:
                    rw.append((k.pe, prev_bm))
                if hd == 128:
                    recip_marks[h] = ph.v(lambda v: v.reciprocal(
                        out=rrow[0:1, :], in_=psR[0:1, :]), waits=rw)
                    rsrc = rrow[0:1, :]
                    blhs = k.oner_f[0:1, 0:hd]
                else:
                    recip_marks[h] = ph.v(lambda v, h=h: v.reciprocal(
                        out=rrow[64:65, :], in_=psC[h % 2][64:65, :]),
                        waits=rw)
                    rsrc = rrow[64:65, :]
                    blhs = on64[64:65, :]
                rv = k.dve.n
                bw = [(k.dve, rv)]
                if h >= 1:
                    bw.append((k.act, prev_am))
                bm = ph.t(lambda t, rsrc=rsrc, blhs=blhs: t.matmul(
                    pb[0:hd, :], blhs, rsrc,
                    start=True, stop=True), waits=bw, inc=1)
                prev_bm = bm
                aw = [(k.pe, bm)]
                if h >= 1:
                    aw.append((k.dve, ctx_done[h - 1]))
                am = ph.s(lambda s: s.activation(
                    out=bcr[0:hd, :], in_=pb[0:hd, :], func=AF.Copy),
                    waits=aw)
                prev_am = am
                if hd == 128:
                    cdst = ctx_sb[:, h * CH:(h + 1) * CH]
                    csrc = psC[h % 2][:]
                    bsrc = bcr[:]
                else:
                    cdst = ctx_sb[(h % 2) * 64:(h % 2) * 64 + 64,
                                  (h // 2) * CH:(h // 2 + 1) * CH]
                    csrc = psC[h % 2][0:64, :]
                    bsrc = bcr[0:64, :]
                ctx_done[h] = ph.v(lambda v, cdst=cdst, csrc=csrc,
                                   bsrc=bsrc: v.tensor_tensor(
                    out=cdst, in0=csrc, in1=bsrc, op=ALU.mult),
                    waits=[(k.act, am)])
            ctx_all = k.dve.n

            osem = k.usem(f"ao{tag}{w}")
            op_marks = {}
            x_marks = {}
            for mc in range(DCN):
                b = mc % 2
                wwaits = [] if mc < 2 else [(k.pe, op_marks[mc - 2])]
                ph.dma("gpsimd", [(wos[:, b * D:(b + 1) * D],
                                   wo_pack_in[mc, :, :])], osem,
                       waits=wwaits)
                ld_n = osem.n
                evw = [(k.dve, ctx_all)] if mc < 2 else \
                    [(k.dve, x_marks[mc - 2])]

                def mmO(t, b=b):
                    last = None
                    for kc in range(DCN):
                        last = t.matmul(
                            psSb[:, (b % 2) * CH:(b % 2 + 1) * CH],
                            wos[:, b * D + kc * 128:b * D + (kc + 1) * 128],
                            ctx_sb[:, kc * CH:(kc + 1) * CH],
                            start=(kc == 0), stop=(kc == DCN - 1))
                    return last
                op_marks[mc] = ph.t(mmO, waits=[(osem, ld_n)] + evw, inc=1)
                if bout_cols is None:
                    x_marks[mc] = ph.v(lambda v, mc=mc, b=b: v.tensor_tensor(
                        out=xT[:, col(mc, w)],
                        in0=psSb[:, (b % 2) * CH:(b % 2 + 1) * CH],
                        in1=xT[:, col(mc, w)], op=ALU.add),
                        waits=[(k.pe, op_marks[mc])])
                else:
                    x_marks[mc] = ph.v(
                        lambda v, mc=mc, b=b: v.scalar_tensor_tensor(
                            out=xT[:, col(mc, w)],
                            in0=psSb[:, (b % 2) * CH:(b % 2 + 1) * CH],
                            scalar=bout_cols[:, mc:mc + 1],
                            in1=xT[:, col(mc, w)], op0=ALU.add, op1=ALU.add),
                        waits=[(k.pe, op_marks[mc])])
            ph.flush()
        if hd != 64:
            psR_cm.__exit__(None, None, None)
    return k.dve.n


def build_fused():
    k = K()
    xin = k.din("xT_in", [D, NTOK], F32)
    wg0p = k.din("wg0p", [HC0, 128, D], I32)
    wu0p = k.din("wu0p", [HC0, 128, D], I32)
    wd0p = k.din("wd0p", [DCN, 128, HID0], I32)
    sg = k.din("s_gate", [HID0], F32)
    su = k.din("s_up", [HID0], F32)
    sd = k.din("s_down", [D], F32)
    wq0p = k.din("wq0p", [DCN, 128, D], F32)
    wk0p = k.din("wk0p", [4, 128, D], F32)
    wv0 = k.din("wv0", [D, CH], F32)
    wo0p = k.din("wo0p", [DCN, 128, D], F32)
    flags0 = k.din("flags0", [128, 34], F32)
    flags1 = k.din("flags1", [128, 34], F32)
    md = k.din("md", [128, 4 * CH], F32)
    wg1p = k.din("wg1p", [HC1, 128, D], F32)
    wu1p = k.din("wu1p", [HC1, 128, D], F32)
    wd1p = k.din("wd1p", [DCN, 128, HID1], F32)
    wq1p = k.din("wq1p", [DCN, 128, D], F32)
    wq1sp = k.din("wq1sp", [DCN, 128, D], F32)
    wk1p = k.din("wk1p", [4, 128, D], F32)
    wk1sp = k.din("wk1sp", [4, 128, D], F32)
    wv1 = k.din("wv1", [D, CH], F32)
    wo1p = k.din("wo1p", [DCN, 128, D], F32)
    bout = k.din("b_out", [D], F32)
    ctab_in = k.din("ropeC", [128, NTOK], F32)
    stab_in = k.din("ropeS", [128, NTOK], F32)
    outx = k.dout("outT", [D, NTOK], F32)

    nc = k.nc
    q0d_h = nc.dram_tensor("q0d", [D, NTOK], BF16)
    k0sh_h = nc.dram_tensor("k0sh", [CH, NTOK], BF16)
    v0sh_h = nc.dram_tensor("v0sh", [NTOK, CH], BF16)
    k0g_h = nc.dram_tensor("k0g", [NC * CH, NTOK], BF16)
    v0g_h = nc.dram_tensor("v0g", [T, CH], BF16)
    q1d_h = nc.dram_tensor("q1d", [D, NTOK], BF16)
    k1sh_h = nc.dram_tensor("k1sh", [CH, NTOK], BF16)
    v1sh_h = nc.dram_tensor("v1sh", [NTOK, CH], BF16)
    k1g_h = nc.dram_tensor("k1g", [NC * CH, NTOK], BF16)
    v1g_h = nc.dram_tensor("v1g", [T, CH], BF16)

    k.start(consts=(RT128, 0.125))
    xT = k.sb("xT", [128, DCN * NTOK], F32)
    flags0_sb = k.sb("flags0_sb", [128, 34], F32)
    flags1_sb = k.sb("flags1_sb", [128, 34], F32)
    md_sb = k.sb("md_sb", [128, 4 * CH], F32)
    ctab = k.sb("ctab", [128, NTOK], F32)
    stab = k.sb("stab", [128, NTOK], F32)
    bout_cols = k.sb("bout_cols", [128, DCN], F32)
    sgc = k.sb("sgc", [128, HC0], F32)
    suc = k.sb("suc", [128, HC0], F32)
    sdc = k.sb("sdc", [128, DCN], F32)

    lx = emit_load_xT(k, xin, xT)
    emit_load_misc(k, [(flags0_sb[:], flags0), (flags1_sb[:], flags1),
                       (md_sb[:], md), (ctab[:], ctab_in),
                       (stab[:], stab_in)])
    emit_vec_cols(k, sg, HID0, sgc)
    emit_vec_cols(k, su, HID0, suc)
    emit_vec_cols(k, sd, D, sdc, scale=0.5)
    emit_vec_cols(k, bout, D, bout_cols)

    with nc.sbuf_tensor("yw1", [128, DCN * CH], BF16) as yw, \
            nc.sbuf_tensor("cs1", [1, 2 * CH], BF16) as csum:
        for w in range(NW):
            emit_ln(k, xT, w, yw, lambda dc: wcol(dc), True,
                    csum[0:1, w * CH:(w + 1) * CH], ld_sem=lx)
            emit_ffn0(k, xT, yw, csum[0:1, w * CH:(w + 1) * CH], wg0p, wu0p,
                      wd0p, sgc, suc, sdc, w, f"fa{w}")
    with nc.sbuf_tensor("yf1", [128, DCN * NTOK], BF16) as yT:
        for w in range(NW):
            emit_ln(k, xT, w, yT, lambda dc, w=w: col(dc, w), False, None)
        emit_qkproj(k, yT, wq0p, DCN, q0d_h.ap(), k.constcol(RT128), "q0")
        emit_qkproj(k, yT, wk0p, 4, k0sh_h.ap(), None, "k0")
        emit_vproj(k, yT, wv0, v0sh_h.ap(), "v0")
    emit_allgather(k, [(k0sh_h, k0g_h), (v0sh_h, v0g_h)])
    emit_attn_f(k, xT, q0d_h.ap(), k0g_h.ap(), v0g_h.ap(), k0sh_h.ap(),
                v0sh_h.ap(), flags0_sb, md_sb, wo0p, 128, 16,
                bout_cols=None, clamp=True, tag="a0")
    with nc.sbuf_tensor("yw3", [128, DCN * CH], BF16) as yw, \
            nc.sbuf_tensor("cs3", [1, 2 * CH], BF16) as csum:
        for w in range(NW):
            emit_ln(k, xT, w, yw, lambda dc: wcol(dc), True,
                    csum[0:1, w * CH:(w + 1) * CH])
            emit_ffn0(k, xT, yw, csum[0:1, w * CH:(w + 1) * CH], wg0p, wu0p,
                      wd0p, sgc, suc, sdc, w, f"fb{w}")
    with nc.sbuf_tensor("yw4", [128, DCN * CH], BF16) as yw:
        for w in range(NW):
            emit_ln(k, xT, w, yw, lambda dc: wcol(dc), False, None)
            emit_ffn1(k, xT, yw, wg1p, wu1p, wd1p, w, f"fc{w}")
    with nc.sbuf_tensor("yf4", [128, DCN * NTOK], BF16) as yT:
        for w in range(NW):
            emit_ln(k, xT, w, yT, lambda dc, w=w: col(dc, w), False, None)
        emit_qkproj(k, yT, wq1p, DCN, q1d_h.ap(), k.constcol(0.125), "q1",
                    wpack2_in=wq1sp, ctab=ctab, stab=stab)
        emit_qkproj(k, yT, wk1p, 4, k1sh_h.ap(), None, "k1",
                    wpack2_in=wk1sp, ctab=ctab, stab=stab)
        emit_vproj(k, yT, wv1, v1sh_h.ap(), "v1")
    emit_allgather(k, [(k1sh_h, k1g_h), (v1sh_h, v1g_h)])
    emit_attn_f(k, xT, q1d_h.ap(), k1g_h.ap(), v1g_h.ap(), k1sh_h.ap(),
                v1sh_h.ap(), flags1_sb, md_sb, wo1p, 64, 32,
                bout_cols=bout_cols, clamp=False, tag="a1")
    with nc.sbuf_tensor("yw6", [128, DCN * CH], BF16) as yw:
        for w in range(NW):
            emit_ln(k, xT, w, yw, lambda dc: wcol(dc), False, None)
            emit_ffn1(k, xT, yw, wg1p, wu1p, wd1p, w, f"fd{w}")
    emit_store_fm(k, xT, outx, D, waits=k.gate())
    return k.finish()


def _flags_f(core):
    f = np.zeros((128, 34), np.float32)
    for w in range(NW):
        jd = 2 * core + w
        for s in range(16):
            if s >= jd:
                f[:, w * 17 + s] = NEG1E9
    return f



def _pack_lhsT(W):
    Kd, N = W.shape
    a = W.reshape(Kd // 128, 128, N // 128, 128).transpose(2, 1, 0, 3)
    return np.ascontiguousarray(a.reshape(N // 128, 128, Kd))


def _md_mask():
    m = np.zeros((128, 4 * CH), np.float32)
    for ks in range(4):
        i = np.arange(128)[:, None] + 128 * ks
        j = np.arange(CH)[None, :]
        m[:, ks * CH:(ks + 1) * CH] = np.where(i > j, NEG1E9, 0.0)
    return m


def _flags(core):
    f = np.zeros((128, 32), np.float32)
    for w in range(NW):
        jd = 2 * core + w
        for s in range(16):
            if s < jd or s == 15:
                f[:, w * 16 + s] = 0.0
            else:
                f[:, w * 16 + s] = NEG1E9
    return f


def _slot_order(jd):
    return [c for c in range(NCH) if c != jd] + [jd]


def _rope_tabs(core):
    pos = np.arange(core * NTOK, (core + 1) * NTOK, dtype=np.float64)
    inv = 1.0 / (10000.0 ** (np.arange(32, dtype=np.float64) / 32.0))
    fr = np.outer(inv, pos)
    cos = np.cos(fr).astype(np.float32)
    sin = np.sin(fr).astype(np.float32)
    C = np.tile(cos, (4, 1))
    S = np.concatenate([-sin, sin, -sin, sin], axis=0)
    return np.ascontiguousarray(C), np.ascontiguousarray(S)


def _swap_heads(W, head_w):
    N = W.shape[1]
    idx = np.arange(N)
    h = idx // head_w
    r = idx % head_w
    half = head_w // 2
    src = h * head_w + np.where(r < half, r + half, r - half)
    return W[:, src]


def _ffn1_perm():
    n = np.arange(HID1)
    c = n // 512
    r = (n % 512) // 128
    p = n % 128
    return 512 * c + 4 * p + r


_CACHE = {}


def _progs():
    if "p" not in _CACHE:
        _CACHE["p"] = (build_p1(), build_p2(), build_p3(), build_p4())
    return _CACHE["p"]


def kernel(**inputs):
    inputs = {kk: np.asarray(vv) for kk, vv in inputs.items()}
    x = np.ascontiguousarray(inputs["L0_attn_hidden_states"][0])
    xT = np.ascontiguousarray(x.T)
    if "f" not in _CACHE:
        _CACHE["f"] = build_fused()
    ncf = _CACHE["f"]

    wg0p = _pack_lhsT(np.ascontiguousarray(inputs["L0_ffn_wq_gate"].T))
    wu0p = _pack_lhsT(np.ascontiguousarray(inputs["L0_ffn_wq_up"].T))
    wd0p = _pack_lhsT(np.ascontiguousarray(inputs["L0_ffn_wq_down"].T))
    wq0p = _pack_lhsT(inputs["L0_attn_Wq"])
    wk0p = _pack_lhsT(inputs["L0_attn_Wk"])
    wo0p = _pack_lhsT(inputs["L0_attn_Wo"])
    perm = _ffn1_perm()
    wg1p = _pack_lhsT(np.ascontiguousarray(inputs["L1_ffn_w_gate"][:, perm]))
    wu1p = _pack_lhsT(np.ascontiguousarray(inputs["L1_ffn_w_up"][:, perm]))
    wd1p = _pack_lhsT(np.ascontiguousarray(inputs["L1_ffn_w_down"][perm, :]))
    wq1 = inputs["L1_attn_Wq"]
    wk1 = inputs["L1_attn_Wk"]
    wq1p = _pack_lhsT(wq1)
    wq1sp = _pack_lhsT(np.ascontiguousarray(_swap_heads(wq1, 64)))
    wk1p = _pack_lhsT(wk1)
    wk1sp = _pack_lhsT(np.ascontiguousarray(_swap_heads(wk1, 64)))
    wo1p = _pack_lhsT(inputs["L1_attn_Wo"])
    md = _md_mask()

    maps = []
    for c in range(NC):
        Ct, St = _rope_tabs(c)
        fl = _flags_f(c)
        maps.append({
            "xT_in": np.ascontiguousarray(xT[:, c * NTOK:(c + 1) * NTOK]),
            "wg0p": wg0p, "wu0p": wu0p, "wd0p": wd0p,
            "s_gate": inputs["L0_ffn_s_gate"],
            "s_up": inputs["L0_ffn_s_up"],
            "s_down": inputs["L0_ffn_s_down"],
            "wq0p": wq0p, "wk0p": wk0p, "wv0": inputs["L0_attn_Wv"],
            "wo0p": wo0p, "flags0": fl, "flags1": fl, "md": md,
            "wg1p": wg1p, "wu1p": wu1p, "wd1p": wd1p,
            "wq1p": wq1p, "wq1sp": wq1sp, "wk1p": wk1p, "wk1sp": wk1sp,
            "wv1": inputs["L1_attn_Wv"], "wo1p": wo1p,
            "b_out": inputs["L1_attn_b_out"],
            "ropeC": Ct, "ropeS": St,
        })
    res = run_bass_kernel_spmd(ncf, maps, core_ids=list(range(NC))).results
    outT = np.concatenate([np.asarray(res[c]["outT"], np.float32)
                           for c in range(NC)], axis=1)
    return np.ascontiguousarray(outT.T)[None, :, :].astype(np.float32)


def kernel_multi(**inputs):
    inputs = {kk: np.asarray(vv) for kk, vv in inputs.items()}
    x = np.ascontiguousarray(inputs["L0_attn_hidden_states"][0])
    xT = np.ascontiguousarray(x.T)
    nc1, nc2, nc3, nc4 = _progs()

    wg0p = _pack_lhsT(np.ascontiguousarray(inputs["L0_ffn_wq_gate"].T))
    wu0p = _pack_lhsT(np.ascontiguousarray(inputs["L0_ffn_wq_up"].T))
    wd0p = _pack_lhsT(np.ascontiguousarray(inputs["L0_ffn_wq_down"].T))
    wq0p = _pack_lhsT(inputs["L0_attn_Wq"])
    wk0p = _pack_lhsT(inputs["L0_attn_Wk"])
    wo0p = _pack_lhsT(inputs["L0_attn_Wo"])
    perm = _ffn1_perm()
    wg1p = _pack_lhsT(np.ascontiguousarray(inputs["L1_ffn_w_gate"][:, perm]))
    wu1p = _pack_lhsT(np.ascontiguousarray(inputs["L1_ffn_w_up"][:, perm]))
    wd1p = _pack_lhsT(np.ascontiguousarray(inputs["L1_ffn_w_down"][perm, :]))
    wq1 = inputs["L1_attn_Wq"]
    wk1 = inputs["L1_attn_Wk"]
    wq1p = _pack_lhsT(wq1)
    wq1sp = _pack_lhsT(np.ascontiguousarray(_swap_heads(wq1, 64)))
    wk1p = _pack_lhsT(wk1)
    wk1sp = _pack_lhsT(np.ascontiguousarray(_swap_heads(wk1, 64)))
    wo1p = _pack_lhsT(inputs["L1_attn_Wo"])
    md = _md_mask()

    maps = []
    for c in range(NC):
        maps.append({
            "xT_in": np.ascontiguousarray(xT[:, c * NTOK:(c + 1) * NTOK]),
            "wg0p": wg0p, "wu0p": wu0p, "wd0p": wd0p,
            "s_gate": inputs["L0_ffn_s_gate"],
            "s_up": inputs["L0_ffn_s_up"],
            "s_down": inputs["L0_ffn_s_down"],
            "wq0p": wq0p, "wk0p": wk0p, "wv0": inputs["L0_attn_Wv"],
        })
    r1 = run_bass_kernel_spmd(nc1, maps, core_ids=list(range(NC))).results
    k0 = np.concatenate([r1[c]["k0T_out"] for c in range(NC)], axis=1)
    v0 = np.concatenate([r1[c]["v0_out"] for c in range(NC)], axis=0)

    maps = []
    for c in range(NC):
        m = {
            "x1T_in": r1[c]["x1T_out"],
            "q0T_in": r1[c]["q0T_out"],
            "flags0": _flags(c), "md0": md, "wo0p": wo0p,
            "wg0p": wg0p, "wu0p": wu0p, "wd0p": wd0p,
            "s_gate": inputs["L0_ffn_s_gate"],
            "s_up": inputs["L0_ffn_s_up"],
            "s_down": inputs["L0_ffn_s_down"],
        }
        for w in range(NW):
            order = _slot_order(2 * c + w)
            m[f"K0w{w}"] = np.ascontiguousarray(np.concatenate(
                [k0[:, ch * CH:(ch + 1) * CH] for ch in order], axis=1))
            m[f"V0w{w}"] = np.ascontiguousarray(np.concatenate(
                [v0[ch * CH:(ch + 1) * CH] for ch in order], axis=0))
        maps.append(m)
    r2 = run_bass_kernel_spmd(nc2, maps, core_ids=list(range(NC))).results

    maps = []
    for c in range(NC):
        Ct, St = _rope_tabs(c)
        maps.append({
            "x3T_in": r2[c]["x3T_out"],
            "wg1p": wg1p, "wu1p": wu1p, "wd1p": wd1p,
            "wq1p": wq1p, "wq1sp": wq1sp, "wk1p": wk1p, "wk1sp": wk1sp,
            "wv1": inputs["L1_attn_Wv"],
            "ropeC": Ct, "ropeS": St,
        })
    r3 = run_bass_kernel_spmd(nc3, maps, core_ids=list(range(NC))).results
    k1 = np.concatenate([r3[c]["k1T_out"] for c in range(NC)], axis=1)
    v1 = np.concatenate([r3[c]["v1_out"] for c in range(NC)], axis=0)
    k1r = k1.reshape(8, 64, NCH, CH)

    maps = []
    for c in range(NC):
        m = {
            "x4T_in": r3[c]["x4T_out"],
            "q1T_in": r3[c]["q1T_out"],
            "flags1": _flags(c), "md1": md, "wo1p": wo1p,
            "b_out": inputs["L1_attn_b_out"],
            "wg1p": wg1p, "wu1p": wu1p, "wd1p": wd1p,
        }
        for w in range(NW):
            order = _slot_order(2 * c + w)
            kk = k1r[:, :, order, :]
            kk = kk.transpose(1, 0, 2, 3).reshape(64, 8 * NCH * CH)
            m[f"K1w{w}"] = np.ascontiguousarray(kk)
            m[f"V1w{w}"] = np.ascontiguousarray(np.concatenate(
                [v1[ch * CH:(ch + 1) * CH] for ch in order], axis=0))
        maps.append(m)
    r4 = run_bass_kernel_spmd(nc4, maps, core_ids=list(range(NC))).results

    outT = np.concatenate([np.asarray(r4[c]["x6T_out"], np.float32)
                           for c in range(NC)], axis=1)
    return np.ascontiguousarray(outT.T)[None, :, :].astype(np.float32)
